# version 07
# baseline (speedup 1.0000x reference)
import numpy as np
import ml_dtypes
import concourse.bass as bass
import concourse.mybir as mybir
from concourse.bass_utils import run_bass_kernel_spmd
from contextlib import ExitStack

F32 = mybir.dt.float32
BF16 = mybir.dt.bfloat16
FP8 = mybir.dt.float8e4
AF = mybir.ActivationFunctionType
ALU = mybir.AluOpType
AX = mybir.AxisListType

NCORE = 8
T = 2048
D = 1024
FFN = 4096
NT = 8
NS = 16
EPS = 1e-5
NEG = -1e30
USE_CC = True
DO_B = True
DO_C = True
CORR_MODE = 0
DO_X = True


def build_nc():
    nc = bass.Bass("TRN2", target_bir_lowering=False, num_devices=NCORE)
    g = lambda n, s, d, k="ExternalInput": nc.dram_tensor(n, s, d, kind=k).ap()
    x_in = g("x", [T, D], F32)
    Wq = g("Wq", [D, D], F32); Wk = g("Wk", [D, D], F32)
    Wv = g("Wv", [D, D], F32); Wg = g("Wg", [D, D], F32)
    bgv = g("bg", [D], F32); Wo = g("Wo", [D, D], F32)
    ln1w = g("ln1w", [D], F32); ln1b = g("ln1b", [D], F32)
    ln2w = g("ln2w", [D], F32); ln2b = g("ln2b", [D], F32)
    W1 = g("W1", [D, FFN], F32); b1 = g("b1", [FFN], F32)
    W2 = g("W2", [FFN, D], F32); b2 = g("b2", [D], F32)
    fk = g("fk", [D, D], BF16)
    fsin = g("fsin", [1], F32); fsk = g("fsk", [1], F32)
    fbias = g("fbias", [D], BF16)
    identf = g("identf", [128, 128], F32); identb = g("identb", [128, 128], BF16)
    maskneg = g("maskneg", [128, 128], F32)
    sel16f = g("sel16f", [128, 128], F32); sel1b = g("sel1b", [128, 128], BF16)
    bsel = g("bsel", [16, 16 * 128], F32)
    onesf = g("onesf", [1, 128], F32); onesb = g("onesb", [1, 128], BF16)
    m1 = g("m1", [128, 8], F32)
    out = g("out", [T, D], F32, k="ExternalOutput")
    cc_src = g("cc_src", [128, 520], F32, k="Internal")
    cc_dst = g("cc_dst", [NCORE * 128, 520], F32, k="Internal")
    y2d = g("y2d", [T, D], BF16, k="Internal")
    qdd = g("qdd", [8, 128, T], BF16, k="Internal")

    st = ExitStack()
    E = st.enter_context
    with st:
        IDF = E(nc.sbuf_tensor("IDF", [128, 128], F32))
        IDB = E(nc.sbuf_tensor("IDB", [128, 128], BF16))
        ONESF = E(nc.sbuf_tensor("ONESF", [1, 128], F32))
        ONESB = E(nc.sbuf_tensor("ONESB", [1, 128], BF16))
        M1S = E(nc.sbuf_tensor("M1S", [128, 8], F32))
        PRM = E(nc.sbuf_tensor("PRM", [128, 48], F32))
        B1C = E(nc.sbuf_tensor("B1C", [128, 32], F32))
        PRMROW = E(nc.sbuf_tensor("PRMROW", [32, 128], F32))
        B2ROWB = E(nc.sbuf_tensor("B2ROWB", [1, D], BF16))
        FBROW = E(nc.sbuf_tensor("FBROW", [1, D], BF16))
        BIASB = E(nc.sbuf_tensor("BIASB", [128, D], BF16))
        SCAL = E(nc.sbuf_tensor("SCAL", [1, 8], F32))
        ZROW = E(nc.sbuf_tensor("ZROW", [1, 512], BF16))
        EPSC = E(nc.sbuf_tensor("EPSC", [128, 1], F32))
        RSIN = E(nc.sbuf_tensor("RSIN", [128, 1], F32))
        RSK = E(nc.sbuf_tensor("RSK", [128, 1], F32))
        SPROD = E(nc.sbuf_tensor("SPROD", [128, 1], F32))
        pA0 = E(nc.psum_tensor("pA0", [128, 256], F32))
        pA1 = E(nc.psum_tensor("pA1", [128, 256], F32))
        pB0 = E(nc.psum_tensor("pB0", [128, 256], F32))
        pB1 = E(nc.psum_tensor("pB1", [128, 256], F32))
        pC0 = E(nc.psum_tensor("pC0", [128, 512], F32))
        pC1 = E(nc.psum_tensor("pC1", [128, 512], F32))
        pAs = [pA0, pA1]
        ptrf = pA1
        ptrbs = None
        pBs = [pB0, pB1]
        pCs = [pC0, pC1]
        pD = E(nc.psum_tensor("pD", [128, 512], F32))
        pD2 = E(nc.psum_tensor("pD2", [128, 512], F32))
        ptrbs = [pA0.bitcast(BF16)[:, 0:128], pA1.bitcast(BF16)[:, 0:128],
                 pD.bitcast(BF16)[:, 0:128], pD2.bitcast(BF16)[:, 0:128]]
        sems = {}
        for name in ["pe", "dve", "act", "din", "dout", "cc", "dw"]:
            sems[name] = E(nc.semaphore("sm_" + name))
        cnt = {k: 0 for k in sems}
        blk = E(nc.Block())

        def vec_body(f):
            def g(e):
                w(e, "dve")
                f(e)
            blk.vector(g)

        def act_body(f):
            def g(e):
                w(e, "act")
                f(e)
            blk.scalar(g)

        def inc(name, inst, n=1):
            inst.then_inc(sems[name], n)
            cnt[name] += n
            return cnt[name]

        def w(e, name, v=None):
            e.wait_ge(sems[name], cnt[name] if v is None else v)

        def ph0_sync(e):
            for ap, sb in [(identf, IDF), (identb, IDB), (onesf, ONESF), (onesb, ONESB), (m1, M1S)]:
                inc("din", e.dma_start(out=sb[:, :], in_=ap), 16)
            inc("din", e.dma_start(out=FBROW[:, :], in_=fbias.rearrange("(o d) -> o d", o=1)), 16)
            inc("din", e.dma_start(out=SCAL[0:1, 0:1], in_=fsin.rearrange("(o d) -> o d", o=1)), 16)
            inc("din", e.dma_start(out=SCAL[0:1, 1:2], in_=fsk.rearrange("(o d) -> o d", o=1)), 16)
            w(e, "din")
        blk.sync(ph0_sync)

        for i, ap in enumerate([bgv, ln1w, ln1b, ln2w, ln2b]):
            PE_PV = cnt["pe"]

            def pv_sync(e, ap=ap, i=i, PE_PV=PE_PV):
                w(e, "pe", PE_PV)
                inc("din", e.dma_start(out=PRMROW[0:8, 0:128].rearrange("p (o a) -> p o a", o=1),
                                       in_=ap.rearrange("(a o p) -> a o p", o=1, p=128)), 16)
                w(e, "din")
            blk.sync(pv_sync)
            DVE_PV = cnt["dve"]

            def pv_pe(e, i=i, DVE_PV=DVE_PV):
                w(e, "din")
                w(e, "dve", DVE_PV)
                inc("pe", e.transpose(ptrf[:, 0:8], PRMROW[0:8, 0:128], IDF[0:8, 0:8]))
            blk.tensor(pv_pe)

            def pv_dve(e, i=i):
                w(e, "pe")
                inc("dve", e.tensor_copy(out=PRM[:, i * 8:(i + 1) * 8], in_=ptrf[:, 0:8]))
            vec_body(pv_dve)
        PE_PARAMS = cnt["pe"]

        def b1_sync(e):
            w(e, "pe", PE_PARAMS)
            inc("din", e.dma_start(out=PRMROW[0:32, 0:128].rearrange("p (o a) -> p o a", o=1),
                                   in_=b1.rearrange("(a o p) -> a o p", o=1, p=128)), 16)
            w(e, "din")
        blk.sync(b1_sync)

        def b1_pe(e):
            w(e, "din")
            w(e, "dve")
            inc("pe", e.transpose(ptrf[:, 0:32], PRMROW[0:32, 0:128], IDF[0:32, 0:32]))
        blk.tensor(b1_pe)
        PE_B1 = cnt["pe"]

        def b1_dve(e):
            w(e, "pe", PE_B1)
            inc("dve", e.tensor_copy(out=B1C[:, :], in_=ptrf[:, 0:32]))
        vec_body(b1_dve)

        def sc_dve(e):
            w(e, "din")
            e.memset(EPSC[:, :], EPS)
            e.memset(ZROW[:, :], 0.0)
            e.reciprocal(out=SCAL[0:1, 2:3], in_=SCAL[0:1, 0:1])
            e.reciprocal(out=SCAL[0:1, 3:4], in_=SCAL[0:1, 1:2])
            inc("dve", e.tensor_tensor(out=SCAL[0:1, 4:5], in0=SCAL[0:1, 0:1],
                                       in1=SCAL[0:1, 1:2], op=ALU.mult))
        vec_body(sc_dve)

        def sc_pe(e):
            w(e, "dve")
            e.matmul(pA0[:, 0:1], ONESF[:, :], SCAL[0:1, 2:3], start=True, stop=False)
            e.matmul(pA0[:, 1:2], ONESF[:, :], SCAL[0:1, 3:4], start=False, stop=False)
            e.matmul(pA0[:, 2:3], ONESF[:, :], SCAL[0:1, 4:5], start=False, stop=True)
            e.matmul(pC0[:, :], ONESB[:, :], FBROW[:, 0:512], start=True, stop=True)
            inc("pe", e.matmul(pC1[:, :], ONESB[:, :], FBROW[:, 512:1024], start=True, stop=True))
        blk.tensor(sc_pe)

        def sc_dve2(e):
            w(e, "pe")
            e.tensor_copy(out=RSIN[:, :], in_=pA0[:, 0:1])
            e.tensor_copy(out=RSK[:, :], in_=pA0[:, 1:2])
            e.tensor_copy(out=SPROD[:, :], in_=pA0[:, 2:3])
            e.tensor_copy(out=BIASB[:, 0:512], in_=pC0[:, :])
            inc("dve", e.tensor_copy(out=BIASB[:, 512:1024], in_=pC1[:, :]))
        vec_body(sc_dve2)


        stA = ExitStack()
        EA = stA.enter_context
        Y2 = EA(nc.sbuf_tensor("Y2", [128, NS * 1024], BF16))
        QDECT = EA(nc.sbuf_tensor("QDECT", [128, 2048], BF16))
        RNQ = EA(nc.sbuf_tensor("RNQ", [128, NS * 16], F32))
        LAL = EA(nc.sbuf_tensor("LAL", [128, 8 * 32], F32))
        CDA = EA(nc.sbuf_tensor("CDA", [128, 8 * 32], F32))
        S = EA(nc.sbuf_tensor("S", [128, 8 * 64], F32))
        Sb = EA(nc.sbuf_tensor("Sb", [128, 8 * 64], BF16))
        SbO = EA(nc.sbuf_tensor("SbO", [128, 8 * 64], BF16))
        SbM = EA(nc.sbuf_tensor("SbM", [128, 8 * 64], BF16))
        ABUF = EA(nc.sbuf_tensor("ABUF", [128, 8], F32))
        SGX = EA(nc.sbuf_tensor("SGX", [128, 2 * 520], F32))
        PVEC = EA(nc.sbuf_tensor("PVEC", [128, 8], F32))
        PM = EA(nc.sbuf_tensor("PM", [128, 8], F32))
        MSK = EA(nc.sbuf_tensor("MSK", [128, 128], F32))
        SELF = EA(nc.sbuf_tensor("SELF", [128, 128], F32))
        SELB = EA(nc.sbuf_tensor("SELB", [128, 128], BF16))
        BSEL = EA(nc.sbuf_tensor("BSEL", [16, 16 * 128], F32))
        SMALL = EA(nc.sbuf_tensor("SMALL", [16, 2048], F32))
        SCT = EA(nc.sbuf_tensor("SCT", [128, 32], F32))
        RKT = EA(nc.sbuf_tensor("RKT", [128, 32], F32))
        WQ = EA(nc.sbuf_tensor("WQs", [128, 8 * 1024], BF16))
        WK = EA(nc.sbuf_tensor("WKs", [128, 8 * 1024], BF16))
        WV = EA(nc.sbuf_tensor("WVs", [128, 8 * 1024], BF16))
        WGs = EA(nc.sbuf_tensor("WGs", [128, 8 * 1024], BF16))
        WOs = EA(nc.sbuf_tensor("WOs", [128, 8 * 1024], BF16))
        XT = EA(nc.sbuf_tensor("XT", [128, 1024], F32))
        SCR = EA(nc.sbuf_tensor("SCR", [128, 1024], F32))
        YTI = EA(nc.sbuf_tensor("YTI", [128, 2048], BF16))
        XN = EA(nc.sbuf_tensor("XN", [128, 2048], BF16))
        XNT = EA(nc.sbuf_tensor("XNT", [128, 2048], BF16))
        QF = EA(nc.sbuf_tensor("QF", [128, 2048], BF16))
        KF = EA(nc.sbuf_tensor("KF", [128, 2048], BF16))
        VT = EA(nc.sbuf_tensor("VT", [128, 2048], BF16))
        ALPHA = EA(nc.sbuf_tensor("ALPHA", [128, 2048], F32))
        LACUM = EA(nc.sbuf_tensor("LACUM", [128, 512], F32))
        EXPB = EA(nc.sbuf_tensor("EXPB", [128, 512], BF16))
        SQB = EA(nc.sbuf_tensor("SQB", [128, 512], BF16))
        ATTB = EA(nc.sbuf_tensor("ATTB", [128, 2048], BF16))
        OTOK = EA(nc.sbuf_tensor("OTOK", [128, 1024], BF16))
        OT = EA(nc.sbuf_tensor("OT", [128, 1024], BF16))
        STATS = EA(nc.sbuf_tensor("STATS", [128, 32], F32))

        def a_init_dve(e):
            e.memset(Y2[:, :], 0.0)
            e.memset(LAL[:, :], 0.0)
            e.memset(S[:, :], 0.0)
            e.memset(Sb[:, :], 0.0)
            inc("dve", e.memset(PVEC[:, :], 1.0))
        vec_body(a_init_dve)

        DIN_WBASE = cnt["din"]
        ACT_WBASE = cnt["act"]
        DW0 = cnt["dw"]
        DVE_INIT = cnt["dve"]
        Y2F = Y2.bitcast(F32)
        WSLOT = [Y2F[:, 2048 + k * 1024: 3072 + k * 1024] for k in range(6)]

        def wload(e):
            for ap, sb in [(maskneg, MSK), (sel16f, SELF), (sel1b, SELB), (bsel, BSEL)]:
                inc("din", e.dma_start(out=sb[:, :], in_=ap), 16)
            w(e, "din")
            e.wait_ge(sems["dve"], DVE_INIT)
            inc("din", e.dma_start(out=WSLOT[0][0:1, 0:1024], in_=b2.rearrange("(o d) -> o d", o=1)), 16)
            w(e, "din")
            i = 0
            for ap in [Wq, Wk, Wg, Wv, Wo]:
                for kc in range(8):
                    e.wait_ge(sems["act"], ACT_WBASE + 1 + max(0, i - 5))
                    inc("dw", e.dma_start(out=WSLOT[i % 6][:, 0:1024],
                                          in_=ap[kc * 128:(kc + 1) * 128, :]), 16)
                    i += 1
        blk.sync(wload)

        def wcast(e):
            e.wait_ge(sems["din"], DIN_WBASE + 5 * 16)
            inc("act", e.activation(B2ROWB[:, :], WSLOT[0][0:1, 0:1024], AF.Copy))
            i = 0
            for sb in [WQ, WK, WGs, WV, WOs]:
                for kc in range(8):
                    e.wait_ge(sems["dw"], DW0 + min(i + 6, 40) * 16)
                    inc("act", e.activation(sb[:, kc * 1024:(kc + 1) * 1024],
                                            WSLOT[i % 6][:, 0:1024], AF.Copy))
                    i += 1
        act_body(wcast)
        ACT_WEND = cnt["act"]

        PREV_SQ = [0]
        for t in range(NT):
            for sh in range(2):
                DVE_PREV = cnt["dve"]

                def a_x(e, t=t, sh=sh, DVE_PREV=DVE_PREV, psq=PREV_SQ[0]):
                    w(e, "dve", DVE_PREV)
                    if psq:
                        e.wait_ge(sems["act"], psq)
                    inc("din", e.dma_start(out=XT[:, 0:1024],
                                           in_=x_in[t * 256 + sh * 128: t * 256 + sh * 128 + 128, :]), 16)
                    w(e, "din")
                blk.sync(a_x)

                def a_sq(e, t=t, sh=sh):
                    w(e, "din")
                    c0 = sh * 16
                    inc("act", e.activation(SCR[:, 0:1024], XT[:, 0:1024], AF.Square,
                                            accum_out=STATS[:, c0 + 1:c0 + 2]))
                act_body(a_sq)
                ACT_SQ0 = cnt["act"]
                PREV_SQ[0] = ACT_SQ0

                def a_ln0(e, t=t, sh=sh, ACT_SQ0=ACT_SQ0):
                    w(e, "din")
                    c0 = sh * 16
                    inc("dve", e.reduce_sum(out=STATS[:, c0:c0 + 1], in_=XT[:, 0:1024], axis=AX.X))
                    w(e, "dve")
                    inc("dve", e.tensor_scalar(out=STATS[:, c0 + 2:c0 + 3], in0=STATS[:, c0:c0 + 1],
                                               scalar1=1.0 / D, scalar2=None, op0=ALU.mult))
                    w(e, "dve")
                    inc("dve", e.tensor_tensor(out=STATS[:, c0 + 3:c0 + 4], in0=STATS[:, c0 + 2:c0 + 3],
                                               in1=STATS[:, c0 + 2:c0 + 3], op=ALU.mult))
                    e.wait_ge(sems["act"], ACT_SQ0)
                    w(e, "dve")
                    inc("dve", e.scalar_tensor_tensor(out=STATS[:, c0 + 4:c0 + 5],
                                                      in0=STATS[:, c0 + 1:c0 + 2],
                                                      scalar=1.0 / D, in1=STATS[:, c0 + 3:c0 + 4],
                                                      op0=ALU.mult, op1=ALU.subtract))
                vec_body(a_ln0)
                DVE_VAR0 = cnt["dve"]

                def a_sd(e, t=t, sh=sh, DVE_VAR0=DVE_VAR0):
                    w(e, "dve", DVE_VAR0)
                    c0 = sh * 16
                    inc("act", e.activation(STATS[:, c0 + 5:c0 + 6], STATS[:, c0 + 4:c0 + 5],
                                            AF.Sqrt, bias=EPSC[:, 0:1]))
                act_body(a_sd)
                ACT_SD0 = cnt["act"]

                def a_y(e, t=t, sh=sh, ACT_SD0=ACT_SD0):
                    w(e, "act", ACT_SD0)
                    c0 = sh * 16
                    inc("dve", e.reciprocal(out=STATS[:, c0 + 6:c0 + 7], in_=STATS[:, c0 + 5:c0 + 6]))
                    w(e, "dve")
                    inc("dve", e.tensor_scalar(out=YTI[:, sh * 1024:(sh + 1) * 1024],
                                               in0=XT[:, 0:1024],
                                               scalar1=STATS[:, c0 + 2:c0 + 3],
                                               scalar2=STATS[:, c0 + 6:c0 + 7],
                                               op0=ALU.subtract, op1=ALU.mult))
                vec_body(a_y)
                DVE_Y = cnt["dve"]

                def a_sq1(e, t=t, sh=sh, DVE_Y=DVE_Y):
                    w(e, "dve", DVE_Y)
                    c0 = sh * 16 + 8
                    inc("act", e.activation(SCR[:, 0:1024], YTI[:, sh * 1024:(sh + 1) * 1024],
                                            AF.Square, accum_out=STATS[:, c0 + 1:c0 + 2]))
                act_body(a_sq1)
                ACT_SQ1 = cnt["act"]

                def a_ln1(e, t=t, sh=sh, DVE_Y=DVE_Y, ACT_SQ1=ACT_SQ1):
                    w(e, "dve", DVE_Y)
                    c0 = sh * 16 + 8
                    inc("dve", e.reduce_sum(out=STATS[:, c0:c0 + 1],
                                            in_=YTI[:, sh * 1024:(sh + 1) * 1024], axis=AX.X))
                    w(e, "dve")
                    inc("dve", e.tensor_scalar(out=STATS[:, c0 + 2:c0 + 3], in0=STATS[:, c0:c0 + 1],
                                               scalar1=1.0 / D, scalar2=None, op0=ALU.mult))
                    w(e, "dve")
                    inc("dve", e.tensor_tensor(out=STATS[:, c0 + 3:c0 + 4], in0=STATS[:, c0 + 2:c0 + 3],
                                               in1=STATS[:, c0 + 2:c0 + 3], op=ALU.mult))
                    e.wait_ge(sems["act"], ACT_SQ1)
                    w(e, "dve")
                    inc("dve", e.scalar_tensor_tensor(out=STATS[:, c0 + 4:c0 + 5],
                                                      in0=STATS[:, c0 + 1:c0 + 2],
                                                      scalar=1.0 / D, in1=STATS[:, c0 + 3:c0 + 4],
                                                      op0=ALU.mult, op1=ALU.subtract))
                vec_body(a_ln1)
                DVE_VAR1 = cnt["dve"]

                def a_sd1(e, t=t, sh=sh, DVE_VAR1=DVE_VAR1):
                    w(e, "dve", DVE_VAR1)
                    c0 = sh * 16 + 8
                    inc("act", e.activation(STATS[:, c0 + 5:c0 + 6], STATS[:, c0 + 4:c0 + 5],
                                            AF.Sqrt, bias=EPSC[:, 0:1]))
                act_body(a_sd1)
                ACT_SD1 = cnt["act"]

                def a_xn(e, t=t, sh=sh, ACT_SD1=ACT_SD1):
                    w(e, "act", ACT_SD1)
                    c0 = sh * 16 + 8
                    inc("dve", e.reciprocal(out=STATS[:, c0 + 6:c0 + 7], in_=STATS[:, c0 + 5:c0 + 6]))
                    w(e, "dve")
                    inc("dve", e.tensor_scalar(out=XN[:, sh * 1024:(sh + 1) * 1024],
                                               in0=YTI[:, sh * 1024:(sh + 1) * 1024],
                                               scalar1=STATS[:, c0 + 2:c0 + 3],
                                               scalar2=STATS[:, c0 + 6:c0 + 7],
                                               op0=ALU.subtract, op1=ALU.mult))
                vec_body(a_xn)
            DVE_XN = cnt["dve"]

            PE_T0 = cnt["pe"]
            DVE_T0 = cnt["dve"]

            def a_xnt_pe(e, t=t):
                w(e, "dve", DVE_XN)
                for i in range(16):
                    sh, fc = i // 8, i % 8
                    if i >= 4:
                        e.wait_ge(sems["dve"], DVE_T0 + i - 3)
                    inc("pe", e.transpose(ptrbs[i % 4],
                                          XN[:, sh * 1024 + fc * 128: sh * 1024 + fc * 128 + 128], IDB[:, :]))
            blk.tensor(a_xnt_pe)

            def a_xnt_dve(e, t=t):
                for i in range(16):
                    sh, fc = i // 8, i % 8
                    e.wait_ge(sems["pe"], PE_T0 + i + 1)
                    inc("dve", e.tensor_scalar(out=XNT[:, fc * 256 + sh * 128: fc * 256 + sh * 128 + 128],
                                               in0=ptrbs[i % 4],
                                               scalar1=PRM[:, 8 + fc:9 + fc], scalar2=PRM[:, 16 + fc:17 + fc],
                                               op0=ALU.mult, op1=ALU.add))
            vec_body(a_xnt_dve)
            DVE_XNT = cnt["dve"]

            PE_PROJ0 = cnt["pe"]
            ACT_PQ0 = cnt["act"]
            DVE_PK0 = cnt["dve"]

            def a_proj_pe(e, t=t):
                w(e, "dve", DVE_XNT)
                if t == 0:
                    e.wait_ge(sems["act"], ACT_WBASE + 9)
                for oc in range(8):
                    if oc >= 2:
                        e.wait_ge(sems["act"], ACT_PQ0 + oc - 1)
                    for kc in range(8):
                        mm = e.matmul(pAs[oc % 2][:, :],
                                      WQ[:, kc * 1024 + oc * 128: kc * 1024 + (oc + 1) * 128],
                                      XNT[:, kc * 256:(kc + 1) * 256], start=(kc == 0), stop=(kc == 7))
                    inc("pe", mm)
                if t == 0:
                    e.wait_ge(sems["act"], ACT_WBASE + 17)
                for oc in range(8):
                    if oc >= 2:
                        e.wait_ge(sems["dve"], DVE_PK0 + oc - 1)
                    for kc in range(8):
                        mm = e.matmul(pBs[oc % 2][:, :],
                                      WK[:, kc * 1024 + oc * 128: kc * 1024 + (oc + 1) * 128],
                                      XNT[:, kc * 256:(kc + 1) * 256], start=(kc == 0), stop=(kc == 7))
                    inc("pe", mm)
                if t == 0:
                    e.wait_ge(sems["act"], ACT_WBASE + 25)
                for oc in range(8):
                    e.wait_ge(sems["act"], ACT_PQ0 + 7 + oc)
                    for kc in range(8):
                        mm = e.matmul(pAs[oc % 2][:, :],
                                      WGs[:, kc * 1024 + oc * 128: kc * 1024 + (oc + 1) * 128],
                                      XNT[:, kc * 256:(kc + 1) * 256], start=(kc == 0), stop=(kc == 7))
                    inc("pe", mm)
                if t == 0:
                    e.wait_ge(sems["act"], ACT_WBASE + 33)
                for sh in range(2):
                    if sh == 1:
                        e.wait_ge(sems["dve"], DVE_PK0 + 9)
                    for nh in range(2):
                        for kc in range(8):
                            mm = e.matmul(pCs[nh][:, :],
                                          XNT[:, kc * 256 + sh * 128: kc * 256 + sh * 128 + 128],
                                          WV[:, kc * 1024 + nh * 512: kc * 1024 + (nh + 1) * 512],
                                          start=(kc == 0), stop=(kc == 7))
                    inc("pe", mm)
            blk.tensor(a_proj_pe)

            def a_projq_act(e, t=t):
                for oc in range(8):
                    e.wait_ge(sems["pe"], PE_PROJ0 + oc + 1)
                    inc("act", e.activation(QF[:, oc * 256:(oc + 1) * 256], pAs[oc % 2][:, :], AF.Copy))
                for oc in range(8):
                    e.wait_ge(sems["pe"], PE_PROJ0 + 16 + oc + 1)
                    inc("act", e.activation(ALPHA[:, oc * 256:(oc + 1) * 256], pAs[oc % 2][:, :],
                                            AF.Sigmoid, bias=PRM[:, 0 + oc:1 + oc]))
            act_body(a_projq_act)
            ACT_G = cnt["act"]

            def a_projk_dve(e, t=t):
                for oc in range(8):
                    e.wait_ge(sems["pe"], PE_PROJ0 + 8 + oc + 1)
                    inc("dve", e.tensor_copy(out=KF[:, oc * 256:(oc + 1) * 256], in_=pBs[oc % 2][:, :]))
                for sh in range(2):
                    e.wait_ge(sems["pe"], PE_PROJ0 + 24 + sh + 1)
                    e.tensor_copy(out=VT[:, sh * 1024: sh * 1024 + 512], in_=pC0[:, :])
                    inc("dve", e.tensor_copy(out=VT[:, sh * 1024 + 512:(sh + 1) * 1024], in_=pC1[:, :]))
            vec_body(a_projk_dve)
            DVE_V = cnt["dve"]

            DVE_CL0 = cnt["dve"]
            ACT_LN0 = cnt["act"]

            def a_clamp_dve(e, t=t, ACT_LN0=ACT_LN0):
                w(e, "act", ACT_G)
                for oc in range(8):
                    if oc >= 2:
                        e.wait_ge(sems["act"], ACT_LN0 + oc - 1)
                    inc("dve", e.tensor_scalar(out=SCR[:, (oc % 2) * 256:(oc % 2) * 256 + 256],
                                               in0=ALPHA[:, oc * 256:(oc + 1) * 256],
                                               scalar1=1e-6, scalar2=None, op0=ALU.max))
            vec_body(a_clamp_dve)

            def a_ln_act(e, t=t, DVE_CL0=DVE_CL0):
                for oc in range(8):
                    e.wait_ge(sems["dve"], DVE_CL0 + oc + 1)
                    inc("act", e.activation(ALPHA[:, oc * 256:(oc + 1) * 256],
                                            SCR[:, (oc % 2) * 256:(oc % 2) * 256 + 256], AF.Ln))
            act_body(a_ln_act)
            ACT_LA = cnt["act"]
            LA = ALPHA

            dseq = []
            for fc in range(8):
                dseq.append(("scan", fc))
                if fc >= 1:
                    dseq.append(("qdec", fc - 1))
                    dseq.append(("dte", fc - 1))
                if fc >= 2:
                    dseq.append(("kdt", fc - 2))
            dseq += [("qdec", 7), ("dte", 7), ("kdt", 6), ("kdt", 7)]
            aseq = [("eL", 0), ("eL", 1)]
            for fc in range(8):
                if fc >= 2:
                    aseq.append(("eL", fc))
                aseq.append(("eD", fc))
            DVE_SC0 = cnt["dve"]
            ACT_SC0 = cnt["act"]
            dcnt = {}
            acnt = {}
            _c = DVE_SC0
            for it in dseq:
                _c += 2 if it[0] == "scan" else 1
                dcnt[it] = _c
            for j, it in enumerate(aseq):
                acnt[it] = ACT_SC0 + j + 1
            ACT_LA_t = ACT_LA

            def a_scanpipe_dve(e, t=t, dcnt=dcnt, acnt=acnt, ACT_LA_t=ACT_LA_t):
                w(e, "act", ACT_LA_t)
                w(e, "dout")
                for op, fc in dseq:
                    sl = (fc % 2) * 256
                    if op == "scan":
                        if fc >= 2:
                            w(e, "dve", dcnt[("dte", fc - 2)])
                        last = None
                        for c in range(4):
                            last = e.tensor_tensor_scan(out=LACUM[:, sl + c * 64: sl + (c + 1) * 64],
                                                        data0=LA[:, fc * 256 + c * 64: fc * 256 + (c + 1) * 64],
                                                        data1=LA[:, fc * 256 + c * 64: fc * 256 + (c + 1) * 64],
                                                        initial=0.0, op0=ALU.add, op1=ALU.bypass)
                        inc("dve", last)
                        w(e, "dve")
                        inc("dve", e.tensor_copy(
                            out=LAL[:, fc * 32 + t * 4: fc * 32 + t * 4 + 4],
                            in_=LACUM[:, sl:sl + 256].rearrange("p (c z) -> p c z", z=64)[:, :, 63:64]))
                    elif op == "qdec":
                        e.wait_ge(sems["act"], acnt[("eL", fc)])
                        inc("dve", e.tensor_tensor(out=QDECT[:, fc * 256:(fc + 1) * 256],
                                                   in0=QF[:, fc * 256:(fc + 1) * 256],
                                                   in1=EXPB[:, sl:sl + 256], op=ALU.mult))
                    elif op == "dte":
                        if fc >= 2:
                            e.wait_ge(sems["act"], acnt[("eD", fc - 2)])
                        last = None
                        for c in range(4):
                            last = e.tensor_scalar(out=SCR[:, sl + c * 64: sl + (c + 1) * 64],
                                                   in0=LACUM[:, sl + c * 64: sl + (c + 1) * 64],
                                                   scalar1=LAL[:, fc * 32 + t * 4 + c: fc * 32 + t * 4 + c + 1],
                                                   scalar2=None, op0=ALU.subtract)
                        inc("dve", last)
                    elif op == "kdt":
                        e.wait_ge(sems["act"], acnt[("eD", fc)])
                        inc("dve", e.tensor_tensor(out=XN[:, fc * 256:(fc + 1) * 256],
                                                   in0=KF[:, fc * 256:(fc + 1) * 256],
                                                   in1=EXPB[:, sl:sl + 256], op=ALU.mult))
            vec_body(a_scanpipe_dve)

            def a_scanpipe_act(e, t=t, dcnt=dcnt):
                for op, fc in aseq:
                    sl = (fc % 2) * 256
                    if op == "eL":
                        e.wait_ge(sems["dve"], dcnt[("scan", fc)])
                        if fc >= 2:
                            e.wait_ge(sems["dve"], dcnt[("kdt", fc - 2)])
                        inc("act", e.activation(EXPB[:, sl:sl + 256], LACUM[:, sl:sl + 256], AF.Exp))
                    else:
                        e.wait_ge(sems["dve"], dcnt[("dte", fc)])
                        inc("act", e.activation(EXPB[:, sl:sl + 256], SCR[:, sl:sl + 256],
                                                AF.Exp, scale=-1.0))
            act_body(a_scanpipe_act)
            DVE_SCAN = dcnt[("scan", 7)]
            DVE_KDT = dcnt[("kdt", 7)]
            KDT = XN
            DVE_QD7 = dcnt[("qdec", 7)]

            def a_qspill(e, t=t, DVE_QD7=DVE_QD7):
                e.wait_ge(sems["dve"], DVE_QD7)
                inc("dout", e.dma_start(out=qdd[:, :, t * 256:(t + 1) * 256].rearrange("f p c -> p f c"),
                                        in_=QDECT[:, :].rearrange("p (f c) -> p f c", c=256)), 16)
                w(e, "dout")
            blk.gpsimd(a_qspill)
            DOUT_QS = cnt["dout"]

            def a_cd_act(e, t=t):
                w(e, "dve", DVE_SCAN)
                inc("act", e.activation(
                    CDA[:, :].rearrange("p (f c) -> p f c", c=32)[:, :, t * 4:(t + 1) * 4],
                    LAL[:, :].rearrange("p (f c) -> p f c", c=32)[:, :, t * 4:(t + 1) * 4], AF.Exp))
            act_body(a_cd_act)

            PE_ML0 = cnt["pe"]
            ACT_NSQ0 = cnt["act"]

            def a_sqn_act(e, t=t, PE_ML0=PE_ML0):
                w(e, "dve", DVE_V)
                for fc in range(8):
                    if fc >= 2:
                        e.wait_ge(sems["pe"], PE_ML0 + 1 + fc - 1)
                    inc("act", e.activation(SQB[:, (fc % 2) * 256:(fc % 2) * 256 + 256],
                                            QF[:, fc * 256:(fc + 1) * 256], AF.Square))
            act_body(a_sqn_act)

            def a_mlog_pe(e, t=t, ACT_NSQ0=ACT_NSQ0):
                w(e, "dve", DVE_SCAN)
                for fc in range(8):
                    mm = e.matmul(pA0[0:16, 0:256], SELF[:, fc * 16:(fc + 1) * 16],
                                  LA[:, fc * 256:(fc + 1) * 256], start=(fc == 0), stop=(fc == 7))
                inc("pe", mm)
                for fc in range(8):
                    e.wait_ge(sems["act"], ACT_NSQ0 + fc + 1)
                    inc("pe", e.matmul(pB0[0:16, 0:256], SELB[:, fc * 16:(fc + 1) * 16],
                                       SQB[:, (fc % 2) * 256:(fc % 2) * 256 + 256],
                                       start=(fc == 0), stop=(fc == 7)))
            blk.tensor(a_mlog_pe)
            PE_ML = cnt["pe"]

            def a_mlog_dve(e, t=t, PE_ML=PE_ML):
                e.wait_ge(sems["pe"], PE_ML - 8)
                inc("dve", e.tensor_copy(out=SMALL[:, 0:256], in_=pA0[0:16, 0:256]))
                w(e, "dve")
                for c in range(4):
                    e.tensor_tensor_scan(out=SMALL[:, 256 + c * 64:256 + (c + 1) * 64],
                                         data0=SMALL[:, c * 64:(c + 1) * 64],
                                         data1=SMALL[:, c * 64:(c + 1) * 64],
                                         initial=0.0, op0=ALU.add, op1=ALU.bypass)
                e.wait_ge(sems["pe"], PE_ML)
                inc("dve", e.tensor_copy(out=SMALL[:, 512:768], in_=pB0[0:16, 0:256]))
            vec_body(a_mlog_dve)
            DVE_QSS = cnt["dve"]
            ACT_KSQ0 = cnt["act"]
            PE_KSS0 = cnt["pe"]

            def a_sqk_act(e, t=t, PE_ML=PE_ML, PE_KSS0=PE_KSS0):
                w(e, "pe", PE_ML)
                for fc in range(8):
                    if fc >= 2:
                        e.wait_ge(sems["pe"], PE_KSS0 + fc - 1)
                    inc("act", e.activation(SQB[:, (fc % 2) * 256:(fc % 2) * 256 + 256],
                                            KF[:, fc * 256:(fc + 1) * 256], AF.Square))
            act_body(a_sqk_act)

            def a_kss_pe(e, t=t, ACT_KSQ0=ACT_KSQ0, DVE_QSS=DVE_QSS):
                w(e, "dve", DVE_QSS)
                for fc in range(8):
                    e.wait_ge(sems["act"], ACT_KSQ0 + fc + 1)
                    inc("pe", e.matmul(pB0[0:16, 0:256], SELB[:, fc * 16:(fc + 1) * 16],
                                       SQB[:, (fc % 2) * 256:(fc % 2) * 256 + 256],
                                       start=(fc == 0), stop=(fc == 7)))
            blk.tensor(a_kss_pe)
            PE_KSS = cnt["pe"]

            def a_kss_dve(e, t=t, PE_KSS=PE_KSS):
                w(e, "pe", PE_KSS)
                inc("dve", e.tensor_copy(out=SMALL[:, 768:1024], in_=pB0[0:16, 0:256]))
            vec_body(a_kss_dve)
            DVE_KSS = cnt["dve"]

            def a_norm_act(e, t=t):
                w(e, "dve", DVE_KSS)
                inc("act", e.activation(SMALL[:, 0:256], SMALL[:, 512:768], AF.Sqrt))
                w(e, "act")
                inc("act", e.activation(SMALL[:, 512:768], SMALL[:, 768:1024], AF.Sqrt))
            act_body(a_norm_act)
            ACT_SQRT = cnt["act"]

            def a_norm_dve(e, t=t):
                w(e, "act", ACT_SQRT)
                inc("dve", e.tensor_scalar(out=SMALL[:, 0:256], in0=SMALL[:, 0:256],
                                           scalar1=1e-12, scalar2=None, op0=ALU.max))
                inc("dve", e.tensor_scalar(out=SMALL[:, 512:768], in0=SMALL[:, 512:768],
                                           scalar1=1e-12, scalar2=None, op0=ALU.max))
                w(e, "dve")
                e.reciprocal(out=SMALL[:, 1024:1280], in_=SMALL[:, 0:256])
                inc("dve", e.reciprocal(out=SMALL[:, 1280:1536], in_=SMALL[:, 512:768]))
            vec_body(a_norm_dve)
            DVE_RQ = cnt["dve"]

            def a_lnr_act(e, t=t):
                w(e, "dve", DVE_RQ)
                inc("act", e.activation(SMALL[:, 512:768], SMALL[:, 1280:1536], AF.Ln))
            act_body(a_lnr_act)
            ACT_LNR = cnt["act"]

            def a_lmq_dve(e, t=t):
                w(e, "act", ACT_LNR)
                inc("dve", e.tensor_tensor(out=SMALL[:, 1792:2048], in0=SMALL[:, 256:512],
                                           in1=SMALL[:, 512:768], op=ALU.subtract))
            vec_body(a_lmq_dve)
            DVE_LMQ = cnt["dve"]

            PE_TT0 = cnt["pe"]
            DVE_TT0 = cnt["dve"]

            def a_tok_pe(e, t=t):
                w(e, "dve", DVE_LMQ)
                for sh in range(2):
                    for j, off in enumerate([1792, 1024, 1280]):
                        k = sh * 3 + j
                        if k >= 1:
                            e.wait_ge(sems["dve"], DVE_TT0 + k)
                        inc("pe", e.transpose(ptrf[:, 0:16],
                                              SMALL[0:16, off + sh * 128: off + sh * 128 + 128],
                                              IDF[0:16, 0:16]))
            blk.tensor(a_tok_pe)

            def a_tok_dve(e, t=t):
                for sh in range(2):
                    s = 2 * t + sh
                    outs = [SCT[:, sh * 16:(sh + 1) * 16], RNQ[:, s * 16:(s + 1) * 16],
                            RKT[:, sh * 16:(sh + 1) * 16]]
                    for j in range(3):
                        k = sh * 3 + j
                        e.wait_ge(sems["pe"], PE_TT0 + k + 1)
                        inc("dve", e.tensor_copy(out=outs[j], in_=ptrf[:, 0:16]))
            vec_body(a_tok_dve)
            DVE_TOK = cnt["dve"]

            PE_KT0 = cnt["pe"]
            DVE_KT0 = cnt["dve"]

            DVE_PRE_KTR = cnt["dve"]

            def a_ktr_pe(e, t=t, DVE_PRE_KTR=DVE_PRE_KTR):
                w(e, "dve", DVE_PRE_KTR)
                for i in range(16):
                    sh, fc = i // 8, i % 8
                    if i >= 4:
                        e.wait_ge(sems["dve"], DVE_KT0 + 2 * (i - 3))
                    inc("pe", e.transpose(ptrbs[i % 4],
                                          KDT[:, fc * 256 + sh * 128: fc * 256 + sh * 128 + 128], IDB[:, :]))
            blk.tensor(a_ktr_pe)

            def a_ktr_dve(e, t=t):
                for i in range(16):
                    sh, fc = i // 8, i % 8
                    e.wait_ge(sems["pe"], PE_KT0 + i + 1)
                    for hh in range(2):
                        inc("dve", e.tensor_scalar(
                            out=XNT[:, sh * 1024 + fc * 128 + hh * 64: sh * 1024 + fc * 128 + hh * 64 + 64],
                            in0=ptrbs[i % 4][:, hh * 64: hh * 64 + 64],
                            scalar1=RKT[:, sh * 16 + 2 * fc + hh: sh * 16 + 2 * fc + hh + 1],
                            scalar2=None, op0=ALU.mult))
            vec_body(a_ktr_dve)
            KTOK = XNT
            DVE_KTOK = cnt["dve"]

            for sh in range(2):
                s = 2 * t + sh
                PE_S0 = cnt["pe"]
                DVE_A0 = cnt["dve"]
                ACT_D0 = cnt["act"]
                stt_cnt = lambda i, b=DVE_A0: b + (1 if i == 0 else 2 * i)
                att_cnt = lambda i, b=DVE_A0: b + (2 * i + 3 if i < 15 else 32)
                exp_cnt = lambda i, b=ACT_D0: b + i + 1

                DVE_NOW = cnt["dve"]

                def g_sb_pe(e, t=t, sh=sh, s=s, DVE_NOW=DVE_NOW, att_cnt=att_cnt):
                    w(e, "dve", DVE_NOW)
                    for i in range(16):
                        hp, hh = i // 2, i % 2
                        if i >= 2:
                            e.wait_ge(sems["dve"], att_cnt(i - 2))
                        hr = slice(hh * 64, hh * 64 + 64)
                        col = hp * 256 + sh * 128
                        e.matmul(pBs[i % 2][:, 0:128],
                                 KF[hr, col:col + 128], QF[hr, col:col + 128], start=True, stop=True)
                        inc("pe", e.matmul(pAs[i % 2][:, 0:128],
                                           BSEL[:, i * 128:(i + 1) * 128],
                                           SMALL[0:16, 256 + sh * 128:256 + sh * 128 + 128],
                                           start=True, stop=True))
                blk.tensor(g_sb_pe)
                PE_SB = cnt["pe"]

                def g_da_dve(e, t=t, sh=sh, s=s, PE_S0=PE_S0, ACT_D0=ACT_D0, exp_cnt=exp_cnt):
                    w(e, "dout")
                    def emit_stt(i):
                        e.wait_ge(sems["pe"], PE_S0 + i + 1)
                        if i >= 2:
                            e.wait_ge(sems["act"], exp_cnt(i - 2))
                        inc("dve", e.scalar_tensor_tensor(
                            out=SCR[:, (i % 2) * 128:(i % 2) * 128 + 128],
                            in0=pAs[i % 2][:, 0:128],
                            scalar=SCT[:, sh * 16 + i: sh * 16 + i + 1],
                            in1=MSK[:, :], op0=ALU.subtract, op1=ALU.add))

                    def emit_att(i):
                        e.wait_ge(sems["act"], exp_cnt(i))
                        inc("dve", e.tensor_tensor(
                            out=ATTB[:, i * 128:(i + 1) * 128],
                            in0=pBs[i % 2][:, 0:128],
                            in1=SQB[:, (i % 4) * 128:(i % 4) * 128 + 128], op=ALU.mult))
                    emit_stt(0)
                    emit_stt(1)
                    for i in range(2, 16):
                        emit_att(i - 2)
                        emit_stt(i)
                    emit_att(14)
                    emit_att(15)
                vec_body(g_da_dve)

                def g_exp_act(e, t=t, sh=sh, s=s, stt_cnt=stt_cnt):
                    for i in range(16):
                        e.wait_ge(sems["dve"], stt_cnt(i))
                        inc("act", e.activation(SQB[:, (i % 4) * 128:(i % 4) * 128 + 128],
                                                SCR[:, (i % 2) * 128:(i % 2) * 128 + 128], AF.Exp))
                act_body(g_exp_act)

                PE_KV0 = cnt["pe"]

                def g_kv_pe(e, t=t, sh=sh, s=s):
                    w(e, "dve")
                    e.matmul(pD[:, :], ONESB[:, :], ZROW[:, :], start=True, stop=False)
                    e.matmul(pD2[:, :], ONESB[:, :], ZROW[:, :], start=True, stop=False)
                    for h in range(16):
                        hp, hh = h // 2, h % 2
                        hr = slice(hh * 64, hh * 64 + 64)
                        mm = e.matmul(pD[hr, hp * 64:(hp + 1) * 64],
                                      KTOK[0:64, sh * 1024 + hp * 128 + hh * 64: sh * 1024 + hp * 128 + hh * 64 + 64],
                                      VT[0:64, sh * 1024 + h * 64: sh * 1024 + (h + 1) * 64],
                                      start=False, stop=False)
                    inc("pe", mm)
                    w(e, "pe")
                    for h in range(16):
                        hp, hh = h // 2, h % 2
                        hr = slice(hh * 64, hh * 64 + 64)
                        e.matmul(pD2[hr, hp * 64:(hp + 1) * 64],
                                 KTOK[64:128, sh * 1024 + hp * 128 + hh * 64: sh * 1024 + hp * 128 + hh * 64 + 64],
                                 VT[64:128, sh * 1024 + h * 64: sh * 1024 + (h + 1) * 64],
                                 start=False, stop=False)
                    e.matmul(pD[:, :], ONESB[:, :], ZROW[:, :], start=False, stop=True)
                    mm = e.matmul(pD2[:, :], ONESB[:, :], ZROW[:, :], start=False, stop=True)
                    inc("pe", mm)
                blk.tensor(g_kv_pe)
                PE_KV1 = cnt["pe"]

                def g_sup_dve(e, t=t, sh=sh, s=s, PE_KV1=PE_KV1):
                    w(e, "pe", PE_KV1)
                    inc("dve", e.tensor_copy(out=SbO[:, :], in_=Sb[:, :]))
                    ci = t * 4 + sh * 2 + 0
                    last = None
                    for hp in range(8):
                        last = e.scalar_tensor_tensor(out=S[:, hp * 64:(hp + 1) * 64],
                                                      in0=S[:, hp * 64:(hp + 1) * 64],
                                                      scalar=CDA[:, hp * 32 + ci: hp * 32 + ci + 1],
                                                      in1=pD[:, hp * 64:(hp + 1) * 64],
                                                      op0=ALU.mult, op1=ALU.add)
                    inc("dve", last)
                    w(e, "dve")
                    inc("dve", e.tensor_copy(out=SbM[:, :], in_=S[:, :]))
                    w(e, "dve")
                    ci2 = t * 4 + sh * 2 + 1
                    last = None
                    for hp in range(8):
                        last = e.scalar_tensor_tensor(out=S[:, hp * 64:(hp + 1) * 64],
                                                      in0=S[:, hp * 64:(hp + 1) * 64],
                                                      scalar=CDA[:, hp * 32 + ci2: hp * 32 + ci2 + 1],
                                                      in1=pD2[:, hp * 64:(hp + 1) * 64],
                                                      op0=ALU.mult, op1=ALU.add)
                    inc("dve", last)
                    w(e, "dve")
                    inc("dve", e.tensor_copy(out=Sb[:, :], in_=S[:, :]))
                blk.vector(g_sup_dve)
                DVE_SUP2 = cnt["dve"]

                def g_o_pe(e, t=t, sh=sh, s=s, att_cnt=att_cnt, DVE_SUP2=DVE_SUP2):
                    w(e, "dve")
                    for h in range(16):
                        e.wait_ge(sems["dve"], att_cnt(h))
                        hp, hh = h // 2, h % 2
                        hr = slice(hh * 64, hh * 64 + 64)
                        pc = pCs[h // 8]
                        col = slice((h % 8) * 64, (h % 8) * 64 + 64)
                        e.matmul(pc[:, col], ATTB[:, h * 128:(h + 1) * 128],
                                 VT[:, sh * 1024 + h * 64: sh * 1024 + (h + 1) * 64],
                                 start=(h % 8 == 0), stop=False)
                        e.matmul(pc[0:64, col],
                                 QDECT[hr, hp * 256 + sh * 128: hp * 256 + sh * 128 + 64],
                                 SbO[hr, hp * 64:(hp + 1) * 64], start=False, stop=False)
                        e.matmul(pc[64:128, col],
                                 QDECT[hr, hp * 256 + sh * 128 + 64: hp * 256 + sh * 128 + 128],
                                 SbM[hr, hp * 64:(hp + 1) * 64], start=False, stop=False)
                        if h % 8 == 7:
                            mm = e.matmul(pc[:, :], ONESB[:, :], ZROW[:, :], start=False, stop=True)
                    inc("pe", mm)
                blk.tensor(g_o_pe)
                PE_O2 = cnt["pe"]

                def g_oe_dve(e, t=t, sh=sh, s=s, PE_O2=PE_O2):
                    w(e, "pe", PE_O2)
                    last = None
                    for h in range(16):
                        last = e.tensor_scalar(out=OTOK[:, h * 64:(h + 1) * 64],
                                               in0=pCs[h // 8][:, (h % 8) * 64:(h % 8) * 64 + 64],
                                               scalar1=RNQ[:, s * 16 + h: s * 16 + h + 1],
                                               scalar2=None, op0=ALU.mult)
                    inc("dve", last)
                blk.vector(g_oe_dve)
                DVE_OE = cnt["dve"]


                PE_OT0 = cnt["pe"]
                DVE_OT0 = cnt["dve"]

                def g_ot_pe(e, t=t, sh=sh, s=s, DVE_OE=DVE_OE, DVE_OT0=DVE_OT0):
                    w(e, "dve", DVE_OE)
                    for fc in range(8):
                        if fc >= 4:
                            e.wait_ge(sems["dve"], DVE_OT0 + fc - 3)
                        inc("pe", e.transpose(ptrbs[fc % 4],
                                              OTOK[:, fc * 128:(fc + 1) * 128], IDB[:, :]))
                blk.tensor(g_ot_pe)

                def g_ot_dve(e, t=t, sh=sh, s=s, PE_OT0=PE_OT0):
                    for fc in range(8):
                        e.wait_ge(sems["pe"], PE_OT0 + fc + 1)
                        inc("dve", e.tensor_copy(out=OT[:, fc * 128:(fc + 1) * 128],
                                                 in_=ptrbs[fc % 4]))
                vec_body(g_ot_dve)
                DVE_OTD = cnt["dve"]

                def g_wo_pe(e, t=t, sh=sh, s=s, DVE_OTD=DVE_OTD):
                    w(e, "dve", DVE_OTD)
                    if t == 0 and sh == 0:
                        e.wait_ge(sems["act"], ACT_WBASE + 41)
                    for nh in range(2):
                        for kc in range(8):
                            mm = e.matmul(pCs[nh][:, :], OT[:, kc * 128:(kc + 1) * 128],
                                          WOs[:, kc * 1024 + nh * 512: kc * 1024 + (nh + 1) * 512],
                                          start=(kc == 0), stop=(kc == 7))
                    inc("pe", mm)
                blk.tensor(g_wo_pe)
                PE_WO = cnt["pe"]

                def g_y2_dve(e, t=t, sh=sh, s=s, PE_WO=PE_WO):
                    w(e, "pe", PE_WO)
                    if s == 4:
                        w(e, "act", ACT_WEND)
                    e.tensor_tensor(out=Y2[:, s * 1024: s * 1024 + 512],
                                    in0=pC0[:, :], in1=YTI[:, sh * 1024: sh * 1024 + 512], op=ALU.add)
                    inc("dve", e.tensor_tensor(out=Y2[:, s * 1024 + 512:(s + 1) * 1024],
                                               in0=pC1[:, :], in1=YTI[:, sh * 1024 + 512:(sh + 1) * 1024],
                                               op=ALU.add))
                vec_body(g_y2_dve)

        global_DOUT_PUB = [0]
        _dox = DO_X
        def x_acore_dve(e):
            w(e, "act")
            last = None
            for hp in range(8):
                last = e.reduce_sum(out=ABUF[:, hp:hp + 1], in_=LAL[:, hp * 32:(hp + 1) * 32], axis=AX.X)
            inc("dve", last)
        vec_body(x_acore_dve)
        DVE_AC = cnt["dve"]

        def x_acore_act(e):
            w(e, "dve", DVE_AC)
            inc("act", e.activation(ABUF[:, :], ABUF[:, :], AF.Exp))
        if _dox: act_body(x_acore_act)
        ACT_AC = cnt["act"]

        DVE_X0 = cnt["dve"]

        def x_pub_gps(e):
            w(e, "act", ACT_AC)
            w(e, "dve")
            inc("dout", e.dma_start(out=cc_src[:, 0:512], in_=S[:, :]), 16)
            inc("dout", e.dma_start(out=cc_src[:, 512:520], in_=ABUF[:, :]), 16)
            e.wait_ge(sems["dout"], cnt["dout"])
            global_DOUT_PUB[0] = cnt["dout"]
            if USE_CC:
                inc("cc", e.collective_compute("AllGather", ALU.bypass,
                                               replica_groups=[list(range(NCORE))],
                                               ins=[cc_src], outs=[cc_dst]), 1)
                e.wait_ge(sems["cc"], cnt["cc"])
            else:
                for r in range(NCORE):
                    inc("cc", e.dma_start(out=cc_dst[r * 128:(r + 1) * 128, :], in_=SGX[:, 0:520]), 16)
                e.wait_ge(sems["cc"], cnt["cc"])
            for j, r in enumerate(range(NCORE - 1, -1, -1)):
                if j >= 2:
                    e.wait_ge(sems["dve"], DVE_X0 + 5 * (j - 1))
                inc("dout", e.dma_start(out=SGX[:, (j % 2) * 520:(j % 2) * 520 + 520],
                                        in_=cc_dst[r * 128:(r + 1) * 128, :]), 16)
                w(e, "dout")
        if _dox: blk.gpsimd(x_pub_gps)
        else: global_DOUT_PUB[0] = cnt['dout']

        DOUT_PUB = global_DOUT_PUB[0]

        def x_sin_dve(e):
            e.wait_ge(sems["dout"], DOUT_PUB)
            e.memset(S[:, :], 0.0)
            for j, r in enumerate(range(NCORE - 1, -1, -1)):
                sgr = SGX[:, (j % 2) * 520:(j % 2) * 520 + 520]
                e.wait_ge(sems["dout"], DOUT_PUB + (j + 1) * 16)
                w(e, "dve")
                inc("dve", e.tensor_scalar(out=PM[:, :], in0=PVEC[:, :], scalar1=M1S[:, r:r + 1],
                                           scalar2=None, op0=ALU.mult))
                w(e, "dve")
                last = None
                for hp in range(8):
                    last = e.scalar_tensor_tensor(out=S[:, hp * 64:(hp + 1) * 64],
                                                  in0=sgr[:, hp * 64:(hp + 1) * 64],
                                                  scalar=PM[:, hp:hp + 1],
                                                  in1=S[:, hp * 64:(hp + 1) * 64],
                                                  op0=ALU.mult, op1=ALU.add)
                inc("dve", last)
                w(e, "dve")
                inc("dve", e.tensor_scalar(out=PM[:, :], in0=sgr[:, 512:520],
                                           scalar1=1.0, scalar2=None, op0=ALU.subtract))
                w(e, "dve")
                inc("dve", e.tensor_scalar(out=PM[:, :], in0=PM[:, :], scalar1=M1S[:, r:r + 1],
                                           scalar2=1.0, op0=ALU.mult, op1=ALU.add))
                w(e, "dve")
                inc("dve", e.tensor_tensor(out=PVEC[:, :], in0=PVEC[:, :], in1=PM[:, :], op=ALU.mult))
            w(e, "dve")
            inc("dve", e.tensor_copy(out=Sb[:, :], in_=S[:, :]))
        if _dox: vec_body(x_sin_dve)

        BSTEP = DO_B if isinstance(DO_B, int) and not isinstance(DO_B, bool) else (99 if DO_B else 0)
        for t in (range(NT) if BSTEP > 0 else []):
            PE_QL = cnt["pe"]

            def b_qload(e, t=t, PE_QL=PE_QL):
                w(e, "pe", PE_QL)
                inc("dout", e.dma_start(out=QDECT[:, :].rearrange("p (f c) -> p f c", c=256),
                                        in_=qdd[:, :, t * 256:(t + 1) * 256].rearrange("f p c -> p f c")), 16)
                w(e, "dout")
            if BSTEP >= 1: blk.gpsimd(b_qload)
            DOUT_QL = cnt["dout"]

            for sh in range(2):
                s_ = 2 * t + sh

                DVE_PB0 = cnt["dve"]

                def b_sup_dve(e, t=t, sh=sh, s=s_):
                    w(e, "dve")
                    inc("dve", e.tensor_copy(out=SbO[:, :], in_=Sb[:, :]))
                    w(e, "dve")
                    ci = s * 2
                    last = None
                    for hp in range(8):
                        last = e.tensor_scalar(out=S[:, hp * 64:(hp + 1) * 64], in0=S[:, hp * 64:(hp + 1) * 64],
                                               scalar1=CDA[:, hp * 32 + ci: hp * 32 + ci + 1],
                                               scalar2=None, op0=ALU.mult)
                    inc("dve", last)
                    w(e, "dve")
                    inc("dve", e.tensor_copy(out=SbM[:, :], in_=S[:, :]))
                    w(e, "dve")
                    ci2 = s * 2 + 1
                    last = None
                    for hp in range(8):
                        last = e.tensor_scalar(out=S[:, hp * 64:(hp + 1) * 64], in0=S[:, hp * 64:(hp + 1) * 64],
                                               scalar1=CDA[:, hp * 32 + ci2: hp * 32 + ci2 + 1],
                                               scalar2=None, op0=ALU.mult)
                    inc("dve", last)
                    w(e, "dve")
                    inc("dve", e.tensor_copy(out=Sb[:, :], in_=S[:, :]))
                blk.vector(b_sup_dve)
                DVE_PB1 = cnt["dve"]

                def b_corr_pe(e, t=t, sh=sh, s=s_, DOUT_QL=DOUT_QL, DVE_PB1=DVE_PB1):
                    w(e, "dve", DVE_PB1)
                    e.wait_ge(sems["dout"], DOUT_QL)
                    mm = e.matmul(pC0[:, :], ONESB[:, :], ZROW[:, :], start=True, stop=False)
                    mm = e.matmul(pC1[:, :], ONESB[:, :], ZROW[:, :], start=True, stop=False)
                    for hh_run in range(2):
                        inc("pe", mm)
                        w(e, "pe")
                        for h in range(hh_run, 16, 2):
                            hp = h // 2
                            hr = slice(hh_run * 64, hh_run * 64 + 64)
                            pc = pCs[h // 8]
                            col = slice((h % 8) * 64, (h % 8) * 64 + 64)
                            e.matmul(pc[0:64, col],
                                     QDECT[hr, hp * 256 + sh * 128: hp * 256 + sh * 128 + 64],
                                     SbO[hr, hp * 64:(hp + 1) * 64], start=False, stop=False)
                            mm = e.matmul(pc[64:128, col],
                                          QDECT[hr, hp * 256 + sh * 128 + 64: hp * 256 + sh * 128 + 128],
                                          SbM[hr, hp * 64:(hp + 1) * 64], start=False, stop=False)
                    inc("pe", mm)
                    w(e, "pe")
                    e.matmul(pC0[:, :], ONESB[:, :], ZROW[:, :], start=False, stop=True)
                    mm = e.matmul(pC1[:, :], ONESB[:, :], ZROW[:, :], start=False, stop=True)
                    inc("pe", mm)
                blk.tensor(b_corr_pe)
                PE_CB2 = cnt["pe"]

                def b_su1_dve(e, s=s_, PE_CB2=PE_CB2):
                    w(e, "pe", PE_CB2)
                    last = None
                    for h in range(16):
                        last = e.tensor_scalar(out=OTOK[:, h * 64:(h + 1) * 64],
                                               in0=pCs[h // 8][:, (h % 8) * 64:(h % 8) * 64 + 64],
                                               scalar1=RNQ[:, s * 16 + h: s * 16 + h + 1],
                                               scalar2=None, op0=ALU.mult)
                    inc("dve", last)
                vec_body(b_su1_dve)
                DVE_BE = cnt["dve"]

                PE_BT0 = cnt["pe"]
                DVE_BT0 = cnt["dve"]

                def b_ot_pe(e, s=s_, DVE_BE=DVE_BE, DVE_BT0=DVE_BT0):
                    w(e, "dve", DVE_BE)
                    for fc in range(8):
                        if fc >= 4:
                            e.wait_ge(sems["dve"], DVE_BT0 + fc - 3)
                        inc("pe", e.transpose(ptrbs[fc % 4],
                                              OTOK[:, fc * 128:(fc + 1) * 128], IDB[:, :]))
                if BSTEP >= 6: blk.tensor(b_ot_pe)

                def b_ot_dve(e, s=s_, PE_BT0=PE_BT0):
                    for fc in range(8):
                        e.wait_ge(sems["pe"], PE_BT0 + fc + 1)
                        inc("dve", e.tensor_copy(out=OT[:, fc * 128:(fc + 1) * 128],
                                                 in_=ptrbs[fc % 4]))
                if BSTEP >= 6: vec_body(b_ot_dve)
                DVE_BTD = cnt["dve"]

                def b_wo_pe(e, s=s_, DVE_BTD=DVE_BTD):
                    w(e, "dve", DVE_BTD)
                    for nh in range(2):
                        for kc in range(8):
                            mm = e.matmul(pCs[nh][:, :], OT[:, kc * 128:(kc + 1) * 128],
                                          WOs[:, kc * 1024 + nh * 512: kc * 1024 + (nh + 1) * 512],
                                          start=(kc == 0), stop=(kc == 7))
                    inc("pe", mm)
                if BSTEP >= 7: blk.tensor(b_wo_pe)
                PE_BWO = cnt["pe"]

                def b_y2_dve(e, s=s_, PE_BWO=PE_BWO):
                    w(e, "pe", PE_BWO)
                    e.tensor_tensor(out=Y2[:, s * 1024: s * 1024 + 512],
                                    in0=pC0[:, :], in1=Y2[:, s * 1024: s * 1024 + 512], op=ALU.add)
                    inc("dve", e.tensor_tensor(out=Y2[:, s * 1024 + 512:(s + 1) * 1024],
                                               in0=pC1[:, :], in1=Y2[:, s * 1024 + 512:(s + 1) * 1024],
                                               op=ALU.add))
                if BSTEP >= 7: vec_body(b_y2_dve)

        def y2_spill(e):
            w(e, "dve")
            inc("dout", e.dma_start(out=y2d.rearrange("(s p) d -> p s d", p=128),
                                    in_=Y2[:, :].rearrange("p (s d) -> p s d", d=1024)), 16)
            w(e, "dout")
        blk.gpsimd(y2_spill)
        DOUT_Y2 = cnt["dout"]

        def bar_pe(e):
            w(e, "dve")
            inc("pe", e.nop())
        blk.tensor(bar_pe)
        PE_BAR = cnt["pe"]

        def bar_act(e):
            w(e, "pe", PE_BAR)
            inc("act", e.nop())
        act_body(bar_act)
        ACT_BAR = cnt["act"]
        stA.close()

        stC = ExitStack()
        EC = stC.enter_context
        W1S = EC(nc.sbuf_tensor("W1S", [128, 8 * FFN], BF16))
        W2S = EC(nc.sbuf_tensor("W2S", [128, 32 * 1024], BF16))
        KQ8S = EC(nc.sbuf_tensor("KQ8S", [128, 8 * 1024], FP8))
        Y2T = EC(nc.sbuf_tensor("Y2T", [128, 2048], BF16))
        XTc = EC(nc.sbuf_tensor("XTc", [128, 2048], F32))
        XY = EC(nc.sbuf_tensor("XYc", [128, 2048], F32))
        X1F = EC(nc.sbuf_tensor("X1F", [128, 2048], F32))
        HB = EC(nc.sbuf_tensor("HB", [128, 2048], BF16))
        HT = EC(nc.sbuf_tensor("HT", [128, 2048], BF16))
        G1 = EC(nc.sbuf_tensor("G1", [128, 32 * 256], BF16))
        ZQ8 = EC(nc.sbuf_tensor("ZQ8", [128, 2048], FP8))
        YBF = EC(nc.sbuf_tensor("YBF", [128, 1024], BF16))
        YBF2 = EC(nc.sbuf_tensor("YBF2", [128, 1024], BF16))
        STATSc = EC(nc.sbuf_tensor("STATSc", [128, 32], F32))
        WSTc = XTc
        WSTc2 = XY
        SCRc = XTc
        ZB = HB
        ZT = HT
        OUTB = XTc

        DIN_CW = cnt["din"]
        ACT_CW = cnt["act"]
        DWC0 = cnt["dw"]
        CSLOT = [XTc[:, 0:1024], XTc[:, 1024:2048], XY[:, 0:1024], XY[:, 1024:2048],
                 X1F[:, 0:1024], X1F[:, 1024:2048]]
        FKSLOT = [HB[:, 0:1024], HB[:, 1024:2048],
                  X1F.bitcast(BF16)[:, 0:1024], X1F.bitcast(BF16)[:, 1024:2048]]
        wsrcs = []
        for kc in range(8):
            for pp in range(4):
                wsrcs.append((W1[kc * 128:(kc + 1) * 128, pp * 1024:(pp + 1) * 1024],
                              W1S[:, kc * FFN + pp * 1024: kc * FFN + (pp + 1) * 1024]))
        for kc in range(32):
            wsrcs.append((W2[kc * 128:(kc + 1) * 128, :], W2S[:, kc * 1024:(kc + 1) * 1024]))
        NWC = len(wsrcs)

        def c_wload(e):
            w(e, "act", ACT_BAR)
            w(e, "dout")
            for i in range(NWC):
                e.wait_ge(sems["act"], ACT_CW + max(0, i - 5))
                inc("dw", e.dma_start(out=CSLOT[i % 6], in_=wsrcs[i][0]), 16)
            for kc in range(8):
                e.wait_ge(sems["act"], ACT_CW + NWC + max(0, kc - 3))
                inc("dw", e.dma_start(out=FKSLOT[kc % 4],
                                      in_=fk[kc * 128:(kc + 1) * 128, :]), 16)
        blk.sync(c_wload)

        def c_wcast(e):
            w(e, "dout")
            for i in range(NWC):
                e.wait_ge(sems["dw"], DWC0 + min(i + 6, NWC) * 16)
                inc("act", e.activation(wsrcs[i][1], CSLOT[i % 6], AF.Copy))
            for kc in range(8):
                e.wait_ge(sems["dw"], DWC0 + min(NWC + kc + 4, NWC + 8) * 16)
                inc("act", e.activation(KQ8S[:, kc * 1024:(kc + 1) * 1024],
                                        FKSLOT[kc % 4],
                                        AF.Copy, scale=RSK[:, 0:1]))
        act_body(c_wcast)
        ACT_CWE = cnt["act"]

        for t in (range(NT) if DO_C else []):
            def c_x(e, t=t):
                w(e, "dve")
                w(e, "act")
                w(e, "dout")
                if t == 0:
                    e.wait_ge(sems["dout"], DOUT_Y2)
                for sh in range(2):
                    inc("din", e.dma_start(out=XTc[:, sh * 1024:(sh + 1) * 1024],
                                           in_=x_in[t * 256 + sh * 128: t * 256 + sh * 128 + 128, :]), 16)
                    inc("din", e.dma_start(out=Y2T[:, sh * 1024:(sh + 1) * 1024],
                                           in_=y2d[t * 256 + sh * 128: t * 256 + sh * 128 + 128, :]), 16)
                w(e, "din")
            blk.sync(c_x)

            def c_xy_dve(e, t=t):
                w(e, "din")
                w(e, "act", ACT_CWE)
                for sh in range(2):
                    inc("dve", e.tensor_tensor(out=XY[:, sh * 1024:(sh + 1) * 1024],
                                               in0=XTc[:, sh * 1024:(sh + 1) * 1024],
                                               in1=Y2T[:, sh * 1024:(sh + 1) * 1024], op=ALU.add))
            vec_body(c_xy_dve)

            def c_sq2(e, t=t):
                w(e, "act", ACT_CWE)
                w(e, "dve")
                for sh in range(2):
                    c0 = sh * 16
                    inc("act", e.activation(SCRc[:, sh * 1024:(sh + 1) * 1024],
                                            Y2T[:, sh * 1024:(sh + 1) * 1024], AF.Square,
                                            accum_out=STATSc[:, c0 + 1:c0 + 2]))
            act_body(c_sq2)
            ACT_S2 = cnt["act"]

            def c_ln2(e, t=t, ACT_S2=ACT_S2):
                for sh in range(2):
                    c0 = sh * 16
                    inc("dve", e.reduce_sum(out=STATSc[:, c0:c0 + 1],
                                            in_=Y2T[:, sh * 1024:(sh + 1) * 1024], axis=AX.X))
                    w(e, "dve")
                    inc("dve", e.tensor_scalar(out=STATSc[:, c0 + 2:c0 + 3], in0=STATSc[:, c0:c0 + 1],
                                               scalar1=1.0 / D, scalar2=None, op0=ALU.mult))
                    w(e, "dve")
                    inc("dve", e.tensor_tensor(out=STATSc[:, c0 + 3:c0 + 4], in0=STATSc[:, c0 + 2:c0 + 3],
                                               in1=STATSc[:, c0 + 2:c0 + 3], op=ALU.mult))
                    e.wait_ge(sems["act"], ACT_S2 - 1 + sh)
                    w(e, "dve")
                    inc("dve", e.scalar_tensor_tensor(out=STATSc[:, c0 + 4:c0 + 5], in0=STATSc[:, c0 + 1:c0 + 2],
                                                      scalar=1.0 / D, in1=STATSc[:, c0 + 3:c0 + 4],
                                                      op0=ALU.mult, op1=ALU.subtract))
            vec_body(c_ln2)
            DVE_V2 = cnt["dve"]

            def c_sd2(e, t=t):
                w(e, "dve", DVE_V2)
                for sh in range(2):
                    c0 = sh * 16
                    inc("act", e.activation(STATSc[:, c0 + 5:c0 + 6], STATSc[:, c0 + 4:c0 + 5],
                                            AF.Sqrt, bias=EPSC[:, 0:1]))
            act_body(c_sd2)
            ACT_SD2 = cnt["act"]

            def c_h(e, t=t):
                w(e, "act", ACT_SD2)
                e.reciprocal(out=STATSc[:, 6:7], in_=STATSc[:, 5:6])
                inc("dve", e.reciprocal(out=STATSc[:, 22:23], in_=STATSc[:, 21:22]))
                w(e, "dve")
                for sh in range(2):
                    c0 = sh * 16
                    inc("dve", e.tensor_scalar(out=HB[:, sh * 1024:(sh + 1) * 1024],
                                               in0=Y2T[:, sh * 1024:(sh + 1) * 1024],
                                               scalar1=STATSc[:, c0 + 2:c0 + 3],
                                               scalar2=STATSc[:, c0 + 6:c0 + 7],
                                               op0=ALU.subtract, op1=ALU.mult))
            vec_body(c_h)
            DVE_H = cnt["dve"]

            PE_HT0 = cnt["pe"]
            DVE_HT0 = cnt["dve"]

            def c_ht_pe(e, t=t, DVE_H=DVE_H, DVE_HT0=DVE_HT0):
                w(e, "dve", DVE_H)
                for i in range(16):
                    sh, fc = i // 8, i % 8
                    if i >= 4:
                        e.wait_ge(sems["dve"], DVE_HT0 + i - 3)
                    inc("pe", e.transpose(ptrbs[i % 4],
                                          HB[:, sh * 1024 + fc * 128: sh * 1024 + fc * 128 + 128], IDB[:, :]))
            blk.tensor(c_ht_pe)

            def c_ht_dve(e, t=t, PE_HT0=PE_HT0):
                for i in range(16):
                    sh, fc = i // 8, i % 8
                    e.wait_ge(sems["pe"], PE_HT0 + i + 1)
                    inc("dve", e.tensor_scalar(out=HT[:, fc * 256 + sh * 128: fc * 256 + sh * 128 + 128],
                                               in0=ptrbs[i % 4],
                                               scalar1=PRM[:, 24 + fc:25 + fc], scalar2=PRM[:, 32 + fc:33 + fc],
                                               op0=ALU.mult, op1=ALU.add))
            vec_body(c_ht_dve)
            DVE_HT = cnt["dve"]

            PE_W10 = cnt["pe"]
            ACT_G10 = cnt["act"]

            def c_w1_pe(e, t=t, DVE_HT=DVE_HT, ACT_G10=ACT_G10):
                w(e, "dve", DVE_HT)
                for oc in range(32):
                    if oc >= 2:
                        e.wait_ge(sems["act"], ACT_G10 + oc - 1)
                    for kc in range(8):
                        mm = e.matmul(pAs[oc % 2][:, :],
                                      W1S[:, kc * FFN + oc * 128: kc * FFN + (oc + 1) * 128],
                                      HT[:, kc * 256:(kc + 1) * 256], start=(kc == 0), stop=(kc == 7))
                    inc("pe", mm)
            blk.tensor(c_w1_pe)

            def c_gelu_act(e, t=t, PE_W10=PE_W10):
                for oc in range(32):
                    e.wait_ge(sems["pe"], PE_W10 + oc + 1)
                    inc("act", e.activation(G1[:, oc * 256:(oc + 1) * 256], pAs[oc % 2][:, :],
                                            AF.Gelu, bias=B1C[:, oc:oc + 1]))
            act_body(c_gelu_act)
            ACT_G1E = cnt["act"]

            for th in range(2):
                def c_w2_pe(e, t=t, th=th, ACT_G1E=ACT_G1E):
                    w(e, "act", ACT_G1E)
                    if th == 1:
                        w(e, "dve")
                    for oc in range(32):
                        for nh in range(2):
                            e.matmul(pCs[nh][:, :],
                                     G1[:, oc * 256 + th * 128: oc * 256 + th * 128 + 128],
                                     W2S[:, oc * 1024 + nh * 512: oc * 1024 + (nh + 1) * 512],
                                     start=(oc == 0), stop=False)
                    for nh in range(2):
                        mm = e.matmul(pCs[nh][:, :], ONESB[:, :],
                                      B2ROWB[:, nh * 512:(nh + 1) * 512], start=False, stop=True)
                    inc("pe", mm)
                blk.tensor(c_w2_pe)
                PE_W2 = cnt["pe"]

                def c_x1f_dve(e, t=t, th=th, PE_W2=PE_W2):
                    w(e, "pe", PE_W2)
                    e.scalar_tensor_tensor(out=X1F[:, th * 1024: th * 1024 + 512],
                                           in0=pC0[:, :], scalar=0.0,
                                           in1=XY[:, th * 1024: th * 1024 + 512],
                                           op0=ALU.bypass, op1=ALU.add)
                    inc("dve", e.scalar_tensor_tensor(out=X1F[:, th * 1024 + 512:(th + 1) * 1024],
                                                      in0=pC1[:, :], scalar=0.0,
                                                      in1=XY[:, th * 1024 + 512:(th + 1) * 1024],
                                                      op0=ALU.bypass, op1=ALU.add))
                vec_body(c_x1f_dve)

            DVE_X1F = cnt["dve"]

            def c_sq3(e, t=t, DVE_X1F=DVE_X1F):
                w(e, "dve", DVE_X1F)
                for sh in range(2):
                    c0 = sh * 16
                    inc("act", e.activation(SCRc[:, sh * 1024:(sh + 1) * 1024],
                                            X1F[:, sh * 1024:(sh + 1) * 1024], AF.Square,
                                            accum_out=STATSc[:, c0 + 1:c0 + 2]))
            act_body(c_sq3)
            ACT_S3 = cnt["act"]

            def c_ln3(e, t=t, DVE_X1F=DVE_X1F, ACT_S3=ACT_S3):
                w(e, "dve", DVE_X1F)
                for sh in range(2):
                    c0 = sh * 16
                    inc("dve", e.reduce_sum(out=STATSc[:, c0:c0 + 1],
                                            in_=X1F[:, sh * 1024:(sh + 1) * 1024], axis=AX.X))
                    w(e, "dve")
                    inc("dve", e.tensor_scalar(out=STATSc[:, c0 + 2:c0 + 3], in0=STATSc[:, c0:c0 + 1],
                                               scalar1=1.0 / D, scalar2=None, op0=ALU.mult))
                    w(e, "dve")
                    inc("dve", e.tensor_tensor(out=STATSc[:, c0 + 3:c0 + 4], in0=STATSc[:, c0 + 2:c0 + 3],
                                               in1=STATSc[:, c0 + 2:c0 + 3], op=ALU.mult))
                    e.wait_ge(sems["act"], ACT_S3 - 1 + sh)
                    w(e, "dve")
                    inc("dve", e.scalar_tensor_tensor(out=STATSc[:, c0 + 4:c0 + 5], in0=STATSc[:, c0 + 1:c0 + 2],
                                                      scalar=1.0 / D, in1=STATSc[:, c0 + 3:c0 + 4],
                                                      op0=ALU.mult, op1=ALU.subtract))
            vec_body(c_ln3)
            DVE_V3 = cnt["dve"]

            def c_sd3(e, t=t, DVE_V3=DVE_V3):
                w(e, "dve", DVE_V3)
                for sh in range(2):
                    c0 = sh * 16
                    inc("act", e.activation(STATSc[:, c0 + 5:c0 + 6], STATSc[:, c0 + 4:c0 + 5],
                                            AF.Sqrt, bias=EPSC[:, 0:1]))
            act_body(c_sd3)
            ACT_SD3 = cnt["act"]

            def c_z(e, t=t, ACT_SD3=ACT_SD3):
                w(e, "act", ACT_SD3)
                e.reciprocal(out=STATSc[:, 6:7], in_=STATSc[:, 5:6])
                inc("dve", e.reciprocal(out=STATSc[:, 22:23], in_=STATSc[:, 21:22]))
                w(e, "dve")
                for sh in range(2):
                    c0 = sh * 16
                    inc("dve", e.tensor_scalar(out=ZB[:, sh * 1024:(sh + 1) * 1024],
                                               in0=X1F[:, sh * 1024:(sh + 1) * 1024],
                                               scalar1=STATSc[:, c0 + 2:c0 + 3],
                                               scalar2=STATSc[:, c0 + 6:c0 + 7],
                                               op0=ALU.subtract, op1=ALU.mult))
            vec_body(c_z)
            DVE_Z = cnt["dve"]

            PE_ZT0 = cnt["pe"]
            DVE_ZT0 = cnt["dve"]

            def c_zt_pe(e, t=t, DVE_Z=DVE_Z, DVE_ZT0=DVE_ZT0):
                w(e, "dve", DVE_Z)
                for i in range(16):
                    sh, fc = i // 8, i % 8
                    if i >= 4:
                        e.wait_ge(sems["dve"], DVE_ZT0 + i - 3)
                    inc("pe", e.transpose(ptrbs[i % 4],
                                          ZB[:, sh * 1024 + fc * 128: sh * 1024 + fc * 128 + 128], IDB[:, :]))
            blk.tensor(c_zt_pe)

            def c_zt_dve(e, t=t, PE_ZT0=PE_ZT0):
                for i in range(16):
                    sh, fc = i // 8, i % 8
                    e.wait_ge(sems["pe"], PE_ZT0 + i + 1)
                    inc("dve", e.tensor_copy(out=ZT[:, fc * 256 + sh * 128: fc * 256 + sh * 128 + 128],
                                             in_=ptrbs[i % 4]))
            vec_body(c_zt_dve)
            DVE_ZT = cnt["dve"]

            def c_zq_act(e, t=t, DVE_ZT=DVE_ZT):
                w(e, "dve", DVE_ZT)
                for fc in range(8):
                    inc("act", e.activation(ZQ8[:, fc * 256:(fc + 1) * 256],
                                            ZT[:, fc * 256:(fc + 1) * 256],
                                            AF.Copy, scale=RSIN[:, 0:1]))
            act_body(c_zq_act)
            ACT_ZQ = cnt["act"]

            for th in range(2):
                def c_fk_pe(e, t=t, th=th):
                    w(e, "act")
                    w(e, "dve")
                    for nh in range(2):
                        for kc in range(8):
                            mm = e.matmul(pCs[nh][:, :],
                                          ZQ8[:, kc * 256 + th * 128: kc * 256 + th * 128 + 128],
                                          KQ8S[:, kc * 1024 + nh * 512: kc * 1024 + (nh + 1) * 512],
                                          start=(kc == 0), stop=(kc == 7))
                    inc("pe", mm)
                blk.tensor(c_fk_pe)
                PE_FK = cnt["pe"]

                def c_yb_act(e, t=t, th=th, PE_FK=PE_FK):
                    w(e, "pe", PE_FK)
                    e.activation(YBF[:, 0:512], pC0[:, :], AF.Copy, scale=SPROD[:, 0:1])
                    inc("act", e.activation(YBF[:, 512:1024], pC1[:, :], AF.Copy, scale=SPROD[:, 0:1]))
                act_body(c_yb_act)
                ACT_YB = cnt["act"]

                def c_out_dve(e, t=t, th=th, ACT_YB=ACT_YB):
                    w(e, "act", ACT_YB)
                    w(e, "dout")
                    inc("dve", e.tensor_tensor(out=YBF2[:, :], in0=YBF[:, :], in1=BIASB[:, :], op=ALU.add))
                    w(e, "dve")
                    inc("dve", e.tensor_tensor(out=OUTB[:, th * 1024:(th + 1) * 1024],
                                               in0=X1F[:, th * 1024:(th + 1) * 1024],
                                               in1=YBF2[:, :], op=ALU.add))
                vec_body(c_out_dve)
                DVE_OUT = cnt["dve"]

                def c_store(e, t=t, th=th, DVE_OUT=DVE_OUT):
                    w(e, "dve", DVE_OUT)
                    inc("dout", e.dma_start(out=out[t * 256 + th * 128: t * 256 + th * 128 + 128, :],
                                            in_=OUTB[:, th * 1024:(th + 1) * 1024]), 16)
                    w(e, "dout")
                blk.gpsimd(c_store)

        def fin(e):
            w(e, "dout")
        blk.gpsimd(fin)
        stC.close()
    return nc


def _consts():
    identf = np.eye(128, dtype=np.float32)
    identb = identf.astype(ml_dtypes.bfloat16)
    mrow = np.arange(128)[:, None]
    ccol = np.arange(128)[None, :]
    valid = (mrow // 64 == ccol // 64) & (mrow <= ccol)
    maskneg = np.where(valid, 0.0, NEG).astype(np.float32)
    sel16f = np.zeros((128, 128), np.float32)
    sel1b_ = np.zeros((128, 128), np.float32)
    for fc in range(8):
        sel16f[0:64, fc * 16 + 2 * fc] = 1.0 / 64
        sel16f[64:128, fc * 16 + 2 * fc + 1] = 1.0 / 64
        sel1b_[0:64, fc * 16 + 2 * fc] = 1.0
        sel1b_[64:128, fc * 16 + 2 * fc + 1] = 1.0
    sel1b = sel1b_.astype(ml_dtypes.bfloat16)
    bsel = np.zeros((16, 16 * 128), np.float32)
    for h in range(16):
        bsel[h, h * 128:(h + 1) * 128] = 1.0
    onesf = np.ones((1, 128), np.float32)
    onesb = np.ones((1, 128), ml_dtypes.bfloat16)
    return dict(identf=identf, identb=identb, maskneg=maskneg, sel16f=sel16f,
                sel1b=sel1b, bsel=bsel, onesf=onesf, onesb=onesb)


_NC_CACHE = {}


def kernel(L0_attn_x, L0_attn_Wq, L0_attn_Wk, L0_attn_Wv, L0_attn_Wg, L0_attn_bg,
           L0_attn_Wo, L0_attn_ln1_w, L0_attn_ln1_b, L0_attn_ln2_w, L0_attn_ln2_b,
           L0_attn_W1, L0_attn_b1, L0_attn_W2, L0_attn_b2,
           L0_ffn_kernel, L0_ffn_input_scale, L0_ffn_kernel_scale, L0_ffn_bias):
    if "nc" not in _NC_CACHE:
        _NC_CACHE["nc"] = build_nc()
    nc = _NC_CACHE["nc"]
    consts = _consts()
    x = np.ascontiguousarray(np.asarray(L0_attn_x).reshape(16384, 1024))
    shared = dict(
        Wq=np.ascontiguousarray(L0_attn_Wq), Wk=np.ascontiguousarray(L0_attn_Wk),
        Wv=np.ascontiguousarray(L0_attn_Wv), Wg=np.ascontiguousarray(L0_attn_Wg),
        bg=np.ascontiguousarray(L0_attn_bg), Wo=np.ascontiguousarray(L0_attn_Wo),
        ln1w=np.ascontiguousarray(L0_attn_ln1_w), ln1b=np.ascontiguousarray(L0_attn_ln1_b),
        ln2w=np.ascontiguousarray(L0_attn_ln2_w), ln2b=np.ascontiguousarray(L0_attn_ln2_b),
        W1=np.ascontiguousarray(L0_attn_W1), b1=np.ascontiguousarray(L0_attn_b1),
        W2=np.ascontiguousarray(L0_attn_W2), b2=np.ascontiguousarray(L0_attn_b2),
        fk=np.ascontiguousarray(L0_ffn_kernel),
        fsin=np.ascontiguousarray(L0_ffn_input_scale),
        fsk=np.ascontiguousarray(L0_ffn_kernel_scale),
        fbias=np.ascontiguousarray(L0_ffn_bias),
        **consts,
    )
    in_maps = []
    for c in range(NCORE):
        m1c = np.zeros((128, 8), np.float32)
        m1c[:, :c] = 1.0
        in_maps.append(dict(shared, x=np.ascontiguousarray(x[c * T:(c + 1) * T]), m1=m1c))
    res = run_bass_kernel_spmd(nc, in_maps, core_ids=list(range(NCORE)))
    full = np.concatenate([res.results[c]["out"] for c in range(NCORE)], axis=0)
    return full.reshape(1, 16384, 1024)


# version 08
# speedup vs baseline: 1.0480x; 1.0480x over previous
import numpy as np
import ml_dtypes
import concourse.bass as bass
import concourse.mybir as mybir
from concourse.bass_utils import run_bass_kernel_spmd
from contextlib import ExitStack

F32 = mybir.dt.float32
BF16 = mybir.dt.bfloat16
FP8 = mybir.dt.float8e4
AF = mybir.ActivationFunctionType
ALU = mybir.AluOpType
AX = mybir.AxisListType

NCORE = 8
T = 2048
D = 1024
FFN = 4096
NT = 8
NS = 16
EPS = 1e-5
NEG = -1e30
USE_CC = True
DO_B = True
DO_C = True
CORR_MODE = 0
DO_X = True


def build_nc():
    nc = bass.Bass("TRN2", target_bir_lowering=False, num_devices=NCORE)
    g = lambda n, s, d, k="ExternalInput": nc.dram_tensor(n, s, d, kind=k).ap()
    x_in = g("x", [T, D], F32)
    Wq = g("Wq", [D, D], F32); Wk = g("Wk", [D, D], F32)
    Wv = g("Wv", [D, D], F32); Wg = g("Wg", [D, D], F32)
    bgv = g("bg", [D], F32); Wo = g("Wo", [D, D], F32)
    ln1w = g("ln1w", [D], F32); ln1b = g("ln1b", [D], F32)
    ln2w = g("ln2w", [D], F32); ln2b = g("ln2b", [D], F32)
    W1 = g("W1", [D, FFN], F32); b1 = g("b1", [FFN], F32)
    W2 = g("W2", [FFN, D], F32); b2 = g("b2", [D], F32)
    fk = g("fk", [D, D], BF16)
    fsin = g("fsin", [1], F32); fsk = g("fsk", [1], F32)
    fbias = g("fbias", [D], BF16)
    identf = g("identf", [128, 128], F32); identb = g("identb", [128, 128], BF16)
    maskneg = g("maskneg", [128, 128], F32)
    sel16f = g("sel16f", [128, 128], F32); sel1b = g("sel1b", [128, 128], BF16)
    bsel = g("bsel", [16, 16 * 128], F32)
    onesf = g("onesf", [1, 128], F32); onesb = g("onesb", [1, 128], BF16)
    m1 = g("m1", [128, 8], F32)
    out = g("out", [T, D], F32, k="ExternalOutput")
    cc_src = g("cc_src", [128, 520], F32, k="Internal")
    cc_dst = g("cc_dst", [NCORE * 128, 520], F32, k="Internal")
    y2d = g("y2d", [T, D], BF16, k="Internal")
    qdd = g("qdd", [8, 128, T], BF16, k="Internal")

    st = ExitStack()
    E = st.enter_context
    with st:
        IDF = E(nc.sbuf_tensor("IDF", [128, 128], F32))
        IDB = E(nc.sbuf_tensor("IDB", [128, 128], BF16))
        ONESF = E(nc.sbuf_tensor("ONESF", [1, 128], F32))
        ONESB = E(nc.sbuf_tensor("ONESB", [1, 128], BF16))
        M1S = E(nc.sbuf_tensor("M1S", [128, 8], F32))
        PRM = E(nc.sbuf_tensor("PRM", [128, 48], F32))
        B1C = E(nc.sbuf_tensor("B1C", [128, 32], F32))
        PRMROW = E(nc.sbuf_tensor("PRMROW", [32, 128], F32))
        B2ROWB = E(nc.sbuf_tensor("B2ROWB", [1, D], BF16))
        FBROW = E(nc.sbuf_tensor("FBROW", [1, D], BF16))
        BIASB = E(nc.sbuf_tensor("BIASB", [128, D], BF16))
        SCAL = E(nc.sbuf_tensor("SCAL", [1, 8], F32))
        ZROW = E(nc.sbuf_tensor("ZROW", [1, 512], BF16))
        EPSC = E(nc.sbuf_tensor("EPSC", [128, 1], F32))
        RSIN = E(nc.sbuf_tensor("RSIN", [128, 1], F32))
        RSK = E(nc.sbuf_tensor("RSK", [128, 1], F32))
        SPROD = E(nc.sbuf_tensor("SPROD", [128, 1], F32))
        pA0 = E(nc.psum_tensor("pA0", [128, 256], F32))
        pA1 = E(nc.psum_tensor("pA1", [128, 256], F32))
        pB0 = E(nc.psum_tensor("pB0", [128, 256], F32))
        pB1 = E(nc.psum_tensor("pB1", [128, 256], F32))
        pC0 = E(nc.psum_tensor("pC0", [128, 512], F32))
        pC1 = E(nc.psum_tensor("pC1", [128, 512], F32))
        pAs = [pA0, pA1]
        ptrf = pA1
        ptrbs = None
        pBs = [pB0, pB1]
        pCs = [pC0, pC1]
        pD = E(nc.psum_tensor("pD", [128, 512], F32))
        pD2 = E(nc.psum_tensor("pD2", [128, 512], F32))
        ptrbs = [pA0.bitcast(BF16)[:, 0:128], pA1.bitcast(BF16)[:, 0:128],
                 pD.bitcast(BF16)[:, 0:128], pD2.bitcast(BF16)[:, 0:128]]
        sems = {}
        for name in ["pe", "dve", "act", "din", "dout", "cc", "dw", "dw2"]:
            sems[name] = E(nc.semaphore("sm_" + name))
        cnt = {k: 0 for k in sems}
        blk = E(nc.Block())

        def vec_body(f):
            def g(e):
                w(e, "dve")
                f(e)
            blk.vector(g)

        def act_body(f):
            def g(e):
                w(e, "act")
                f(e)
            blk.scalar(g)

        def inc(name, inst, n=1):
            inst.then_inc(sems[name], n)
            cnt[name] += n
            return cnt[name]

        def w(e, name, v=None):
            e.wait_ge(sems[name], cnt[name] if v is None else v)

        def ph0_sync(e):
            for ap, sb in [(identf, IDF), (identb, IDB), (onesf, ONESF), (onesb, ONESB), (m1, M1S)]:
                inc("din", e.dma_start(out=sb[:, :], in_=ap), 16)
            inc("din", e.dma_start(out=FBROW[:, :], in_=fbias.rearrange("(o d) -> o d", o=1)), 16)
            inc("din", e.dma_start(out=SCAL[0:1, 0:1], in_=fsin.rearrange("(o d) -> o d", o=1)), 16)
            inc("din", e.dma_start(out=SCAL[0:1, 1:2], in_=fsk.rearrange("(o d) -> o d", o=1)), 16)
            w(e, "din")
        blk.sync(ph0_sync)

        for i, ap in enumerate([bgv, ln1w, ln1b, ln2w, ln2b]):
            PE_PV = cnt["pe"]

            def pv_sync(e, ap=ap, i=i, PE_PV=PE_PV):
                w(e, "pe", PE_PV)
                inc("din", e.dma_start(out=PRMROW[0:8, 0:128].rearrange("p (o a) -> p o a", o=1),
                                       in_=ap.rearrange("(a o p) -> a o p", o=1, p=128)), 16)
                w(e, "din")
            blk.sync(pv_sync)
            DVE_PV = cnt["dve"]

            def pv_pe(e, i=i, DVE_PV=DVE_PV):
                w(e, "din")
                w(e, "dve", DVE_PV)
                inc("pe", e.transpose(ptrf[:, 0:8], PRMROW[0:8, 0:128], IDF[0:8, 0:8]))
            blk.tensor(pv_pe)

            def pv_dve(e, i=i):
                w(e, "pe")
                inc("dve", e.tensor_copy(out=PRM[:, i * 8:(i + 1) * 8], in_=ptrf[:, 0:8]))
            vec_body(pv_dve)
        PE_PARAMS = cnt["pe"]

        def b1_sync(e):
            w(e, "pe", PE_PARAMS)
            inc("din", e.dma_start(out=PRMROW[0:32, 0:128].rearrange("p (o a) -> p o a", o=1),
                                   in_=b1.rearrange("(a o p) -> a o p", o=1, p=128)), 16)
            w(e, "din")
        blk.sync(b1_sync)

        def b1_pe(e):
            w(e, "din")
            w(e, "dve")
            inc("pe", e.transpose(ptrf[:, 0:32], PRMROW[0:32, 0:128], IDF[0:32, 0:32]))
        blk.tensor(b1_pe)
        PE_B1 = cnt["pe"]

        def b1_dve(e):
            w(e, "pe", PE_B1)
            inc("dve", e.tensor_copy(out=B1C[:, :], in_=ptrf[:, 0:32]))
        vec_body(b1_dve)

        def sc_dve(e):
            w(e, "din")
            e.memset(EPSC[:, :], EPS)
            e.memset(ZROW[:, :], 0.0)
            e.reciprocal(out=SCAL[0:1, 2:3], in_=SCAL[0:1, 0:1])
            e.reciprocal(out=SCAL[0:1, 3:4], in_=SCAL[0:1, 1:2])
            inc("dve", e.tensor_tensor(out=SCAL[0:1, 4:5], in0=SCAL[0:1, 0:1],
                                       in1=SCAL[0:1, 1:2], op=ALU.mult))
        vec_body(sc_dve)

        def sc_pe(e):
            w(e, "dve")
            e.matmul(pA0[:, 0:1], ONESF[:, :], SCAL[0:1, 2:3], start=True, stop=False)
            e.matmul(pA0[:, 1:2], ONESF[:, :], SCAL[0:1, 3:4], start=False, stop=False)
            e.matmul(pA0[:, 2:3], ONESF[:, :], SCAL[0:1, 4:5], start=False, stop=True)
            e.matmul(pC0[:, :], ONESB[:, :], FBROW[:, 0:512], start=True, stop=True)
            inc("pe", e.matmul(pC1[:, :], ONESB[:, :], FBROW[:, 512:1024], start=True, stop=True))
        blk.tensor(sc_pe)

        def sc_dve2(e):
            w(e, "pe")
            e.tensor_copy(out=RSIN[:, :], in_=pA0[:, 0:1])
            e.tensor_copy(out=RSK[:, :], in_=pA0[:, 1:2])
            e.tensor_copy(out=SPROD[:, :], in_=pA0[:, 2:3])
            e.tensor_copy(out=BIASB[:, 0:512], in_=pC0[:, :])
            inc("dve", e.tensor_copy(out=BIASB[:, 512:1024], in_=pC1[:, :]))
        vec_body(sc_dve2)


        stA = ExitStack()
        EA = stA.enter_context
        Y2 = EA(nc.sbuf_tensor("Y2", [128, NS * 1024], BF16))
        QDECT = EA(nc.sbuf_tensor("QDECT", [128, 2048], BF16))
        RNQ = EA(nc.sbuf_tensor("RNQ", [128, NS * 16], F32))
        LAL = EA(nc.sbuf_tensor("LAL", [128, 8 * 32], F32))
        CDA = EA(nc.sbuf_tensor("CDA", [128, 8 * 32], F32))
        S = EA(nc.sbuf_tensor("S", [128, 8 * 64], F32))
        Sb = EA(nc.sbuf_tensor("Sb", [128, 8 * 64], BF16))
        SbO = EA(nc.sbuf_tensor("SbO", [128, 8 * 64], BF16))
        SbM = EA(nc.sbuf_tensor("SbM", [128, 8 * 64], BF16))
        ABUF = EA(nc.sbuf_tensor("ABUF", [128, 8], F32))
        SGX = EA(nc.sbuf_tensor("SGX", [128, 2 * 520], F32))
        PVEC = EA(nc.sbuf_tensor("PVEC", [128, 8], F32))
        PM = EA(nc.sbuf_tensor("PM", [128, 8], F32))
        MSK = EA(nc.sbuf_tensor("MSK", [128, 128], F32))
        SELF = EA(nc.sbuf_tensor("SELF", [128, 128], F32))
        SELB = EA(nc.sbuf_tensor("SELB", [128, 128], BF16))
        BSEL = EA(nc.sbuf_tensor("BSEL", [16, 16 * 128], F32))
        SMALL = EA(nc.sbuf_tensor("SMALL", [16, 2048], F32))
        SCT = EA(nc.sbuf_tensor("SCT", [128, 32], F32))
        RKT = EA(nc.sbuf_tensor("RKT", [128, 32], F32))
        WQ = EA(nc.sbuf_tensor("WQs", [128, 8 * 1024], BF16))
        WK = EA(nc.sbuf_tensor("WKs", [128, 8 * 1024], BF16))
        WV = EA(nc.sbuf_tensor("WVs", [128, 8 * 1024], BF16))
        WGs = EA(nc.sbuf_tensor("WGs", [128, 8 * 1024], BF16))
        WOs = EA(nc.sbuf_tensor("WOs", [128, 8 * 1024], BF16))
        XT = EA(nc.sbuf_tensor("XT", [128, 1024], F32))
        SCR = EA(nc.sbuf_tensor("SCR", [128, 1024], F32))
        YTI = EA(nc.sbuf_tensor("YTI", [128, 2048], BF16))
        XN = EA(nc.sbuf_tensor("XN", [128, 2048], BF16))
        XNT = EA(nc.sbuf_tensor("XNT", [128, 2048], BF16))
        QF = EA(nc.sbuf_tensor("QF", [128, 2048], BF16))
        KF = EA(nc.sbuf_tensor("KF", [128, 2048], BF16))
        VT = EA(nc.sbuf_tensor("VT", [128, 2048], BF16))
        ALPHA = EA(nc.sbuf_tensor("ALPHA", [128, 2048], F32))
        LACUM = EA(nc.sbuf_tensor("LACUM", [128, 512], F32))
        EXPB = EA(nc.sbuf_tensor("EXPB", [128, 512], BF16))
        SQB = EA(nc.sbuf_tensor("SQB", [128, 512], BF16))
        ATTB = EA(nc.sbuf_tensor("ATTB", [128, 2048], BF16))
        OTOK = EA(nc.sbuf_tensor("OTOK", [128, 1024], BF16))
        OT = EA(nc.sbuf_tensor("OT", [128, 1024], BF16))
        STATS = EA(nc.sbuf_tensor("STATS", [128, 32], F32))

        def a_init_dve(e):
            e.memset(Y2[:, :], 0.0)
            e.memset(LAL[:, :], 0.0)
            e.memset(S[:, :], 0.0)
            e.memset(Sb[:, :], 0.0)
            inc("dve", e.memset(PVEC[:, :], 1.0))
        vec_body(a_init_dve)

        DIN_WBASE = cnt["din"]
        ACT_WBASE = cnt["act"]
        DW0 = cnt["dw"]
        DW20 = cnt["dw2"]
        DVE_INIT = cnt["dve"]
        Y2F = Y2.bitcast(F32)
        WSLOT = [Y2F[:, 2048 + k * 1024: 3072 + k * 1024] for k in range(6)]

        def wload(e):
            for ap, sb in [(maskneg, MSK), (sel16f, SELF), (sel1b, SELB), (bsel, BSEL)]:
                inc("din", e.dma_start(out=sb[:, :], in_=ap), 16)
            w(e, "din")
            e.wait_ge(sems["dve"], DVE_INIT)
            inc("din", e.dma_start(out=WSLOT[0][0:1, 0:1024], in_=b2.rearrange("(o d) -> o d", o=1)), 16)
            w(e, "din")
            i = 0
            for ap in [Wq, Wk, Wg, Wv, Wo]:
                for kc in range(8):
                    if i % 2 == 0:
                        e.wait_ge(sems["act"], ACT_WBASE + 1 + max(0, i - 5))
                        inc("dw", e.dma_start(out=WSLOT[i % 6][:, 0:1024],
                                              in_=ap[kc * 128:(kc + 1) * 128, :]), 16)
                    i += 1
        blk.sync(wload)

        def wload2(e):
            e.wait_ge(sems["dve"], DVE_INIT)
            e.wait_ge(sems["din"], DIN_WBASE + 5 * 16)
            i = 0
            for ap in [Wq, Wk, Wg, Wv, Wo]:
                for kc in range(8):
                    if i % 2 == 1:
                        e.wait_ge(sems["act"], ACT_WBASE + 1 + max(0, i - 5))
                        inc("dw2", e.dma_start(out=WSLOT[i % 6][:, 0:1024],
                                               in_=ap[kc * 128:(kc + 1) * 128, :]), 16)
                    i += 1
        blk.gpsimd(wload2)

        def wcast(e):
            e.wait_ge(sems["din"], DIN_WBASE + 5 * 16)
            inc("act", e.activation(B2ROWB[:, :], WSLOT[0][0:1, 0:1024], AF.Copy))
            i = 0
            for sb in [WQ, WK, WGs, WV, WOs]:
                for kc in range(8):
                    j = i // 2
                    if i % 2 == 0:
                        e.wait_ge(sems["dw"], DW0 + min(j + 3, 20) * 16)
                    else:
                        e.wait_ge(sems["dw2"], DW20 + min(j + 3, 20) * 16)
                    inc("act", e.activation(sb[:, kc * 1024:(kc + 1) * 1024],
                                            WSLOT[i % 6][:, 0:1024], AF.Copy))
                    i += 1
        act_body(wcast)
        ACT_WEND = cnt["act"]

        PREV_SQ = [0]
        for t in range(NT):
            for sh in range(2):
                DVE_PREV = cnt["dve"]

                def a_x(e, t=t, sh=sh, DVE_PREV=DVE_PREV, psq=PREV_SQ[0]):
                    w(e, "dve", DVE_PREV)
                    if psq:
                        e.wait_ge(sems["act"], psq)
                    inc("din", e.dma_start(out=XT[:, 0:1024],
                                           in_=x_in[t * 256 + sh * 128: t * 256 + sh * 128 + 128, :]), 16)
                    w(e, "din")
                blk.sync(a_x)

                def a_sq(e, t=t, sh=sh):
                    w(e, "din")
                    c0 = sh * 16
                    inc("act", e.activation(SCR[:, 0:1024], XT[:, 0:1024], AF.Square,
                                            accum_out=STATS[:, c0 + 1:c0 + 2]))
                act_body(a_sq)
                ACT_SQ0 = cnt["act"]
                PREV_SQ[0] = ACT_SQ0

                def a_ln0(e, t=t, sh=sh, ACT_SQ0=ACT_SQ0):
                    w(e, "din")
                    c0 = sh * 16
                    inc("dve", e.reduce_sum(out=STATS[:, c0:c0 + 1], in_=XT[:, 0:1024], axis=AX.X))
                    w(e, "dve")
                    inc("dve", e.tensor_scalar(out=STATS[:, c0 + 2:c0 + 3], in0=STATS[:, c0:c0 + 1],
                                               scalar1=1.0 / D, scalar2=None, op0=ALU.mult))
                    w(e, "dve")
                    inc("dve", e.tensor_tensor(out=STATS[:, c0 + 3:c0 + 4], in0=STATS[:, c0 + 2:c0 + 3],
                                               in1=STATS[:, c0 + 2:c0 + 3], op=ALU.mult))
                    e.wait_ge(sems["act"], ACT_SQ0)
                    w(e, "dve")
                    inc("dve", e.scalar_tensor_tensor(out=STATS[:, c0 + 4:c0 + 5],
                                                      in0=STATS[:, c0 + 1:c0 + 2],
                                                      scalar=1.0 / D, in1=STATS[:, c0 + 3:c0 + 4],
                                                      op0=ALU.mult, op1=ALU.subtract))
                vec_body(a_ln0)
                DVE_VAR0 = cnt["dve"]

                def a_sd(e, t=t, sh=sh, DVE_VAR0=DVE_VAR0):
                    w(e, "dve", DVE_VAR0)
                    c0 = sh * 16
                    inc("act", e.activation(STATS[:, c0 + 5:c0 + 6], STATS[:, c0 + 4:c0 + 5],
                                            AF.Sqrt, bias=EPSC[:, 0:1]))
                act_body(a_sd)
                ACT_SD0 = cnt["act"]

                def a_y(e, t=t, sh=sh, ACT_SD0=ACT_SD0):
                    w(e, "act", ACT_SD0)
                    c0 = sh * 16
                    inc("dve", e.reciprocal(out=STATS[:, c0 + 6:c0 + 7], in_=STATS[:, c0 + 5:c0 + 6]))
                    w(e, "dve")
                    inc("dve", e.tensor_scalar(out=YTI[:, sh * 1024:(sh + 1) * 1024],
                                               in0=XT[:, 0:1024],
                                               scalar1=STATS[:, c0 + 2:c0 + 3],
                                               scalar2=STATS[:, c0 + 6:c0 + 7],
                                               op0=ALU.subtract, op1=ALU.mult))
                vec_body(a_y)
                DVE_Y = cnt["dve"]

                def a_sq1(e, t=t, sh=sh, DVE_Y=DVE_Y):
                    w(e, "dve", DVE_Y)
                    c0 = sh * 16 + 8
                    inc("act", e.activation(SCR[:, 0:1024], YTI[:, sh * 1024:(sh + 1) * 1024],
                                            AF.Square, accum_out=STATS[:, c0 + 1:c0 + 2]))
                act_body(a_sq1)
                ACT_SQ1 = cnt["act"]

                def a_ln1(e, t=t, sh=sh, DVE_Y=DVE_Y, ACT_SQ1=ACT_SQ1):
                    w(e, "dve", DVE_Y)
                    c0 = sh * 16 + 8
                    inc("dve", e.reduce_sum(out=STATS[:, c0:c0 + 1],
                                            in_=YTI[:, sh * 1024:(sh + 1) * 1024], axis=AX.X))
                    w(e, "dve")
                    inc("dve", e.tensor_scalar(out=STATS[:, c0 + 2:c0 + 3], in0=STATS[:, c0:c0 + 1],
                                               scalar1=1.0 / D, scalar2=None, op0=ALU.mult))
                    w(e, "dve")
                    inc("dve", e.tensor_tensor(out=STATS[:, c0 + 3:c0 + 4], in0=STATS[:, c0 + 2:c0 + 3],
                                               in1=STATS[:, c0 + 2:c0 + 3], op=ALU.mult))
                    e.wait_ge(sems["act"], ACT_SQ1)
                    w(e, "dve")
                    inc("dve", e.scalar_tensor_tensor(out=STATS[:, c0 + 4:c0 + 5],
                                                      in0=STATS[:, c0 + 1:c0 + 2],
                                                      scalar=1.0 / D, in1=STATS[:, c0 + 3:c0 + 4],
                                                      op0=ALU.mult, op1=ALU.subtract))
                vec_body(a_ln1)
                DVE_VAR1 = cnt["dve"]

                def a_sd1(e, t=t, sh=sh, DVE_VAR1=DVE_VAR1):
                    w(e, "dve", DVE_VAR1)
                    c0 = sh * 16 + 8
                    inc("act", e.activation(STATS[:, c0 + 5:c0 + 6], STATS[:, c0 + 4:c0 + 5],
                                            AF.Sqrt, bias=EPSC[:, 0:1]))
                act_body(a_sd1)
                ACT_SD1 = cnt["act"]

                def a_xn(e, t=t, sh=sh, ACT_SD1=ACT_SD1):
                    w(e, "act", ACT_SD1)
                    c0 = sh * 16 + 8
                    inc("dve", e.reciprocal(out=STATS[:, c0 + 6:c0 + 7], in_=STATS[:, c0 + 5:c0 + 6]))
                    w(e, "dve")
                    inc("dve", e.tensor_scalar(out=XN[:, sh * 1024:(sh + 1) * 1024],
                                               in0=YTI[:, sh * 1024:(sh + 1) * 1024],
                                               scalar1=STATS[:, c0 + 2:c0 + 3],
                                               scalar2=STATS[:, c0 + 6:c0 + 7],
                                               op0=ALU.subtract, op1=ALU.mult))
                vec_body(a_xn)
            DVE_XN = cnt["dve"]

            PE_T0 = cnt["pe"]
            DVE_T0 = cnt["dve"]

            def a_xnt_pe(e, t=t):
                w(e, "dve", DVE_XN)
                for i in range(16):
                    sh, fc = i // 8, i % 8
                    if i >= 4:
                        e.wait_ge(sems["dve"], DVE_T0 + i - 3)
                    inc("pe", e.transpose(ptrbs[i % 4],
                                          XN[:, sh * 1024 + fc * 128: sh * 1024 + fc * 128 + 128], IDB[:, :]))
            blk.tensor(a_xnt_pe)

            def a_xnt_dve(e, t=t):
                for i in range(16):
                    sh, fc = i // 8, i % 8
                    e.wait_ge(sems["pe"], PE_T0 + i + 1)
                    inc("dve", e.tensor_scalar(out=XNT[:, fc * 256 + sh * 128: fc * 256 + sh * 128 + 128],
                                               in0=ptrbs[i % 4],
                                               scalar1=PRM[:, 8 + fc:9 + fc], scalar2=PRM[:, 16 + fc:17 + fc],
                                               op0=ALU.mult, op1=ALU.add))
            vec_body(a_xnt_dve)
            DVE_XNT = cnt["dve"]

            PE_PROJ0 = cnt["pe"]
            ACT_PQ0 = cnt["act"]
            DVE_PK0 = cnt["dve"]

            def a_proj_pe(e, t=t):
                w(e, "dve", DVE_XNT)
                if t == 0:
                    e.wait_ge(sems["act"], ACT_WBASE + 9)
                for oc in range(8):
                    if oc >= 2:
                        e.wait_ge(sems["act"], ACT_PQ0 + oc - 1)
                    for kc in range(8):
                        mm = e.matmul(pAs[oc % 2][:, :],
                                      WQ[:, kc * 1024 + oc * 128: kc * 1024 + (oc + 1) * 128],
                                      XNT[:, kc * 256:(kc + 1) * 256], start=(kc == 0), stop=(kc == 7))
                    inc("pe", mm)
                if t == 0:
                    e.wait_ge(sems["act"], ACT_WBASE + 17)
                for oc in range(8):
                    if oc >= 2:
                        e.wait_ge(sems["dve"], DVE_PK0 + oc - 1)
                    for kc in range(8):
                        mm = e.matmul(pBs[oc % 2][:, :],
                                      WK[:, kc * 1024 + oc * 128: kc * 1024 + (oc + 1) * 128],
                                      XNT[:, kc * 256:(kc + 1) * 256], start=(kc == 0), stop=(kc == 7))
                    inc("pe", mm)
                if t == 0:
                    e.wait_ge(sems["act"], ACT_WBASE + 25)
                for oc in range(8):
                    e.wait_ge(sems["act"], ACT_PQ0 + 7 + oc)
                    for kc in range(8):
                        mm = e.matmul(pAs[oc % 2][:, :],
                                      WGs[:, kc * 1024 + oc * 128: kc * 1024 + (oc + 1) * 128],
                                      XNT[:, kc * 256:(kc + 1) * 256], start=(kc == 0), stop=(kc == 7))
                    inc("pe", mm)
                if t == 0:
                    e.wait_ge(sems["act"], ACT_WBASE + 33)
                for sh in range(2):
                    if sh == 1:
                        e.wait_ge(sems["dve"], DVE_PK0 + 9)
                    for nh in range(2):
                        for kc in range(8):
                            mm = e.matmul(pCs[nh][:, :],
                                          XNT[:, kc * 256 + sh * 128: kc * 256 + sh * 128 + 128],
                                          WV[:, kc * 1024 + nh * 512: kc * 1024 + (nh + 1) * 512],
                                          start=(kc == 0), stop=(kc == 7))
                    inc("pe", mm)
            blk.tensor(a_proj_pe)

            def a_projq_act(e, t=t):
                for oc in range(8):
                    e.wait_ge(sems["pe"], PE_PROJ0 + oc + 1)
                    inc("act", e.activation(QF[:, oc * 256:(oc + 1) * 256], pAs[oc % 2][:, :], AF.Copy))
                for oc in range(8):
                    e.wait_ge(sems["pe"], PE_PROJ0 + 16 + oc + 1)
                    inc("act", e.activation(ALPHA[:, oc * 256:(oc + 1) * 256], pAs[oc % 2][:, :],
                                            AF.Sigmoid, bias=PRM[:, 0 + oc:1 + oc]))
            act_body(a_projq_act)
            ACT_G = cnt["act"]

            def a_projk_dve(e, t=t):
                for oc in range(8):
                    e.wait_ge(sems["pe"], PE_PROJ0 + 8 + oc + 1)
                    inc("dve", e.tensor_copy(out=KF[:, oc * 256:(oc + 1) * 256], in_=pBs[oc % 2][:, :]))
                for sh in range(2):
                    e.wait_ge(sems["pe"], PE_PROJ0 + 24 + sh + 1)
                    e.tensor_copy(out=VT[:, sh * 1024: sh * 1024 + 512], in_=pC0[:, :])
                    inc("dve", e.tensor_copy(out=VT[:, sh * 1024 + 512:(sh + 1) * 1024], in_=pC1[:, :]))
            vec_body(a_projk_dve)
            DVE_V = cnt["dve"]

            DVE_CL0 = cnt["dve"]
            ACT_LN0 = cnt["act"]

            def a_clamp_dve(e, t=t, ACT_LN0=ACT_LN0):
                w(e, "act", ACT_G)
                for oc in range(8):
                    if oc >= 2:
                        e.wait_ge(sems["act"], ACT_LN0 + oc - 1)
                    inc("dve", e.tensor_scalar(out=SCR[:, (oc % 2) * 256:(oc % 2) * 256 + 256],
                                               in0=ALPHA[:, oc * 256:(oc + 1) * 256],
                                               scalar1=1e-6, scalar2=None, op0=ALU.max))
            vec_body(a_clamp_dve)

            def a_ln_act(e, t=t, DVE_CL0=DVE_CL0):
                for oc in range(8):
                    e.wait_ge(sems["dve"], DVE_CL0 + oc + 1)
                    inc("act", e.activation(ALPHA[:, oc * 256:(oc + 1) * 256],
                                            SCR[:, (oc % 2) * 256:(oc % 2) * 256 + 256], AF.Ln))
            act_body(a_ln_act)
            ACT_LA = cnt["act"]
            LA = ALPHA

            dseq = []
            for fc in range(8):
                dseq.append(("scan", fc))
                if fc >= 1:
                    dseq.append(("qdec", fc - 1))
                    dseq.append(("dte", fc - 1))
                if fc >= 2:
                    dseq.append(("kdt", fc - 2))
            dseq += [("qdec", 7), ("dte", 7), ("kdt", 6), ("kdt", 7)]
            aseq = [("eL", 0), ("eL", 1)]
            for fc in range(8):
                if fc >= 2:
                    aseq.append(("eL", fc))
                aseq.append(("eD", fc))
            DVE_SC0 = cnt["dve"]
            ACT_SC0 = cnt["act"]
            dcnt = {}
            acnt = {}
            _c = DVE_SC0
            for it in dseq:
                _c += 2 if it[0] == "scan" else 1
                dcnt[it] = _c
            for j, it in enumerate(aseq):
                acnt[it] = ACT_SC0 + j + 1
            ACT_LA_t = ACT_LA

            def a_scanpipe_dve(e, t=t, dcnt=dcnt, acnt=acnt, ACT_LA_t=ACT_LA_t):
                w(e, "act", ACT_LA_t)
                w(e, "dout")
                for op, fc in dseq:
                    sl = (fc % 2) * 256
                    if op == "scan":
                        if fc >= 2:
                            w(e, "dve", dcnt[("dte", fc - 2)])
                        last = None
                        for c in range(4):
                            last = e.tensor_tensor_scan(out=LACUM[:, sl + c * 64: sl + (c + 1) * 64],
                                                        data0=LA[:, fc * 256 + c * 64: fc * 256 + (c + 1) * 64],
                                                        data1=LA[:, fc * 256 + c * 64: fc * 256 + (c + 1) * 64],
                                                        initial=0.0, op0=ALU.add, op1=ALU.bypass)
                        inc("dve", last)
                        w(e, "dve")
                        inc("dve", e.tensor_copy(
                            out=LAL[:, fc * 32 + t * 4: fc * 32 + t * 4 + 4],
                            in_=LACUM[:, sl:sl + 256].rearrange("p (c z) -> p c z", z=64)[:, :, 63:64]))
                    elif op == "qdec":
                        e.wait_ge(sems["act"], acnt[("eL", fc)])
                        inc("dve", e.tensor_tensor(out=QDECT[:, fc * 256:(fc + 1) * 256],
                                                   in0=QF[:, fc * 256:(fc + 1) * 256],
                                                   in1=EXPB[:, sl:sl + 256], op=ALU.mult))
                    elif op == "dte":
                        if fc >= 2:
                            e.wait_ge(sems["act"], acnt[("eD", fc - 2)])
                        last = None
                        for c in range(4):
                            last = e.tensor_scalar(out=SCR[:, sl + c * 64: sl + (c + 1) * 64],
                                                   in0=LACUM[:, sl + c * 64: sl + (c + 1) * 64],
                                                   scalar1=LAL[:, fc * 32 + t * 4 + c: fc * 32 + t * 4 + c + 1],
                                                   scalar2=None, op0=ALU.subtract)
                        inc("dve", last)
                    elif op == "kdt":
                        e.wait_ge(sems["act"], acnt[("eD", fc)])
                        inc("dve", e.tensor_tensor(out=XN[:, fc * 256:(fc + 1) * 256],
                                                   in0=KF[:, fc * 256:(fc + 1) * 256],
                                                   in1=EXPB[:, sl:sl + 256], op=ALU.mult))
            vec_body(a_scanpipe_dve)

            def a_scanpipe_act(e, t=t, dcnt=dcnt):
                for op, fc in aseq:
                    sl = (fc % 2) * 256
                    if op == "eL":
                        e.wait_ge(sems["dve"], dcnt[("scan", fc)])
                        if fc >= 2:
                            e.wait_ge(sems["dve"], dcnt[("kdt", fc - 2)])
                        inc("act", e.activation(EXPB[:, sl:sl + 256], LACUM[:, sl:sl + 256], AF.Exp))
                    else:
                        e.wait_ge(sems["dve"], dcnt[("dte", fc)])
                        inc("act", e.activation(EXPB[:, sl:sl + 256], SCR[:, sl:sl + 256],
                                                AF.Exp, scale=-1.0))
            act_body(a_scanpipe_act)
            DVE_SCAN = dcnt[("scan", 7)]
            DVE_KDT = dcnt[("kdt", 7)]
            KDT = XN
            DVE_QD7 = dcnt[("qdec", 7)]

            def a_qspill(e, t=t, DVE_QD7=DVE_QD7):
                e.wait_ge(sems["dve"], DVE_QD7)
                inc("dout", e.dma_start(out=qdd[:, :, t * 256:(t + 1) * 256].rearrange("f p c -> p f c"),
                                        in_=QDECT[:, :].rearrange("p (f c) -> p f c", c=256)), 16)
                w(e, "dout")
            blk.gpsimd(a_qspill)
            DOUT_QS = cnt["dout"]

            def a_cd_act(e, t=t):
                w(e, "dve", DVE_SCAN)
                inc("act", e.activation(
                    CDA[:, :].rearrange("p (f c) -> p f c", c=32)[:, :, t * 4:(t + 1) * 4],
                    LAL[:, :].rearrange("p (f c) -> p f c", c=32)[:, :, t * 4:(t + 1) * 4], AF.Exp))
            act_body(a_cd_act)

            PE_ML0 = cnt["pe"]
            ACT_NSQ0 = cnt["act"]

            def a_sqn_act(e, t=t, PE_ML0=PE_ML0):
                w(e, "dve", DVE_V)
                for fc in range(8):
                    if fc >= 2:
                        e.wait_ge(sems["pe"], PE_ML0 + 1 + fc - 1)
                    inc("act", e.activation(SQB[:, (fc % 2) * 256:(fc % 2) * 256 + 256],
                                            QF[:, fc * 256:(fc + 1) * 256], AF.Square))
            act_body(a_sqn_act)

            def a_mlog_pe(e, t=t, ACT_NSQ0=ACT_NSQ0):
                w(e, "dve", DVE_SCAN)
                for fc in range(8):
                    mm = e.matmul(pA0[0:16, 0:256], SELF[:, fc * 16:(fc + 1) * 16],
                                  LA[:, fc * 256:(fc + 1) * 256], start=(fc == 0), stop=(fc == 7))
                inc("pe", mm)
                for fc in range(8):
                    e.wait_ge(sems["act"], ACT_NSQ0 + fc + 1)
                    inc("pe", e.matmul(pB0[0:16, 0:256], SELB[:, fc * 16:(fc + 1) * 16],
                                       SQB[:, (fc % 2) * 256:(fc % 2) * 256 + 256],
                                       start=(fc == 0), stop=(fc == 7)))
            blk.tensor(a_mlog_pe)
            PE_ML = cnt["pe"]

            def a_mlog_dve(e, t=t, PE_ML=PE_ML):
                e.wait_ge(sems["pe"], PE_ML - 8)
                inc("dve", e.tensor_copy(out=SMALL[:, 0:256], in_=pA0[0:16, 0:256]))
                w(e, "dve")
                for c in range(4):
                    e.tensor_tensor_scan(out=SMALL[:, 256 + c * 64:256 + (c + 1) * 64],
                                         data0=SMALL[:, c * 64:(c + 1) * 64],
                                         data1=SMALL[:, c * 64:(c + 1) * 64],
                                         initial=0.0, op0=ALU.add, op1=ALU.bypass)
                e.wait_ge(sems["pe"], PE_ML)
                inc("dve", e.tensor_copy(out=SMALL[:, 512:768], in_=pB0[0:16, 0:256]))
            vec_body(a_mlog_dve)
            DVE_QSS = cnt["dve"]
            ACT_KSQ0 = cnt["act"]
            PE_KSS0 = cnt["pe"]

            def a_sqk_act(e, t=t, PE_ML=PE_ML, PE_KSS0=PE_KSS0):
                w(e, "pe", PE_ML)
                for fc in range(8):
                    if fc >= 2:
                        e.wait_ge(sems["pe"], PE_KSS0 + fc - 1)
                    inc("act", e.activation(SQB[:, (fc % 2) * 256:(fc % 2) * 256 + 256],
                                            KF[:, fc * 256:(fc + 1) * 256], AF.Square))
            act_body(a_sqk_act)

            def a_kss_pe(e, t=t, ACT_KSQ0=ACT_KSQ0, DVE_QSS=DVE_QSS):
                w(e, "dve", DVE_QSS)
                for fc in range(8):
                    e.wait_ge(sems["act"], ACT_KSQ0 + fc + 1)
                    inc("pe", e.matmul(pB0[0:16, 0:256], SELB[:, fc * 16:(fc + 1) * 16],
                                       SQB[:, (fc % 2) * 256:(fc % 2) * 256 + 256],
                                       start=(fc == 0), stop=(fc == 7)))
            blk.tensor(a_kss_pe)
            PE_KSS = cnt["pe"]

            def a_kss_dve(e, t=t, PE_KSS=PE_KSS):
                w(e, "pe", PE_KSS)
                inc("dve", e.tensor_copy(out=SMALL[:, 768:1024], in_=pB0[0:16, 0:256]))
            vec_body(a_kss_dve)
            DVE_KSS = cnt["dve"]

            def a_norm_act(e, t=t):
                w(e, "dve", DVE_KSS)
                inc("act", e.activation(SMALL[:, 0:256], SMALL[:, 512:768], AF.Sqrt))
                w(e, "act")
                inc("act", e.activation(SMALL[:, 512:768], SMALL[:, 768:1024], AF.Sqrt))
            act_body(a_norm_act)
            ACT_SQRT = cnt["act"]

            def a_norm_dve(e, t=t):
                w(e, "act", ACT_SQRT)
                inc("dve", e.tensor_scalar(out=SMALL[:, 0:256], in0=SMALL[:, 0:256],
                                           scalar1=1e-12, scalar2=None, op0=ALU.max))
                inc("dve", e.tensor_scalar(out=SMALL[:, 512:768], in0=SMALL[:, 512:768],
                                           scalar1=1e-12, scalar2=None, op0=ALU.max))
                w(e, "dve")
                e.reciprocal(out=SMALL[:, 1024:1280], in_=SMALL[:, 0:256])
                inc("dve", e.reciprocal(out=SMALL[:, 1280:1536], in_=SMALL[:, 512:768]))
            vec_body(a_norm_dve)
            DVE_RQ = cnt["dve"]

            def a_lnr_act(e, t=t):
                w(e, "dve", DVE_RQ)
                inc("act", e.activation(SMALL[:, 512:768], SMALL[:, 1280:1536], AF.Ln))
            act_body(a_lnr_act)
            ACT_LNR = cnt["act"]

            def a_lmq_dve(e, t=t):
                w(e, "act", ACT_LNR)
                inc("dve", e.tensor_tensor(out=SMALL[:, 1792:2048], in0=SMALL[:, 256:512],
                                           in1=SMALL[:, 512:768], op=ALU.subtract))
            vec_body(a_lmq_dve)
            DVE_LMQ = cnt["dve"]

            PE_TT0 = cnt["pe"]
            DVE_TT0 = cnt["dve"]

            def a_tok_pe(e, t=t):
                w(e, "dve", DVE_LMQ)
                for sh in range(2):
                    for j, off in enumerate([1792, 1024, 1280]):
                        k = sh * 3 + j
                        if k >= 1:
                            e.wait_ge(sems["dve"], DVE_TT0 + k)
                        inc("pe", e.transpose(ptrf[:, 0:16],
                                              SMALL[0:16, off + sh * 128: off + sh * 128 + 128],
                                              IDF[0:16, 0:16]))
            blk.tensor(a_tok_pe)

            def a_tok_dve(e, t=t):
                for sh in range(2):
                    s = 2 * t + sh
                    outs = [SCT[:, sh * 16:(sh + 1) * 16], RNQ[:, s * 16:(s + 1) * 16],
                            RKT[:, sh * 16:(sh + 1) * 16]]
                    for j in range(3):
                        k = sh * 3 + j
                        e.wait_ge(sems["pe"], PE_TT0 + k + 1)
                        inc("dve", e.tensor_copy(out=outs[j], in_=ptrf[:, 0:16]))
            vec_body(a_tok_dve)
            DVE_TOK = cnt["dve"]

            PE_KT0 = cnt["pe"]
            DVE_KT0 = cnt["dve"]

            DVE_PRE_KTR = cnt["dve"]

            def a_ktr_pe(e, t=t, DVE_PRE_KTR=DVE_PRE_KTR):
                w(e, "dve", DVE_PRE_KTR)
                for i in range(16):
                    sh, fc = i // 8, i % 8
                    if i >= 4:
                        e.wait_ge(sems["dve"], DVE_KT0 + 2 * (i - 3))
                    inc("pe", e.transpose(ptrbs[i % 4],
                                          KDT[:, fc * 256 + sh * 128: fc * 256 + sh * 128 + 128], IDB[:, :]))
            blk.tensor(a_ktr_pe)

            def a_ktr_dve(e, t=t):
                for i in range(16):
                    sh, fc = i // 8, i % 8
                    e.wait_ge(sems["pe"], PE_KT0 + i + 1)
                    for hh in range(2):
                        inc("dve", e.tensor_scalar(
                            out=XNT[:, sh * 1024 + fc * 128 + hh * 64: sh * 1024 + fc * 128 + hh * 64 + 64],
                            in0=ptrbs[i % 4][:, hh * 64: hh * 64 + 64],
                            scalar1=RKT[:, sh * 16 + 2 * fc + hh: sh * 16 + 2 * fc + hh + 1],
                            scalar2=None, op0=ALU.mult))
            vec_body(a_ktr_dve)
            KTOK = XNT
            DVE_KTOK = cnt["dve"]

            for sh in range(2):
                s = 2 * t + sh
                PE_S0 = cnt["pe"]
                DVE_A0 = cnt["dve"]
                ACT_D0 = cnt["act"]
                stt_cnt = lambda i, b=DVE_A0: b + (1 if i == 0 else 2 * i)
                att_cnt = lambda i, b=DVE_A0: b + (2 * i + 3 if i < 15 else 32)
                exp_cnt = lambda i, b=ACT_D0: b + i + 1

                DVE_NOW = cnt["dve"]

                def g_sb_pe(e, t=t, sh=sh, s=s, DVE_NOW=DVE_NOW, att_cnt=att_cnt):
                    w(e, "dve", DVE_NOW)
                    for i in range(16):
                        hp, hh = i // 2, i % 2
                        if i >= 2:
                            e.wait_ge(sems["dve"], att_cnt(i - 2))
                        hr = slice(hh * 64, hh * 64 + 64)
                        col = hp * 256 + sh * 128
                        e.matmul(pBs[i % 2][:, 0:128],
                                 KF[hr, col:col + 128], QF[hr, col:col + 128], start=True, stop=True)
                        inc("pe", e.matmul(pAs[i % 2][:, 0:128],
                                           BSEL[:, i * 128:(i + 1) * 128],
                                           SMALL[0:16, 256 + sh * 128:256 + sh * 128 + 128],
                                           start=True, stop=True))
                blk.tensor(g_sb_pe)
                PE_SB = cnt["pe"]

                def g_da_dve(e, t=t, sh=sh, s=s, PE_S0=PE_S0, ACT_D0=ACT_D0, exp_cnt=exp_cnt):
                    w(e, "dout")
                    def emit_stt(i):
                        e.wait_ge(sems["pe"], PE_S0 + i + 1)
                        if i >= 2:
                            e.wait_ge(sems["act"], exp_cnt(i - 2))
                        inc("dve", e.scalar_tensor_tensor(
                            out=SCR[:, (i % 2) * 128:(i % 2) * 128 + 128],
                            in0=pAs[i % 2][:, 0:128],
                            scalar=SCT[:, sh * 16 + i: sh * 16 + i + 1],
                            in1=MSK[:, :], op0=ALU.subtract, op1=ALU.add))

                    def emit_att(i):
                        e.wait_ge(sems["act"], exp_cnt(i))
                        inc("dve", e.tensor_tensor(
                            out=ATTB[:, i * 128:(i + 1) * 128],
                            in0=pBs[i % 2][:, 0:128],
                            in1=SQB[:, (i % 4) * 128:(i % 4) * 128 + 128], op=ALU.mult))
                    emit_stt(0)
                    emit_stt(1)
                    for i in range(2, 16):
                        emit_att(i - 2)
                        emit_stt(i)
                    emit_att(14)
                    emit_att(15)
                vec_body(g_da_dve)

                def g_exp_act(e, t=t, sh=sh, s=s, stt_cnt=stt_cnt):
                    for i in range(16):
                        e.wait_ge(sems["dve"], stt_cnt(i))
                        inc("act", e.activation(SQB[:, (i % 4) * 128:(i % 4) * 128 + 128],
                                                SCR[:, (i % 2) * 128:(i % 2) * 128 + 128], AF.Exp))
                act_body(g_exp_act)

                PE_KV0 = cnt["pe"]

                def g_kv_pe(e, t=t, sh=sh, s=s):
                    w(e, "dve")
                    e.matmul(pD[:, :], ONESB[:, :], ZROW[:, :], start=True, stop=False)
                    e.matmul(pD2[:, :], ONESB[:, :], ZROW[:, :], start=True, stop=False)
                    for h in range(16):
                        hp, hh = h // 2, h % 2
                        hr = slice(hh * 64, hh * 64 + 64)
                        mm = e.matmul(pD[hr, hp * 64:(hp + 1) * 64],
                                      KTOK[0:64, sh * 1024 + hp * 128 + hh * 64: sh * 1024 + hp * 128 + hh * 64 + 64],
                                      VT[0:64, sh * 1024 + h * 64: sh * 1024 + (h + 1) * 64],
                                      start=False, stop=False)
                    inc("pe", mm)
                    w(e, "pe")
                    for h in range(16):
                        hp, hh = h // 2, h % 2
                        hr = slice(hh * 64, hh * 64 + 64)
                        e.matmul(pD2[hr, hp * 64:(hp + 1) * 64],
                                 KTOK[64:128, sh * 1024 + hp * 128 + hh * 64: sh * 1024 + hp * 128 + hh * 64 + 64],
                                 VT[64:128, sh * 1024 + h * 64: sh * 1024 + (h + 1) * 64],
                                 start=False, stop=False)
                    e.matmul(pD[:, :], ONESB[:, :], ZROW[:, :], start=False, stop=True)
                    mm = e.matmul(pD2[:, :], ONESB[:, :], ZROW[:, :], start=False, stop=True)
                    inc("pe", mm)
                blk.tensor(g_kv_pe)
                PE_KV1 = cnt["pe"]

                def g_sup_dve(e, t=t, sh=sh, s=s, PE_KV1=PE_KV1):
                    w(e, "pe", PE_KV1)
                    inc("dve", e.tensor_copy(out=SbO[:, :], in_=Sb[:, :]))
                    ci = t * 4 + sh * 2 + 0
                    last = None
                    for hp in range(8):
                        last = e.scalar_tensor_tensor(out=S[:, hp * 64:(hp + 1) * 64],
                                                      in0=S[:, hp * 64:(hp + 1) * 64],
                                                      scalar=CDA[:, hp * 32 + ci: hp * 32 + ci + 1],
                                                      in1=pD[:, hp * 64:(hp + 1) * 64],
                                                      op0=ALU.mult, op1=ALU.add)
                    inc("dve", last)
                    w(e, "dve")
                    inc("dve", e.tensor_copy(out=SbM[:, :], in_=S[:, :]))
                    w(e, "dve")
                    ci2 = t * 4 + sh * 2 + 1
                    last = None
                    for hp in range(8):
                        last = e.scalar_tensor_tensor(out=S[:, hp * 64:(hp + 1) * 64],
                                                      in0=S[:, hp * 64:(hp + 1) * 64],
                                                      scalar=CDA[:, hp * 32 + ci2: hp * 32 + ci2 + 1],
                                                      in1=pD2[:, hp * 64:(hp + 1) * 64],
                                                      op0=ALU.mult, op1=ALU.add)
                    inc("dve", last)
                    w(e, "dve")
                    inc("dve", e.tensor_copy(out=Sb[:, :], in_=S[:, :]))
                blk.vector(g_sup_dve)
                DVE_SUP2 = cnt["dve"]

                def g_o_pe(e, t=t, sh=sh, s=s, att_cnt=att_cnt, DVE_SUP2=DVE_SUP2):
                    w(e, "dve")
                    for h in range(16):
                        e.wait_ge(sems["dve"], att_cnt(h))
                        hp, hh = h // 2, h % 2
                        hr = slice(hh * 64, hh * 64 + 64)
                        pc = pCs[h // 8]
                        col = slice((h % 8) * 64, (h % 8) * 64 + 64)
                        e.matmul(pc[:, col], ATTB[:, h * 128:(h + 1) * 128],
                                 VT[:, sh * 1024 + h * 64: sh * 1024 + (h + 1) * 64],
                                 start=(h % 8 == 0), stop=False)
                        e.matmul(pc[0:64, col],
                                 QDECT[hr, hp * 256 + sh * 128: hp * 256 + sh * 128 + 64],
                                 SbO[hr, hp * 64:(hp + 1) * 64], start=False, stop=False)
                        e.matmul(pc[64:128, col],
                                 QDECT[hr, hp * 256 + sh * 128 + 64: hp * 256 + sh * 128 + 128],
                                 SbM[hr, hp * 64:(hp + 1) * 64], start=False, stop=False)
                        if h % 8 == 7:
                            mm = e.matmul(pc[:, :], ONESB[:, :], ZROW[:, :], start=False, stop=True)
                    inc("pe", mm)
                blk.tensor(g_o_pe)
                PE_O2 = cnt["pe"]

                def g_oe_dve(e, t=t, sh=sh, s=s, PE_O2=PE_O2):
                    w(e, "pe", PE_O2)
                    last = None
                    for h in range(16):
                        last = e.tensor_scalar(out=OTOK[:, h * 64:(h + 1) * 64],
                                               in0=pCs[h // 8][:, (h % 8) * 64:(h % 8) * 64 + 64],
                                               scalar1=RNQ[:, s * 16 + h: s * 16 + h + 1],
                                               scalar2=None, op0=ALU.mult)
                    inc("dve", last)
                blk.vector(g_oe_dve)
                DVE_OE = cnt["dve"]


                PE_OT0 = cnt["pe"]
                DVE_OT0 = cnt["dve"]

                def g_ot_pe(e, t=t, sh=sh, s=s, DVE_OE=DVE_OE, DVE_OT0=DVE_OT0):
                    w(e, "dve", DVE_OE)
                    for fc in range(8):
                        if fc >= 4:
                            e.wait_ge(sems["dve"], DVE_OT0 + fc - 3)
                        inc("pe", e.transpose(ptrbs[fc % 4],
                                              OTOK[:, fc * 128:(fc + 1) * 128], IDB[:, :]))
                blk.tensor(g_ot_pe)

                def g_ot_dve(e, t=t, sh=sh, s=s, PE_OT0=PE_OT0):
                    for fc in range(8):
                        e.wait_ge(sems["pe"], PE_OT0 + fc + 1)
                        inc("dve", e.tensor_copy(out=OT[:, fc * 128:(fc + 1) * 128],
                                                 in_=ptrbs[fc % 4]))
                vec_body(g_ot_dve)
                DVE_OTD = cnt["dve"]

                def g_wo_pe(e, t=t, sh=sh, s=s, DVE_OTD=DVE_OTD):
                    w(e, "dve", DVE_OTD)
                    if t == 0 and sh == 0:
                        e.wait_ge(sems["act"], ACT_WBASE + 41)
                    for nh in range(2):
                        for kc in range(8):
                            mm = e.matmul(pCs[nh][:, :], OT[:, kc * 128:(kc + 1) * 128],
                                          WOs[:, kc * 1024 + nh * 512: kc * 1024 + (nh + 1) * 512],
                                          start=(kc == 0), stop=(kc == 7))
                    inc("pe", mm)
                blk.tensor(g_wo_pe)
                PE_WO = cnt["pe"]

                def g_y2_dve(e, t=t, sh=sh, s=s, PE_WO=PE_WO):
                    w(e, "pe", PE_WO)
                    if s == 4:
                        w(e, "act", ACT_WEND)
                    e.tensor_tensor(out=Y2[:, s * 1024: s * 1024 + 512],
                                    in0=pC0[:, :], in1=YTI[:, sh * 1024: sh * 1024 + 512], op=ALU.add)
                    inc("dve", e.tensor_tensor(out=Y2[:, s * 1024 + 512:(s + 1) * 1024],
                                               in0=pC1[:, :], in1=YTI[:, sh * 1024 + 512:(sh + 1) * 1024],
                                               op=ALU.add))
                vec_body(g_y2_dve)

        global_DOUT_PUB = [0]
        _dox = DO_X
        def x_acore_dve(e):
            w(e, "act")
            last = None
            for hp in range(8):
                last = e.reduce_sum(out=ABUF[:, hp:hp + 1], in_=LAL[:, hp * 32:(hp + 1) * 32], axis=AX.X)
            inc("dve", last)
        vec_body(x_acore_dve)
        DVE_AC = cnt["dve"]

        def x_acore_act(e):
            w(e, "dve", DVE_AC)
            inc("act", e.activation(ABUF[:, :], ABUF[:, :], AF.Exp))
        if _dox: act_body(x_acore_act)
        ACT_AC = cnt["act"]

        DVE_X0 = cnt["dve"]

        def x_pub_gps(e):
            w(e, "act", ACT_AC)
            w(e, "dve")
            inc("dout", e.dma_start(out=cc_src[:, 0:512], in_=S[:, :]), 16)
            inc("dout", e.dma_start(out=cc_src[:, 512:520], in_=ABUF[:, :]), 16)
            e.wait_ge(sems["dout"], cnt["dout"])
            global_DOUT_PUB[0] = cnt["dout"]
            if USE_CC:
                inc("cc", e.collective_compute("AllGather", ALU.bypass,
                                               replica_groups=[list(range(NCORE))],
                                               ins=[cc_src], outs=[cc_dst]), 1)
                e.wait_ge(sems["cc"], cnt["cc"])
            else:
                for r in range(NCORE):
                    inc("cc", e.dma_start(out=cc_dst[r * 128:(r + 1) * 128, :], in_=SGX[:, 0:520]), 16)
                e.wait_ge(sems["cc"], cnt["cc"])
            for j, r in enumerate(range(NCORE - 1, -1, -1)):
                if j >= 2:
                    e.wait_ge(sems["dve"], DVE_X0 + 5 * (j - 1))
                inc("dout", e.dma_start(out=SGX[:, (j % 2) * 520:(j % 2) * 520 + 520],
                                        in_=cc_dst[r * 128:(r + 1) * 128, :]), 16)
                w(e, "dout")
        if _dox: blk.gpsimd(x_pub_gps)
        else: global_DOUT_PUB[0] = cnt['dout']

        DOUT_PUB = global_DOUT_PUB[0]

        def x_sin_dve(e):
            e.wait_ge(sems["dout"], DOUT_PUB)
            e.memset(S[:, :], 0.0)
            for j, r in enumerate(range(NCORE - 1, -1, -1)):
                sgr = SGX[:, (j % 2) * 520:(j % 2) * 520 + 520]
                e.wait_ge(sems["dout"], DOUT_PUB + (j + 1) * 16)
                w(e, "dve")
                inc("dve", e.tensor_scalar(out=PM[:, :], in0=PVEC[:, :], scalar1=M1S[:, r:r + 1],
                                           scalar2=None, op0=ALU.mult))
                w(e, "dve")
                last = None
                for hp in range(8):
                    last = e.scalar_tensor_tensor(out=S[:, hp * 64:(hp + 1) * 64],
                                                  in0=sgr[:, hp * 64:(hp + 1) * 64],
                                                  scalar=PM[:, hp:hp + 1],
                                                  in1=S[:, hp * 64:(hp + 1) * 64],
                                                  op0=ALU.mult, op1=ALU.add)
                inc("dve", last)
                w(e, "dve")
                inc("dve", e.tensor_scalar(out=PM[:, :], in0=sgr[:, 512:520],
                                           scalar1=1.0, scalar2=None, op0=ALU.subtract))
                w(e, "dve")
                inc("dve", e.tensor_scalar(out=PM[:, :], in0=PM[:, :], scalar1=M1S[:, r:r + 1],
                                           scalar2=1.0, op0=ALU.mult, op1=ALU.add))
                w(e, "dve")
                inc("dve", e.tensor_tensor(out=PVEC[:, :], in0=PVEC[:, :], in1=PM[:, :], op=ALU.mult))
            w(e, "dve")
            inc("dve", e.tensor_copy(out=Sb[:, :], in_=S[:, :]))
        if _dox: vec_body(x_sin_dve)

        BSTEP = DO_B if isinstance(DO_B, int) and not isinstance(DO_B, bool) else (99 if DO_B else 0)
        for t in (range(NT) if BSTEP > 0 else []):
            PE_QL = cnt["pe"]

            def b_qload(e, t=t, PE_QL=PE_QL):
                w(e, "pe", PE_QL)
                inc("dout", e.dma_start(out=QDECT[:, :].rearrange("p (f c) -> p f c", c=256),
                                        in_=qdd[:, :, t * 256:(t + 1) * 256].rearrange("f p c -> p f c")), 16)
                w(e, "dout")
            if BSTEP >= 1: blk.gpsimd(b_qload)
            DOUT_QL = cnt["dout"]

            for sh in range(2):
                s_ = 2 * t + sh

                DVE_PB0 = cnt["dve"]

                def b_sup_dve(e, t=t, sh=sh, s=s_):
                    w(e, "dve")
                    inc("dve", e.tensor_copy(out=SbO[:, :], in_=Sb[:, :]))
                    w(e, "dve")
                    ci = s * 2
                    last = None
                    for hp in range(8):
                        last = e.tensor_scalar(out=S[:, hp * 64:(hp + 1) * 64], in0=S[:, hp * 64:(hp + 1) * 64],
                                               scalar1=CDA[:, hp * 32 + ci: hp * 32 + ci + 1],
                                               scalar2=None, op0=ALU.mult)
                    inc("dve", last)
                    w(e, "dve")
                    inc("dve", e.tensor_copy(out=SbM[:, :], in_=S[:, :]))
                    w(e, "dve")
                    ci2 = s * 2 + 1
                    last = None
                    for hp in range(8):
                        last = e.tensor_scalar(out=S[:, hp * 64:(hp + 1) * 64], in0=S[:, hp * 64:(hp + 1) * 64],
                                               scalar1=CDA[:, hp * 32 + ci2: hp * 32 + ci2 + 1],
                                               scalar2=None, op0=ALU.mult)
                    inc("dve", last)
                    w(e, "dve")
                    inc("dve", e.tensor_copy(out=Sb[:, :], in_=S[:, :]))
                blk.vector(b_sup_dve)
                DVE_PB1 = cnt["dve"]

                def b_corr_pe(e, t=t, sh=sh, s=s_, DOUT_QL=DOUT_QL, DVE_PB1=DVE_PB1):
                    w(e, "dve", DVE_PB1)
                    e.wait_ge(sems["dout"], DOUT_QL)
                    mm = e.matmul(pC0[:, :], ONESB[:, :], ZROW[:, :], start=True, stop=False)
                    mm = e.matmul(pC1[:, :], ONESB[:, :], ZROW[:, :], start=True, stop=False)
                    for hh_run in range(2):
                        inc("pe", mm)
                        w(e, "pe")
                        for h in range(hh_run, 16, 2):
                            hp = h // 2
                            hr = slice(hh_run * 64, hh_run * 64 + 64)
                            pc = pCs[h // 8]
                            col = slice((h % 8) * 64, (h % 8) * 64 + 64)
                            e.matmul(pc[0:64, col],
                                     QDECT[hr, hp * 256 + sh * 128: hp * 256 + sh * 128 + 64],
                                     SbO[hr, hp * 64:(hp + 1) * 64], start=False, stop=False)
                            mm = e.matmul(pc[64:128, col],
                                          QDECT[hr, hp * 256 + sh * 128 + 64: hp * 256 + sh * 128 + 128],
                                          SbM[hr, hp * 64:(hp + 1) * 64], start=False, stop=False)
                    inc("pe", mm)
                    w(e, "pe")
                    e.matmul(pC0[:, :], ONESB[:, :], ZROW[:, :], start=False, stop=True)
                    mm = e.matmul(pC1[:, :], ONESB[:, :], ZROW[:, :], start=False, stop=True)
                    inc("pe", mm)
                blk.tensor(b_corr_pe)
                PE_CB2 = cnt["pe"]

                def b_su1_dve(e, s=s_, PE_CB2=PE_CB2):
                    w(e, "pe", PE_CB2)
                    last = None
                    for h in range(16):
                        last = e.tensor_scalar(out=OTOK[:, h * 64:(h + 1) * 64],
                                               in0=pCs[h // 8][:, (h % 8) * 64:(h % 8) * 64 + 64],
                                               scalar1=RNQ[:, s * 16 + h: s * 16 + h + 1],
                                               scalar2=None, op0=ALU.mult)
                    inc("dve", last)
                vec_body(b_su1_dve)
                DVE_BE = cnt["dve"]

                PE_BT0 = cnt["pe"]
                DVE_BT0 = cnt["dve"]

                def b_ot_pe(e, s=s_, DVE_BE=DVE_BE, DVE_BT0=DVE_BT0):
                    w(e, "dve", DVE_BE)
                    for fc in range(8):
                        if fc >= 4:
                            e.wait_ge(sems["dve"], DVE_BT0 + fc - 3)
                        inc("pe", e.transpose(ptrbs[fc % 4],
                                              OTOK[:, fc * 128:(fc + 1) * 128], IDB[:, :]))
                if BSTEP >= 6: blk.tensor(b_ot_pe)

                def b_ot_dve(e, s=s_, PE_BT0=PE_BT0):
                    for fc in range(8):
                        e.wait_ge(sems["pe"], PE_BT0 + fc + 1)
                        inc("dve", e.tensor_copy(out=OT[:, fc * 128:(fc + 1) * 128],
                                                 in_=ptrbs[fc % 4]))
                if BSTEP >= 6: vec_body(b_ot_dve)
                DVE_BTD = cnt["dve"]

                def b_wo_pe(e, s=s_, DVE_BTD=DVE_BTD):
                    w(e, "dve", DVE_BTD)
                    for nh in range(2):
                        for kc in range(8):
                            mm = e.matmul(pCs[nh][:, :], OT[:, kc * 128:(kc + 1) * 128],
                                          WOs[:, kc * 1024 + nh * 512: kc * 1024 + (nh + 1) * 512],
                                          start=(kc == 0), stop=(kc == 7))
                    inc("pe", mm)
                if BSTEP >= 7: blk.tensor(b_wo_pe)
                PE_BWO = cnt["pe"]

                def b_y2_dve(e, s=s_, PE_BWO=PE_BWO):
                    w(e, "pe", PE_BWO)
                    e.tensor_tensor(out=Y2[:, s * 1024: s * 1024 + 512],
                                    in0=pC0[:, :], in1=Y2[:, s * 1024: s * 1024 + 512], op=ALU.add)
                    inc("dve", e.tensor_tensor(out=Y2[:, s * 1024 + 512:(s + 1) * 1024],
                                               in0=pC1[:, :], in1=Y2[:, s * 1024 + 512:(s + 1) * 1024],
                                               op=ALU.add))
                if BSTEP >= 7: vec_body(b_y2_dve)

        def y2_spill(e):
            w(e, "dve")
            inc("dout", e.dma_start(out=y2d.rearrange("(s p) d -> p s d", p=128),
                                    in_=Y2[:, :].rearrange("p (s d) -> p s d", d=1024)), 16)
            w(e, "dout")
        blk.gpsimd(y2_spill)
        DOUT_Y2 = cnt["dout"]

        def bar_pe(e):
            w(e, "dve")
            inc("pe", e.nop())
        blk.tensor(bar_pe)
        PE_BAR = cnt["pe"]

        def bar_act(e):
            w(e, "pe", PE_BAR)
            inc("act", e.nop())
        act_body(bar_act)
        ACT_BAR = cnt["act"]
        stA.close()

        stC = ExitStack()
        EC = stC.enter_context
        W1S = EC(nc.sbuf_tensor("W1S", [128, 8 * FFN], BF16))
        W2S = EC(nc.sbuf_tensor("W2S", [128, 32 * 1024], BF16))
        KQ8S = EC(nc.sbuf_tensor("KQ8S", [128, 8 * 1024], FP8))
        Y2T = EC(nc.sbuf_tensor("Y2T", [128, 2048], BF16))
        XTc = EC(nc.sbuf_tensor("XTc", [128, 2048], F32))
        XY = EC(nc.sbuf_tensor("XYc", [128, 2048], F32))
        X1F = EC(nc.sbuf_tensor("X1F", [128, 2048], F32))
        HB = EC(nc.sbuf_tensor("HB", [128, 2048], BF16))
        HT = EC(nc.sbuf_tensor("HT", [128, 2048], BF16))
        G1 = EC(nc.sbuf_tensor("G1", [128, 32 * 256], BF16))
        ZQ8 = EC(nc.sbuf_tensor("ZQ8", [128, 2048], FP8))
        YBF = EC(nc.sbuf_tensor("YBF", [128, 1024], BF16))
        YBF2 = EC(nc.sbuf_tensor("YBF2", [128, 1024], BF16))
        STATSc = EC(nc.sbuf_tensor("STATSc", [128, 32], F32))
        WSTc = XTc
        WSTc2 = XY
        SCRc = XTc
        ZB = HB
        ZT = HT
        OUTB = XTc

        DIN_CW = cnt["din"]
        ACT_CW = cnt["act"]
        DWC0 = cnt["dw"]
        DWC1 = cnt["dw2"]
        CSLOT = [XTc[:, 0:1024], XTc[:, 1024:2048], XY[:, 0:1024], XY[:, 1024:2048],
                 X1F[:, 0:1024], X1F[:, 1024:2048]]
        FKSLOT = [HB[:, 0:1024], HB[:, 1024:2048],
                  X1F.bitcast(BF16)[:, 0:1024], X1F.bitcast(BF16)[:, 1024:2048]]
        wsrcs = []
        for kc in range(8):
            for pp in range(4):
                wsrcs.append((W1[kc * 128:(kc + 1) * 128, pp * 1024:(pp + 1) * 1024],
                              W1S[:, kc * FFN + pp * 1024: kc * FFN + (pp + 1) * 1024]))
        for kc in range(32):
            wsrcs.append((W2[kc * 128:(kc + 1) * 128, :], W2S[:, kc * 1024:(kc + 1) * 1024]))
        NWC = len(wsrcs)

        def c_wload(e):
            w(e, "act", ACT_BAR)
            w(e, "dout")
            for i in range(0, NWC, 2):
                e.wait_ge(sems["act"], ACT_CW + max(0, i - 5))
                inc("dw", e.dma_start(out=CSLOT[i % 6], in_=wsrcs[i][0]), 16)
            for kc in range(8):
                e.wait_ge(sems["act"], ACT_CW + NWC + max(0, kc - 3))
                inc("dw", e.dma_start(out=FKSLOT[kc % 4],
                                      in_=fk[kc * 128:(kc + 1) * 128, :]), 16)
        blk.sync(c_wload)

        def c_wload2(e):
            w(e, "act", ACT_BAR)
            w(e, "dout")
            for i in range(1, NWC, 2):
                e.wait_ge(sems["act"], ACT_CW + max(0, i - 5))
                inc("dw2", e.dma_start(out=CSLOT[i % 6], in_=wsrcs[i][0]), 16)
        blk.gpsimd(c_wload2)

        def c_wcast(e):
            w(e, "dout")
            for i in range(NWC):
                j = i // 2
                if i % 2 == 0:
                    e.wait_ge(sems["dw"], DWC0 + min(j + 3, NWC // 2) * 16)
                else:
                    e.wait_ge(sems["dw2"], DWC1 + min(j + 3, NWC // 2) * 16)
                inc("act", e.activation(wsrcs[i][1], CSLOT[i % 6], AF.Copy))
            for kc in range(8):
                e.wait_ge(sems["dw"], DWC0 + (NWC // 2 + min(kc + 4, 8)) * 16)
                inc("act", e.activation(KQ8S[:, kc * 1024:(kc + 1) * 1024],
                                        FKSLOT[kc % 4],
                                        AF.Copy, scale=RSK[:, 0:1]))
        act_body(c_wcast)
        ACT_CWE = cnt["act"]

        for t in (range(NT) if DO_C else []):
            def c_x(e, t=t):
                w(e, "dve")
                w(e, "act")
                w(e, "dout")
                if t == 0:
                    e.wait_ge(sems["dout"], DOUT_Y2)
                for sh in range(2):
                    inc("din", e.dma_start(out=XTc[:, sh * 1024:(sh + 1) * 1024],
                                           in_=x_in[t * 256 + sh * 128: t * 256 + sh * 128 + 128, :]), 16)
                    inc("din", e.dma_start(out=Y2T[:, sh * 1024:(sh + 1) * 1024],
                                           in_=y2d[t * 256 + sh * 128: t * 256 + sh * 128 + 128, :]), 16)
                w(e, "din")
            blk.sync(c_x)

            def c_xy_dve(e, t=t):
                w(e, "din")
                w(e, "act", ACT_CWE)
                for sh in range(2):
                    inc("dve", e.tensor_tensor(out=XY[:, sh * 1024:(sh + 1) * 1024],
                                               in0=XTc[:, sh * 1024:(sh + 1) * 1024],
                                               in1=Y2T[:, sh * 1024:(sh + 1) * 1024], op=ALU.add))
            vec_body(c_xy_dve)

            def c_sq2(e, t=t):
                w(e, "act", ACT_CWE)
                w(e, "dve")
                for sh in range(2):
                    c0 = sh * 16
                    inc("act", e.activation(SCRc[:, sh * 1024:(sh + 1) * 1024],
                                            Y2T[:, sh * 1024:(sh + 1) * 1024], AF.Square,
                                            accum_out=STATSc[:, c0 + 1:c0 + 2]))
            act_body(c_sq2)
            ACT_S2 = cnt["act"]

            def c_ln2(e, t=t, ACT_S2=ACT_S2):
                for sh in range(2):
                    c0 = sh * 16
                    inc("dve", e.reduce_sum(out=STATSc[:, c0:c0 + 1],
                                            in_=Y2T[:, sh * 1024:(sh + 1) * 1024], axis=AX.X))
                    w(e, "dve")
                    inc("dve", e.tensor_scalar(out=STATSc[:, c0 + 2:c0 + 3], in0=STATSc[:, c0:c0 + 1],
                                               scalar1=1.0 / D, scalar2=None, op0=ALU.mult))
                    w(e, "dve")
                    inc("dve", e.tensor_tensor(out=STATSc[:, c0 + 3:c0 + 4], in0=STATSc[:, c0 + 2:c0 + 3],
                                               in1=STATSc[:, c0 + 2:c0 + 3], op=ALU.mult))
                    e.wait_ge(sems["act"], ACT_S2 - 1 + sh)
                    w(e, "dve")
                    inc("dve", e.scalar_tensor_tensor(out=STATSc[:, c0 + 4:c0 + 5], in0=STATSc[:, c0 + 1:c0 + 2],
                                                      scalar=1.0 / D, in1=STATSc[:, c0 + 3:c0 + 4],
                                                      op0=ALU.mult, op1=ALU.subtract))
            vec_body(c_ln2)
            DVE_V2 = cnt["dve"]

            def c_sd2(e, t=t):
                w(e, "dve", DVE_V2)
                for sh in range(2):
                    c0 = sh * 16
                    inc("act", e.activation(STATSc[:, c0 + 5:c0 + 6], STATSc[:, c0 + 4:c0 + 5],
                                            AF.Sqrt, bias=EPSC[:, 0:1]))
            act_body(c_sd2)
            ACT_SD2 = cnt["act"]

            def c_h(e, t=t):
                w(e, "act", ACT_SD2)
                e.reciprocal(out=STATSc[:, 6:7], in_=STATSc[:, 5:6])
                inc("dve", e.reciprocal(out=STATSc[:, 22:23], in_=STATSc[:, 21:22]))
                w(e, "dve")
                for sh in range(2):
                    c0 = sh * 16
                    inc("dve", e.tensor_scalar(out=HB[:, sh * 1024:(sh + 1) * 1024],
                                               in0=Y2T[:, sh * 1024:(sh + 1) * 1024],
                                               scalar1=STATSc[:, c0 + 2:c0 + 3],
                                               scalar2=STATSc[:, c0 + 6:c0 + 7],
                                               op0=ALU.subtract, op1=ALU.mult))
            vec_body(c_h)
            DVE_H = cnt["dve"]

            PE_HT0 = cnt["pe"]
            DVE_HT0 = cnt["dve"]

            def c_ht_pe(e, t=t, DVE_H=DVE_H, DVE_HT0=DVE_HT0):
                w(e, "dve", DVE_H)
                for i in range(16):
                    sh, fc = i // 8, i % 8
                    if i >= 4:
                        e.wait_ge(sems["dve"], DVE_HT0 + i - 3)
                    inc("pe", e.transpose(ptrbs[i % 4],
                                          HB[:, sh * 1024 + fc * 128: sh * 1024 + fc * 128 + 128], IDB[:, :]))
            blk.tensor(c_ht_pe)

            def c_ht_dve(e, t=t, PE_HT0=PE_HT0):
                for i in range(16):
                    sh, fc = i // 8, i % 8
                    e.wait_ge(sems["pe"], PE_HT0 + i + 1)
                    inc("dve", e.tensor_scalar(out=HT[:, fc * 256 + sh * 128: fc * 256 + sh * 128 + 128],
                                               in0=ptrbs[i % 4],
                                               scalar1=PRM[:, 24 + fc:25 + fc], scalar2=PRM[:, 32 + fc:33 + fc],
                                               op0=ALU.mult, op1=ALU.add))
            vec_body(c_ht_dve)
            DVE_HT = cnt["dve"]

            PE_W10 = cnt["pe"]
            ACT_G10 = cnt["act"]

            def c_w1_pe(e, t=t, DVE_HT=DVE_HT, ACT_G10=ACT_G10):
                w(e, "dve", DVE_HT)
                for oc in range(32):
                    if oc >= 2:
                        e.wait_ge(sems["act"], ACT_G10 + oc - 1)
                    for kc in range(8):
                        mm = e.matmul(pAs[oc % 2][:, :],
                                      W1S[:, kc * FFN + oc * 128: kc * FFN + (oc + 1) * 128],
                                      HT[:, kc * 256:(kc + 1) * 256], start=(kc == 0), stop=(kc == 7))
                    inc("pe", mm)
            blk.tensor(c_w1_pe)

            def c_gelu_act(e, t=t, PE_W10=PE_W10):
                for oc in range(32):
                    e.wait_ge(sems["pe"], PE_W10 + oc + 1)
                    inc("act", e.activation(G1[:, oc * 256:(oc + 1) * 256], pAs[oc % 2][:, :],
                                            AF.Gelu, bias=B1C[:, oc:oc + 1]))
            act_body(c_gelu_act)
            ACT_G1E = cnt["act"]

            for th in range(2):
                def c_w2_pe(e, t=t, th=th, ACT_G1E=ACT_G1E):
                    w(e, "act", ACT_G1E)
                    if th == 1:
                        w(e, "dve")
                    for oc in range(32):
                        for nh in range(2):
                            e.matmul(pCs[nh][:, :],
                                     G1[:, oc * 256 + th * 128: oc * 256 + th * 128 + 128],
                                     W2S[:, oc * 1024 + nh * 512: oc * 1024 + (nh + 1) * 512],
                                     start=(oc == 0), stop=False)
                    for nh in range(2):
                        mm = e.matmul(pCs[nh][:, :], ONESB[:, :],
                                      B2ROWB[:, nh * 512:(nh + 1) * 512], start=False, stop=True)
                    inc("pe", mm)
                blk.tensor(c_w2_pe)
                PE_W2 = cnt["pe"]

                def c_x1f_dve(e, t=t, th=th, PE_W2=PE_W2):
                    w(e, "pe", PE_W2)
                    e.scalar_tensor_tensor(out=X1F[:, th * 1024: th * 1024 + 512],
                                           in0=pC0[:, :], scalar=0.0,
                                           in1=XY[:, th * 1024: th * 1024 + 512],
                                           op0=ALU.bypass, op1=ALU.add)
                    inc("dve", e.scalar_tensor_tensor(out=X1F[:, th * 1024 + 512:(th + 1) * 1024],
                                                      in0=pC1[:, :], scalar=0.0,
                                                      in1=XY[:, th * 1024 + 512:(th + 1) * 1024],
                                                      op0=ALU.bypass, op1=ALU.add))
                vec_body(c_x1f_dve)

            DVE_X1F = cnt["dve"]

            def c_sq3(e, t=t, DVE_X1F=DVE_X1F):
                w(e, "dve", DVE_X1F)
                for sh in range(2):
                    c0 = sh * 16
                    inc("act", e.activation(SCRc[:, sh * 1024:(sh + 1) * 1024],
                                            X1F[:, sh * 1024:(sh + 1) * 1024], AF.Square,
                                            accum_out=STATSc[:, c0 + 1:c0 + 2]))
            act_body(c_sq3)
            ACT_S3 = cnt["act"]

            def c_ln3(e, t=t, DVE_X1F=DVE_X1F, ACT_S3=ACT_S3):
                w(e, "dve", DVE_X1F)
                for sh in range(2):
                    c0 = sh * 16
                    inc("dve", e.reduce_sum(out=STATSc[:, c0:c0 + 1],
                                            in_=X1F[:, sh * 1024:(sh + 1) * 1024], axis=AX.X))
                    w(e, "dve")
                    inc("dve", e.tensor_scalar(out=STATSc[:, c0 + 2:c0 + 3], in0=STATSc[:, c0:c0 + 1],
                                               scalar1=1.0 / D, scalar2=None, op0=ALU.mult))
                    w(e, "dve")
                    inc("dve", e.tensor_tensor(out=STATSc[:, c0 + 3:c0 + 4], in0=STATSc[:, c0 + 2:c0 + 3],
                                               in1=STATSc[:, c0 + 2:c0 + 3], op=ALU.mult))
                    e.wait_ge(sems["act"], ACT_S3 - 1 + sh)
                    w(e, "dve")
                    inc("dve", e.scalar_tensor_tensor(out=STATSc[:, c0 + 4:c0 + 5], in0=STATSc[:, c0 + 1:c0 + 2],
                                                      scalar=1.0 / D, in1=STATSc[:, c0 + 3:c0 + 4],
                                                      op0=ALU.mult, op1=ALU.subtract))
            vec_body(c_ln3)
            DVE_V3 = cnt["dve"]

            def c_sd3(e, t=t, DVE_V3=DVE_V3):
                w(e, "dve", DVE_V3)
                for sh in range(2):
                    c0 = sh * 16
                    inc("act", e.activation(STATSc[:, c0 + 5:c0 + 6], STATSc[:, c0 + 4:c0 + 5],
                                            AF.Sqrt, bias=EPSC[:, 0:1]))
            act_body(c_sd3)
            ACT_SD3 = cnt["act"]

            def c_z(e, t=t, ACT_SD3=ACT_SD3):
                w(e, "act", ACT_SD3)
                e.reciprocal(out=STATSc[:, 6:7], in_=STATSc[:, 5:6])
                inc("dve", e.reciprocal(out=STATSc[:, 22:23], in_=STATSc[:, 21:22]))
                w(e, "dve")
                for sh in range(2):
                    c0 = sh * 16
                    inc("dve", e.tensor_scalar(out=ZB[:, sh * 1024:(sh + 1) * 1024],
                                               in0=X1F[:, sh * 1024:(sh + 1) * 1024],
                                               scalar1=STATSc[:, c0 + 2:c0 + 3],
                                               scalar2=STATSc[:, c0 + 6:c0 + 7],
                                               op0=ALU.subtract, op1=ALU.mult))
            vec_body(c_z)
            DVE_Z = cnt["dve"]

            PE_ZT0 = cnt["pe"]
            DVE_ZT0 = cnt["dve"]

            def c_zt_pe(e, t=t, DVE_Z=DVE_Z, DVE_ZT0=DVE_ZT0):
                w(e, "dve", DVE_Z)
                for i in range(16):
                    sh, fc = i // 8, i % 8
                    if i >= 4:
                        e.wait_ge(sems["dve"], DVE_ZT0 + i - 3)
                    inc("pe", e.transpose(ptrbs[i % 4],
                                          ZB[:, sh * 1024 + fc * 128: sh * 1024 + fc * 128 + 128], IDB[:, :]))
            blk.tensor(c_zt_pe)

            def c_zt_dve(e, t=t, PE_ZT0=PE_ZT0):
                for i in range(16):
                    sh, fc = i // 8, i % 8
                    e.wait_ge(sems["pe"], PE_ZT0 + i + 1)
                    inc("dve", e.tensor_copy(out=ZT[:, fc * 256 + sh * 128: fc * 256 + sh * 128 + 128],
                                             in_=ptrbs[i % 4]))
            vec_body(c_zt_dve)
            DVE_ZT = cnt["dve"]

            def c_zq_act(e, t=t, DVE_ZT=DVE_ZT):
                w(e, "dve", DVE_ZT)
                for fc in range(8):
                    inc("act", e.activation(ZQ8[:, fc * 256:(fc + 1) * 256],
                                            ZT[:, fc * 256:(fc + 1) * 256],
                                            AF.Copy, scale=RSIN[:, 0:1]))
            act_body(c_zq_act)
            ACT_ZQ = cnt["act"]

            for th in range(2):
                def c_fk_pe(e, t=t, th=th):
                    w(e, "act")
                    w(e, "dve")
                    for nh in range(2):
                        for kc in range(8):
                            mm = e.matmul(pCs[nh][:, :],
                                          ZQ8[:, kc * 256 + th * 128: kc * 256 + th * 128 + 128],
                                          KQ8S[:, kc * 1024 + nh * 512: kc * 1024 + (nh + 1) * 512],
                                          start=(kc == 0), stop=(kc == 7))
                    inc("pe", mm)
                blk.tensor(c_fk_pe)
                PE_FK = cnt["pe"]

                def c_yb_act(e, t=t, th=th, PE_FK=PE_FK):
                    w(e, "pe", PE_FK)
                    e.activation(YBF[:, 0:512], pC0[:, :], AF.Copy, scale=SPROD[:, 0:1])
                    inc("act", e.activation(YBF[:, 512:1024], pC1[:, :], AF.Copy, scale=SPROD[:, 0:1]))
                act_body(c_yb_act)
                ACT_YB = cnt["act"]

                def c_out_dve(e, t=t, th=th, ACT_YB=ACT_YB):
                    w(e, "act", ACT_YB)
                    w(e, "dout")
                    inc("dve", e.tensor_tensor(out=YBF2[:, :], in0=YBF[:, :], in1=BIASB[:, :], op=ALU.add))
                    w(e, "dve")
                    inc("dve", e.tensor_tensor(out=OUTB[:, th * 1024:(th + 1) * 1024],
                                               in0=X1F[:, th * 1024:(th + 1) * 1024],
                                               in1=YBF2[:, :], op=ALU.add))
                vec_body(c_out_dve)
                DVE_OUT = cnt["dve"]

                def c_store(e, t=t, th=th, DVE_OUT=DVE_OUT):
                    w(e, "dve", DVE_OUT)
                    inc("dout", e.dma_start(out=out[t * 256 + th * 128: t * 256 + th * 128 + 128, :],
                                            in_=OUTB[:, th * 1024:(th + 1) * 1024]), 16)
                    w(e, "dout")
                blk.gpsimd(c_store)

        def fin(e):
            w(e, "dout")
        blk.gpsimd(fin)
        stC.close()
    return nc


def _consts():
    identf = np.eye(128, dtype=np.float32)
    identb = identf.astype(ml_dtypes.bfloat16)
    mrow = np.arange(128)[:, None]
    ccol = np.arange(128)[None, :]
    valid = (mrow // 64 == ccol // 64) & (mrow <= ccol)
    maskneg = np.where(valid, 0.0, NEG).astype(np.float32)
    sel16f = np.zeros((128, 128), np.float32)
    sel1b_ = np.zeros((128, 128), np.float32)
    for fc in range(8):
        sel16f[0:64, fc * 16 + 2 * fc] = 1.0 / 64
        sel16f[64:128, fc * 16 + 2 * fc + 1] = 1.0 / 64
        sel1b_[0:64, fc * 16 + 2 * fc] = 1.0
        sel1b_[64:128, fc * 16 + 2 * fc + 1] = 1.0
    sel1b = sel1b_.astype(ml_dtypes.bfloat16)
    bsel = np.zeros((16, 16 * 128), np.float32)
    for h in range(16):
        bsel[h, h * 128:(h + 1) * 128] = 1.0
    onesf = np.ones((1, 128), np.float32)
    onesb = np.ones((1, 128), ml_dtypes.bfloat16)
    return dict(identf=identf, identb=identb, maskneg=maskneg, sel16f=sel16f,
                sel1b=sel1b, bsel=bsel, onesf=onesf, onesb=onesb)


_NC_CACHE = {}


def kernel(L0_attn_x, L0_attn_Wq, L0_attn_Wk, L0_attn_Wv, L0_attn_Wg, L0_attn_bg,
           L0_attn_Wo, L0_attn_ln1_w, L0_attn_ln1_b, L0_attn_ln2_w, L0_attn_ln2_b,
           L0_attn_W1, L0_attn_b1, L0_attn_W2, L0_attn_b2,
           L0_ffn_kernel, L0_ffn_input_scale, L0_ffn_kernel_scale, L0_ffn_bias):
    if "nc" not in _NC_CACHE:
        _NC_CACHE["nc"] = build_nc()
    nc = _NC_CACHE["nc"]
    consts = _consts()
    x = np.ascontiguousarray(np.asarray(L0_attn_x).reshape(16384, 1024))
    shared = dict(
        Wq=np.ascontiguousarray(L0_attn_Wq), Wk=np.ascontiguousarray(L0_attn_Wk),
        Wv=np.ascontiguousarray(L0_attn_Wv), Wg=np.ascontiguousarray(L0_attn_Wg),
        bg=np.ascontiguousarray(L0_attn_bg), Wo=np.ascontiguousarray(L0_attn_Wo),
        ln1w=np.ascontiguousarray(L0_attn_ln1_w), ln1b=np.ascontiguousarray(L0_attn_ln1_b),
        ln2w=np.ascontiguousarray(L0_attn_ln2_w), ln2b=np.ascontiguousarray(L0_attn_ln2_b),
        W1=np.ascontiguousarray(L0_attn_W1), b1=np.ascontiguousarray(L0_attn_b1),
        W2=np.ascontiguousarray(L0_attn_W2), b2=np.ascontiguousarray(L0_attn_b2),
        fk=np.ascontiguousarray(L0_ffn_kernel),
        fsin=np.ascontiguousarray(L0_ffn_input_scale),
        fsk=np.ascontiguousarray(L0_ffn_kernel_scale),
        fbias=np.ascontiguousarray(L0_ffn_bias),
        **consts,
    )
    in_maps = []
    for c in range(NCORE):
        m1c = np.zeros((128, 8), np.float32)
        m1c[:, :c] = 1.0
        in_maps.append(dict(shared, x=np.ascontiguousarray(x[c * T:(c + 1) * T]), m1=m1c))
    res = run_bass_kernel_spmd(nc, in_maps, core_ids=list(range(NCORE)))
    full = np.concatenate([res.results[c]["out"] for c in range(NCORE)], axis=0)
    return full.reshape(1, 16384, 1024)


# version 09
# speedup vs baseline: 1.0602x; 1.0116x over previous
import numpy as np
import ml_dtypes
import concourse.bass as bass
import concourse.mybir as mybir
from concourse.bass_utils import run_bass_kernel_spmd
from contextlib import ExitStack

F32 = mybir.dt.float32
BF16 = mybir.dt.bfloat16
FP8 = mybir.dt.float8e4
AF = mybir.ActivationFunctionType
ALU = mybir.AluOpType
AX = mybir.AxisListType

NCORE = 8
T = 2048
D = 1024
FFN = 4096
NT = 8
NS = 16
EPS = 1e-5
NEG = -1e30
USE_CC = True
DO_B = True
DO_C = True
CORR_MODE = 0
DO_X = True


def build_nc():
    nc = bass.Bass("TRN2", target_bir_lowering=False, num_devices=NCORE)
    g = lambda n, s, d, k="ExternalInput": nc.dram_tensor(n, s, d, kind=k).ap()
    x_in = g("x", [T, D], F32)
    Wq = g("Wq", [D, D], F32); Wk = g("Wk", [D, D], F32)
    Wv = g("Wv", [D, D], F32); Wg = g("Wg", [D, D], F32)
    bgv = g("bg", [D], F32); Wo = g("Wo", [D, D], F32)
    ln1w = g("ln1w", [D], F32); ln1b = g("ln1b", [D], F32)
    ln2w = g("ln2w", [D], F32); ln2b = g("ln2b", [D], F32)
    W1 = g("W1", [D, FFN], F32); b1 = g("b1", [FFN], F32)
    W2 = g("W2", [FFN, D], F32); b2 = g("b2", [D], F32)
    fk = g("fk", [D, D], BF16)
    fsin = g("fsin", [1], F32); fsk = g("fsk", [1], F32)
    fbias = g("fbias", [D], BF16)
    identf = g("identf", [128, 128], F32); identb = g("identb", [128, 128], BF16)
    maskneg = g("maskneg", [128, 128], F32)
    sel16f = g("sel16f", [128, 128], F32); sel1b = g("sel1b", [128, 128], BF16)
    bsel = g("bsel", [16, 16 * 128], F32)
    onesf = g("onesf", [1, 128], F32); onesb = g("onesb", [1, 128], BF16)
    m1 = g("m1", [128, 8], F32)
    out = g("out", [T, D], F32, k="ExternalOutput")
    cc_src = g("cc_src", [128, 520], F32, k="Internal")
    cc_dst = g("cc_dst", [NCORE * 128, 520], F32, k="Internal")
    y2d = g("y2d", [T, D], BF16, k="Internal")
    qdd = g("qdd", [8, 128, T], BF16, k="Internal")

    st = ExitStack()
    E = st.enter_context
    with st:
        IDF = E(nc.sbuf_tensor("IDF", [128, 128], F32))
        IDB = E(nc.sbuf_tensor("IDB", [128, 128], BF16))
        ONESF = E(nc.sbuf_tensor("ONESF", [1, 128], F32))
        ONESB = E(nc.sbuf_tensor("ONESB", [1, 128], BF16))
        M1S = E(nc.sbuf_tensor("M1S", [128, 8], F32))
        PRM = E(nc.sbuf_tensor("PRM", [128, 48], F32))
        B1C = E(nc.sbuf_tensor("B1C", [128, 32], F32))
        PRMROW = E(nc.sbuf_tensor("PRMROW", [32, 128], F32))
        B2ROWB = E(nc.sbuf_tensor("B2ROWB", [1, D], BF16))
        FBROW = E(nc.sbuf_tensor("FBROW", [1, D], BF16))
        BIASB = E(nc.sbuf_tensor("BIASB", [128, D], BF16))
        SCAL = E(nc.sbuf_tensor("SCAL", [1, 8], F32))
        ZROW = E(nc.sbuf_tensor("ZROW", [1, 512], BF16))
        EPSC = E(nc.sbuf_tensor("EPSC", [128, 1], F32))
        RSIN = E(nc.sbuf_tensor("RSIN", [128, 1], F32))
        RSK = E(nc.sbuf_tensor("RSK", [128, 1], F32))
        SPROD = E(nc.sbuf_tensor("SPROD", [128, 1], F32))
        pA0 = E(nc.psum_tensor("pA0", [128, 256], F32))
        pA1 = E(nc.psum_tensor("pA1", [128, 256], F32))
        pB0 = E(nc.psum_tensor("pB0", [128, 256], F32))
        pB1 = E(nc.psum_tensor("pB1", [128, 256], F32))
        pC0 = E(nc.psum_tensor("pC0", [128, 512], F32))
        pC1 = E(nc.psum_tensor("pC1", [128, 512], F32))
        pAs = [pA0, pA1]
        ptrf = pA1
        ptrbs = None
        pBs = [pB0, pB1]
        pCs = [pC0, pC1]
        pD = E(nc.psum_tensor("pD", [128, 512], F32))
        pD2 = E(nc.psum_tensor("pD2", [128, 512], F32))
        ptrbs = [pA0.bitcast(BF16)[:, 0:128], pA1.bitcast(BF16)[:, 0:128],
                 pD.bitcast(BF16)[:, 0:128], pD2.bitcast(BF16)[:, 0:128]]
        sems = {}
        for name in ["pe", "dve", "act", "din", "dout", "cc", "dw", "dw2", "dw3"]:
            sems[name] = E(nc.semaphore("sm_" + name))
        cnt = {k: 0 for k in sems}
        blk = E(nc.Block())

        def vec_body(f):
            def g(e):
                w(e, "dve")
                f(e)
            blk.vector(g)

        def act_body(f):
            def g(e):
                w(e, "act")
                f(e)
            blk.scalar(g)

        def inc(name, inst, n=1):
            inst.then_inc(sems[name], n)
            cnt[name] += n
            return cnt[name]

        def w(e, name, v=None):
            e.wait_ge(sems[name], cnt[name] if v is None else v)

        def ph0_sync(e):
            for ap, sb in [(identf, IDF), (identb, IDB), (onesf, ONESF), (onesb, ONESB), (m1, M1S)]:
                inc("din", e.dma_start(out=sb[:, :], in_=ap), 16)
            inc("din", e.dma_start(out=FBROW[:, :], in_=fbias.rearrange("(o d) -> o d", o=1)), 16)
            inc("din", e.dma_start(out=SCAL[0:1, 0:1], in_=fsin.rearrange("(o d) -> o d", o=1)), 16)
            inc("din", e.dma_start(out=SCAL[0:1, 1:2], in_=fsk.rearrange("(o d) -> o d", o=1)), 16)
            w(e, "din")
        blk.sync(ph0_sync)

        for i, ap in enumerate([bgv, ln1w, ln1b, ln2w, ln2b]):
            PE_PV = cnt["pe"]

            def pv_sync(e, ap=ap, i=i, PE_PV=PE_PV):
                w(e, "pe", PE_PV)
                inc("din", e.dma_start(out=PRMROW[0:8, 0:128].rearrange("p (o a) -> p o a", o=1),
                                       in_=ap.rearrange("(a o p) -> a o p", o=1, p=128)), 16)
                w(e, "din")
            blk.sync(pv_sync)
            DVE_PV = cnt["dve"]

            def pv_pe(e, i=i, DVE_PV=DVE_PV):
                w(e, "din")
                w(e, "dve", DVE_PV)
                inc("pe", e.transpose(ptrf[:, 0:8], PRMROW[0:8, 0:128], IDF[0:8, 0:8]))
            blk.tensor(pv_pe)

            def pv_dve(e, i=i):
                w(e, "pe")
                inc("dve", e.tensor_copy(out=PRM[:, i * 8:(i + 1) * 8], in_=ptrf[:, 0:8]))
            vec_body(pv_dve)
        PE_PARAMS = cnt["pe"]

        def b1_sync(e):
            w(e, "pe", PE_PARAMS)
            inc("din", e.dma_start(out=PRMROW[0:32, 0:128].rearrange("p (o a) -> p o a", o=1),
                                   in_=b1.rearrange("(a o p) -> a o p", o=1, p=128)), 16)
            w(e, "din")
        blk.sync(b1_sync)

        def b1_pe(e):
            w(e, "din")
            w(e, "dve")
            inc("pe", e.transpose(ptrf[:, 0:32], PRMROW[0:32, 0:128], IDF[0:32, 0:32]))
        blk.tensor(b1_pe)
        PE_B1 = cnt["pe"]

        def b1_dve(e):
            w(e, "pe", PE_B1)
            inc("dve", e.tensor_copy(out=B1C[:, :], in_=ptrf[:, 0:32]))
        vec_body(b1_dve)

        def sc_dve(e):
            w(e, "din")
            e.memset(EPSC[:, :], EPS)
            e.memset(ZROW[:, :], 0.0)
            e.reciprocal(out=SCAL[0:1, 2:3], in_=SCAL[0:1, 0:1])
            e.reciprocal(out=SCAL[0:1, 3:4], in_=SCAL[0:1, 1:2])
            inc("dve", e.tensor_tensor(out=SCAL[0:1, 4:5], in0=SCAL[0:1, 0:1],
                                       in1=SCAL[0:1, 1:2], op=ALU.mult))
        vec_body(sc_dve)

        def sc_pe(e):
            w(e, "dve")
            e.matmul(pA0[:, 0:1], ONESF[:, :], SCAL[0:1, 2:3], start=True, stop=False)
            e.matmul(pA0[:, 1:2], ONESF[:, :], SCAL[0:1, 3:4], start=False, stop=False)
            e.matmul(pA0[:, 2:3], ONESF[:, :], SCAL[0:1, 4:5], start=False, stop=True)
            e.matmul(pC0[:, :], ONESB[:, :], FBROW[:, 0:512], start=True, stop=True)
            inc("pe", e.matmul(pC1[:, :], ONESB[:, :], FBROW[:, 512:1024], start=True, stop=True))
        blk.tensor(sc_pe)

        def sc_dve2(e):
            w(e, "pe")
            e.tensor_copy(out=RSIN[:, :], in_=pA0[:, 0:1])
            e.tensor_copy(out=RSK[:, :], in_=pA0[:, 1:2])
            e.tensor_copy(out=SPROD[:, :], in_=pA0[:, 2:3])
            e.tensor_copy(out=BIASB[:, 0:512], in_=pC0[:, :])
            inc("dve", e.tensor_copy(out=BIASB[:, 512:1024], in_=pC1[:, :]))
        vec_body(sc_dve2)


        stA = ExitStack()
        EA = stA.enter_context
        Y2 = EA(nc.sbuf_tensor("Y2", [128, NS * 1024], BF16))
        QDECT = EA(nc.sbuf_tensor("QDECT", [128, 2048], BF16))
        RNQ = EA(nc.sbuf_tensor("RNQ", [128, NS * 16], F32))
        LAL = EA(nc.sbuf_tensor("LAL", [128, 8 * 32], F32))
        CDA = EA(nc.sbuf_tensor("CDA", [128, 8 * 32], F32))
        S = EA(nc.sbuf_tensor("S", [128, 8 * 64], F32))
        Sb = EA(nc.sbuf_tensor("Sb", [128, 8 * 64], BF16))
        SbO = EA(nc.sbuf_tensor("SbO", [128, 8 * 64], BF16))
        SbM = EA(nc.sbuf_tensor("SbM", [128, 8 * 64], BF16))
        ABUF = EA(nc.sbuf_tensor("ABUF", [128, 8], F32))
        SGX = EA(nc.sbuf_tensor("SGX", [128, 2 * 520], F32))
        PVEC = EA(nc.sbuf_tensor("PVEC", [128, 8], F32))
        PM = EA(nc.sbuf_tensor("PM", [128, 8], F32))
        MSK = EA(nc.sbuf_tensor("MSK", [128, 128], F32))
        SELF = EA(nc.sbuf_tensor("SELF", [128, 128], F32))
        SELB = EA(nc.sbuf_tensor("SELB", [128, 128], BF16))
        BSEL = EA(nc.sbuf_tensor("BSEL", [16, 16 * 128], F32))
        SMALL = EA(nc.sbuf_tensor("SMALL", [16, 2048], F32))
        SCT = EA(nc.sbuf_tensor("SCT", [128, 32], F32))
        RKT = EA(nc.sbuf_tensor("RKT", [128, 32], F32))
        WQ = EA(nc.sbuf_tensor("WQs", [128, 8 * 1024], BF16))
        WK = EA(nc.sbuf_tensor("WKs", [128, 8 * 1024], BF16))
        WV = EA(nc.sbuf_tensor("WVs", [128, 8 * 1024], BF16))
        WGs = EA(nc.sbuf_tensor("WGs", [128, 8 * 1024], BF16))
        WOs = EA(nc.sbuf_tensor("WOs", [128, 8 * 1024], BF16))
        XT = EA(nc.sbuf_tensor("XT", [128, 1024], F32))
        SCR = EA(nc.sbuf_tensor("SCR", [128, 1024], F32))
        YTI = EA(nc.sbuf_tensor("YTI", [128, 2048], BF16))
        XN = EA(nc.sbuf_tensor("XN", [128, 2048], BF16))
        XNT = EA(nc.sbuf_tensor("XNT", [128, 2048], BF16))
        QF = EA(nc.sbuf_tensor("QF", [128, 2048], BF16))
        KF = EA(nc.sbuf_tensor("KF", [128, 2048], BF16))
        VT = EA(nc.sbuf_tensor("VT", [128, 2048], BF16))
        ALPHA = EA(nc.sbuf_tensor("ALPHA", [128, 2048], F32))
        LACUM = EA(nc.sbuf_tensor("LACUM", [128, 512], F32))
        EXPB = EA(nc.sbuf_tensor("EXPB", [128, 512], BF16))
        SQB = EA(nc.sbuf_tensor("SQB", [128, 512], BF16))
        ATTB = EA(nc.sbuf_tensor("ATTB", [128, 2048], BF16))
        OTOK = EA(nc.sbuf_tensor("OTOK", [128, 1024], BF16))
        OT = EA(nc.sbuf_tensor("OT", [128, 1024], BF16))
        STATS = EA(nc.sbuf_tensor("STATS", [128, 32], F32))

        def a_init_dve(e):
            e.memset(Y2[:, :], 0.0)
            e.memset(LAL[:, :], 0.0)
            e.memset(S[:, :], 0.0)
            e.memset(Sb[:, :], 0.0)
            inc("dve", e.memset(PVEC[:, :], 1.0))
        vec_body(a_init_dve)

        DIN_WBASE = cnt["din"]
        ACT_WBASE = cnt["act"]
        DW0 = cnt["dw"]
        DW20 = cnt["dw2"]
        DVE_INIT = cnt["dve"]
        Y2F = Y2.bitcast(F32)
        WSLOT = [Y2F[:, 2048 + k * 1024: 3072 + k * 1024] for k in range(6)]

        def wload(e):
            for ap, sb in [(maskneg, MSK), (sel16f, SELF), (sel1b, SELB), (bsel, BSEL)]:
                inc("din", e.dma_start(out=sb[:, :], in_=ap), 16)
            w(e, "din")
            e.wait_ge(sems["dve"], DVE_INIT)
            inc("din", e.dma_start(out=WSLOT[0][0:1, 0:1024], in_=b2.rearrange("(o d) -> o d", o=1)), 16)
            w(e, "din")
            i = 0
            for ap in [Wq, Wk, Wg, Wv, Wo]:
                for kc in range(8):
                    if i % 2 == 0:
                        e.wait_ge(sems["act"], ACT_WBASE + 1 + max(0, i - 5))
                        inc("dw", e.dma_start(out=WSLOT[i % 6][:, 0:1024],
                                              in_=ap[kc * 128:(kc + 1) * 128, :]), 16)
                    i += 1
        blk.sync(wload)

        def wload2(e):
            e.wait_ge(sems["dve"], DVE_INIT)
            e.wait_ge(sems["din"], DIN_WBASE + 5 * 16)
            i = 0
            for ap in [Wq, Wk, Wg, Wv, Wo]:
                for kc in range(8):
                    if i % 2 == 1:
                        e.wait_ge(sems["act"], ACT_WBASE + 1 + max(0, i - 5))
                        inc("dw2", e.dma_start(out=WSLOT[i % 6][:, 0:1024],
                                               in_=ap[kc * 128:(kc + 1) * 128, :]), 16)
                    i += 1
        blk.gpsimd(wload2)

        def wcast(e):
            e.wait_ge(sems["din"], DIN_WBASE + 5 * 16)
            inc("act", e.activation(B2ROWB[:, :], WSLOT[0][0:1, 0:1024], AF.Copy))
            i = 0
            for sb in [WQ, WK, WGs, WV, WOs]:
                for kc in range(8):
                    j = i // 2
                    if i % 2 == 0:
                        e.wait_ge(sems["dw"], DW0 + min(j + 3, 20) * 16)
                    else:
                        e.wait_ge(sems["dw2"], DW20 + min(j + 3, 20) * 16)
                    inc("act", e.activation(sb[:, kc * 1024:(kc + 1) * 1024],
                                            WSLOT[i % 6][:, 0:1024], AF.Copy))
                    i += 1
        act_body(wcast)
        ACT_WEND = cnt["act"]

        PREV_SQ = [0]
        for t in range(NT):
            for sh in range(2):
                DVE_PREV = cnt["dve"]

                def a_x(e, t=t, sh=sh, DVE_PREV=DVE_PREV, psq=PREV_SQ[0]):
                    w(e, "dve", DVE_PREV)
                    if psq:
                        e.wait_ge(sems["act"], psq)
                    inc("din", e.dma_start(out=XT[:, 0:1024],
                                           in_=x_in[t * 256 + sh * 128: t * 256 + sh * 128 + 128, :]), 16)
                    w(e, "din")
                blk.sync(a_x)

                def a_sq(e, t=t, sh=sh):
                    w(e, "din")
                    c0 = sh * 16
                    inc("act", e.activation(SCR[:, 0:1024], XT[:, 0:1024], AF.Square,
                                            accum_out=STATS[:, c0 + 1:c0 + 2]))
                act_body(a_sq)
                ACT_SQ0 = cnt["act"]
                PREV_SQ[0] = ACT_SQ0

                def a_ln0(e, t=t, sh=sh, ACT_SQ0=ACT_SQ0):
                    w(e, "din")
                    c0 = sh * 16
                    inc("dve", e.reduce_sum(out=STATS[:, c0:c0 + 1], in_=XT[:, 0:1024], axis=AX.X))
                    w(e, "dve")
                    inc("dve", e.tensor_scalar(out=STATS[:, c0 + 2:c0 + 3], in0=STATS[:, c0:c0 + 1],
                                               scalar1=1.0 / D, scalar2=None, op0=ALU.mult))
                    w(e, "dve")
                    inc("dve", e.tensor_tensor(out=STATS[:, c0 + 3:c0 + 4], in0=STATS[:, c0 + 2:c0 + 3],
                                               in1=STATS[:, c0 + 2:c0 + 3], op=ALU.mult))
                    e.wait_ge(sems["act"], ACT_SQ0)
                    w(e, "dve")
                    inc("dve", e.scalar_tensor_tensor(out=STATS[:, c0 + 4:c0 + 5],
                                                      in0=STATS[:, c0 + 1:c0 + 2],
                                                      scalar=1.0 / D, in1=STATS[:, c0 + 3:c0 + 4],
                                                      op0=ALU.mult, op1=ALU.subtract))
                vec_body(a_ln0)
                DVE_VAR0 = cnt["dve"]

                def a_sd(e, t=t, sh=sh, DVE_VAR0=DVE_VAR0):
                    w(e, "dve", DVE_VAR0)
                    c0 = sh * 16
                    inc("act", e.activation(STATS[:, c0 + 5:c0 + 6], STATS[:, c0 + 4:c0 + 5],
                                            AF.Sqrt, bias=EPSC[:, 0:1]))
                act_body(a_sd)
                ACT_SD0 = cnt["act"]

                def a_y(e, t=t, sh=sh, ACT_SD0=ACT_SD0):
                    w(e, "act", ACT_SD0)
                    c0 = sh * 16
                    inc("dve", e.reciprocal(out=STATS[:, c0 + 6:c0 + 7], in_=STATS[:, c0 + 5:c0 + 6]))
                    w(e, "dve")
                    inc("dve", e.tensor_scalar(out=YTI[:, sh * 1024:(sh + 1) * 1024],
                                               in0=XT[:, 0:1024],
                                               scalar1=STATS[:, c0 + 2:c0 + 3],
                                               scalar2=STATS[:, c0 + 6:c0 + 7],
                                               op0=ALU.subtract, op1=ALU.mult))
                vec_body(a_y)
                DVE_Y = cnt["dve"]

                def a_sq1(e, t=t, sh=sh, DVE_Y=DVE_Y):
                    w(e, "dve", DVE_Y)
                    c0 = sh * 16 + 8
                    inc("act", e.activation(SCR[:, 0:1024], YTI[:, sh * 1024:(sh + 1) * 1024],
                                            AF.Square, accum_out=STATS[:, c0 + 1:c0 + 2]))
                act_body(a_sq1)
                ACT_SQ1 = cnt["act"]

                def a_ln1(e, t=t, sh=sh, DVE_Y=DVE_Y, ACT_SQ1=ACT_SQ1):
                    w(e, "dve", DVE_Y)
                    c0 = sh * 16 + 8
                    inc("dve", e.reduce_sum(out=STATS[:, c0:c0 + 1],
                                            in_=YTI[:, sh * 1024:(sh + 1) * 1024], axis=AX.X))
                    w(e, "dve")
                    inc("dve", e.tensor_scalar(out=STATS[:, c0 + 2:c0 + 3], in0=STATS[:, c0:c0 + 1],
                                               scalar1=1.0 / D, scalar2=None, op0=ALU.mult))
                    w(e, "dve")
                    inc("dve", e.tensor_tensor(out=STATS[:, c0 + 3:c0 + 4], in0=STATS[:, c0 + 2:c0 + 3],
                                               in1=STATS[:, c0 + 2:c0 + 3], op=ALU.mult))
                    e.wait_ge(sems["act"], ACT_SQ1)
                    w(e, "dve")
                    inc("dve", e.scalar_tensor_tensor(out=STATS[:, c0 + 4:c0 + 5],
                                                      in0=STATS[:, c0 + 1:c0 + 2],
                                                      scalar=1.0 / D, in1=STATS[:, c0 + 3:c0 + 4],
                                                      op0=ALU.mult, op1=ALU.subtract))
                vec_body(a_ln1)
                DVE_VAR1 = cnt["dve"]

                def a_sd1(e, t=t, sh=sh, DVE_VAR1=DVE_VAR1):
                    w(e, "dve", DVE_VAR1)
                    c0 = sh * 16 + 8
                    inc("act", e.activation(STATS[:, c0 + 5:c0 + 6], STATS[:, c0 + 4:c0 + 5],
                                            AF.Sqrt, bias=EPSC[:, 0:1]))
                act_body(a_sd1)
                ACT_SD1 = cnt["act"]

                def a_xn(e, t=t, sh=sh, ACT_SD1=ACT_SD1):
                    w(e, "act", ACT_SD1)
                    c0 = sh * 16 + 8
                    inc("dve", e.reciprocal(out=STATS[:, c0 + 6:c0 + 7], in_=STATS[:, c0 + 5:c0 + 6]))
                    w(e, "dve")
                    inc("dve", e.tensor_scalar(out=XN[:, sh * 1024:(sh + 1) * 1024],
                                               in0=YTI[:, sh * 1024:(sh + 1) * 1024],
                                               scalar1=STATS[:, c0 + 2:c0 + 3],
                                               scalar2=STATS[:, c0 + 6:c0 + 7],
                                               op0=ALU.subtract, op1=ALU.mult))
                vec_body(a_xn)
            DVE_XN = cnt["dve"]

            PE_T0 = cnt["pe"]
            DVE_T0 = cnt["dve"]

            def a_xnt_pe(e, t=t):
                w(e, "dve", DVE_XN)
                for i in range(16):
                    sh, fc = i // 8, i % 8
                    if i >= 4:
                        e.wait_ge(sems["dve"], DVE_T0 + i - 3)
                    inc("pe", e.transpose(ptrbs[i % 4],
                                          XN[:, sh * 1024 + fc * 128: sh * 1024 + fc * 128 + 128], IDB[:, :]))
            blk.tensor(a_xnt_pe)

            def a_xnt_dve(e, t=t):
                for i in range(16):
                    sh, fc = i // 8, i % 8
                    e.wait_ge(sems["pe"], PE_T0 + i + 1)
                    inc("dve", e.tensor_scalar(out=XNT[:, fc * 256 + sh * 128: fc * 256 + sh * 128 + 128],
                                               in0=ptrbs[i % 4],
                                               scalar1=PRM[:, 8 + fc:9 + fc], scalar2=PRM[:, 16 + fc:17 + fc],
                                               op0=ALU.mult, op1=ALU.add))
            vec_body(a_xnt_dve)
            DVE_XNT = cnt["dve"]

            PE_PROJ0 = cnt["pe"]
            ACT_PQ0 = cnt["act"]
            DVE_PK0 = cnt["dve"]

            def a_proj_pe(e, t=t):
                w(e, "dve", DVE_XNT)
                if t == 0:
                    e.wait_ge(sems["act"], ACT_WBASE + 9)
                for oc in range(8):
                    if oc >= 2:
                        e.wait_ge(sems["act"], ACT_PQ0 + oc - 1)
                    for kc in range(8):
                        mm = e.matmul(pAs[oc % 2][:, :],
                                      WQ[:, kc * 1024 + oc * 128: kc * 1024 + (oc + 1) * 128],
                                      XNT[:, kc * 256:(kc + 1) * 256], start=(kc == 0), stop=(kc == 7))
                    inc("pe", mm)
                if t == 0:
                    e.wait_ge(sems["act"], ACT_WBASE + 17)
                for oc in range(8):
                    if oc >= 2:
                        e.wait_ge(sems["dve"], DVE_PK0 + oc - 1)
                    for kc in range(8):
                        mm = e.matmul(pBs[oc % 2][:, :],
                                      WK[:, kc * 1024 + oc * 128: kc * 1024 + (oc + 1) * 128],
                                      XNT[:, kc * 256:(kc + 1) * 256], start=(kc == 0), stop=(kc == 7))
                    inc("pe", mm)
                if t == 0:
                    e.wait_ge(sems["act"], ACT_WBASE + 25)
                for oc in range(8):
                    e.wait_ge(sems["act"], ACT_PQ0 + 7 + oc)
                    for kc in range(8):
                        mm = e.matmul(pAs[oc % 2][:, :],
                                      WGs[:, kc * 1024 + oc * 128: kc * 1024 + (oc + 1) * 128],
                                      XNT[:, kc * 256:(kc + 1) * 256], start=(kc == 0), stop=(kc == 7))
                    inc("pe", mm)
                if t == 0:
                    e.wait_ge(sems["act"], ACT_WBASE + 33)
                for sh in range(2):
                    if sh == 1:
                        e.wait_ge(sems["dve"], DVE_PK0 + 9)
                    for nh in range(2):
                        for kc in range(8):
                            mm = e.matmul(pCs[nh][:, :],
                                          XNT[:, kc * 256 + sh * 128: kc * 256 + sh * 128 + 128],
                                          WV[:, kc * 1024 + nh * 512: kc * 1024 + (nh + 1) * 512],
                                          start=(kc == 0), stop=(kc == 7))
                    inc("pe", mm)
            blk.tensor(a_proj_pe)

            def a_projq_act(e, t=t):
                for oc in range(8):
                    e.wait_ge(sems["pe"], PE_PROJ0 + oc + 1)
                    inc("act", e.activation(QF[:, oc * 256:(oc + 1) * 256], pAs[oc % 2][:, :], AF.Copy))
                for oc in range(8):
                    e.wait_ge(sems["pe"], PE_PROJ0 + 16 + oc + 1)
                    inc("act", e.activation(ALPHA[:, oc * 256:(oc + 1) * 256], pAs[oc % 2][:, :],
                                            AF.Sigmoid, bias=PRM[:, 0 + oc:1 + oc]))
            act_body(a_projq_act)
            ACT_G = cnt["act"]

            def a_projk_dve(e, t=t):
                for oc in range(8):
                    e.wait_ge(sems["pe"], PE_PROJ0 + 8 + oc + 1)
                    inc("dve", e.tensor_copy(out=KF[:, oc * 256:(oc + 1) * 256], in_=pBs[oc % 2][:, :]))
                for sh in range(2):
                    e.wait_ge(sems["pe"], PE_PROJ0 + 24 + sh + 1)
                    e.tensor_copy(out=VT[:, sh * 1024: sh * 1024 + 512], in_=pC0[:, :])
                    inc("dve", e.tensor_copy(out=VT[:, sh * 1024 + 512:(sh + 1) * 1024], in_=pC1[:, :]))
            vec_body(a_projk_dve)
            DVE_V = cnt["dve"]

            DVE_CL0 = cnt["dve"]
            ACT_LN0 = cnt["act"]

            def a_clamp_dve(e, t=t, ACT_LN0=ACT_LN0):
                w(e, "act", ACT_G)
                for oc in range(8):
                    if oc >= 2:
                        e.wait_ge(sems["act"], ACT_LN0 + oc - 1)
                    inc("dve", e.tensor_scalar(out=SCR[:, (oc % 2) * 256:(oc % 2) * 256 + 256],
                                               in0=ALPHA[:, oc * 256:(oc + 1) * 256],
                                               scalar1=1e-6, scalar2=None, op0=ALU.max))
            vec_body(a_clamp_dve)

            def a_ln_act(e, t=t, DVE_CL0=DVE_CL0):
                for oc in range(8):
                    e.wait_ge(sems["dve"], DVE_CL0 + oc + 1)
                    inc("act", e.activation(ALPHA[:, oc * 256:(oc + 1) * 256],
                                            SCR[:, (oc % 2) * 256:(oc % 2) * 256 + 256], AF.Ln))
            act_body(a_ln_act)
            ACT_LA = cnt["act"]
            LA = ALPHA

            dseq = []
            for fc in range(8):
                dseq.append(("scan", fc))
                if fc >= 1:
                    dseq.append(("qdec", fc - 1))
                    dseq.append(("dte", fc - 1))
                if fc >= 2:
                    dseq.append(("kdt", fc - 2))
            dseq += [("qdec", 7), ("dte", 7), ("kdt", 6), ("kdt", 7)]
            aseq = [("eL", 0), ("eL", 1)]
            for fc in range(8):
                if fc >= 2:
                    aseq.append(("eL", fc))
                aseq.append(("eD", fc))
            DVE_SC0 = cnt["dve"]
            ACT_SC0 = cnt["act"]
            dcnt = {}
            acnt = {}
            _c = DVE_SC0
            for it in dseq:
                _c += 2 if it[0] == "scan" else 1
                dcnt[it] = _c
            for j, it in enumerate(aseq):
                acnt[it] = ACT_SC0 + j + 1
            ACT_LA_t = ACT_LA

            def a_scanpipe_dve(e, t=t, dcnt=dcnt, acnt=acnt, ACT_LA_t=ACT_LA_t):
                w(e, "act", ACT_LA_t)
                w(e, "dout")
                for op, fc in dseq:
                    sl = (fc % 2) * 256
                    if op == "scan":
                        if fc >= 2:
                            w(e, "dve", dcnt[("dte", fc - 2)])
                        last = None
                        for c in range(4):
                            last = e.tensor_tensor_scan(out=LACUM[:, sl + c * 64: sl + (c + 1) * 64],
                                                        data0=LA[:, fc * 256 + c * 64: fc * 256 + (c + 1) * 64],
                                                        data1=LA[:, fc * 256 + c * 64: fc * 256 + (c + 1) * 64],
                                                        initial=0.0, op0=ALU.add, op1=ALU.bypass)
                        inc("dve", last)
                        w(e, "dve")
                        inc("dve", e.tensor_copy(
                            out=LAL[:, fc * 32 + t * 4: fc * 32 + t * 4 + 4],
                            in_=LACUM[:, sl:sl + 256].rearrange("p (c z) -> p c z", z=64)[:, :, 63:64]))
                    elif op == "qdec":
                        e.wait_ge(sems["act"], acnt[("eL", fc)])
                        inc("dve", e.tensor_tensor(out=QDECT[:, fc * 256:(fc + 1) * 256],
                                                   in0=QF[:, fc * 256:(fc + 1) * 256],
                                                   in1=EXPB[:, sl:sl + 256], op=ALU.mult))
                    elif op == "dte":
                        if fc >= 2:
                            e.wait_ge(sems["act"], acnt[("eD", fc - 2)])
                        last = None
                        for c in range(4):
                            last = e.tensor_scalar(out=SCR[:, sl + c * 64: sl + (c + 1) * 64],
                                                   in0=LACUM[:, sl + c * 64: sl + (c + 1) * 64],
                                                   scalar1=LAL[:, fc * 32 + t * 4 + c: fc * 32 + t * 4 + c + 1],
                                                   scalar2=None, op0=ALU.subtract)
                        inc("dve", last)
                    elif op == "kdt":
                        e.wait_ge(sems["act"], acnt[("eD", fc)])
                        inc("dve", e.tensor_tensor(out=XN[:, fc * 256:(fc + 1) * 256],
                                                   in0=KF[:, fc * 256:(fc + 1) * 256],
                                                   in1=EXPB[:, sl:sl + 256], op=ALU.mult))
            vec_body(a_scanpipe_dve)

            def a_scanpipe_act(e, t=t, dcnt=dcnt):
                for op, fc in aseq:
                    sl = (fc % 2) * 256
                    if op == "eL":
                        e.wait_ge(sems["dve"], dcnt[("scan", fc)])
                        if fc >= 2:
                            e.wait_ge(sems["dve"], dcnt[("kdt", fc - 2)])
                        inc("act", e.activation(EXPB[:, sl:sl + 256], LACUM[:, sl:sl + 256], AF.Exp))
                    else:
                        e.wait_ge(sems["dve"], dcnt[("dte", fc)])
                        inc("act", e.activation(EXPB[:, sl:sl + 256], SCR[:, sl:sl + 256],
                                                AF.Exp, scale=-1.0))
            act_body(a_scanpipe_act)
            DVE_SCAN = dcnt[("scan", 7)]
            DVE_KDT = dcnt[("kdt", 7)]
            KDT = XN
            DVE_QD7 = dcnt[("qdec", 7)]

            def a_qspill(e, t=t, DVE_QD7=DVE_QD7):
                e.wait_ge(sems["dve"], DVE_QD7)
                inc("dout", e.dma_start(out=qdd[:, :, t * 256:(t + 1) * 256].rearrange("f p c -> p f c"),
                                        in_=QDECT[:, :].rearrange("p (f c) -> p f c", c=256)), 16)
                w(e, "dout")
            blk.gpsimd(a_qspill)
            DOUT_QS = cnt["dout"]

            def a_cd_act(e, t=t):
                w(e, "dve", DVE_SCAN)
                inc("act", e.activation(
                    CDA[:, :].rearrange("p (f c) -> p f c", c=32)[:, :, t * 4:(t + 1) * 4],
                    LAL[:, :].rearrange("p (f c) -> p f c", c=32)[:, :, t * 4:(t + 1) * 4], AF.Exp))
            act_body(a_cd_act)

            PE_ML0 = cnt["pe"]
            ACT_NSQ0 = cnt["act"]

            def a_sqn_act(e, t=t, PE_ML0=PE_ML0):
                w(e, "dve", DVE_V)
                for fc in range(8):
                    if fc >= 2:
                        e.wait_ge(sems["pe"], PE_ML0 + 1 + fc - 1)
                    inc("act", e.activation(SQB[:, (fc % 2) * 256:(fc % 2) * 256 + 256],
                                            QF[:, fc * 256:(fc + 1) * 256], AF.Square))
            act_body(a_sqn_act)

            def a_mlog_pe(e, t=t, ACT_NSQ0=ACT_NSQ0):
                w(e, "dve", DVE_SCAN)
                for fc in range(8):
                    mm = e.matmul(pA0[0:16, 0:256], SELF[:, fc * 16:(fc + 1) * 16],
                                  LA[:, fc * 256:(fc + 1) * 256], start=(fc == 0), stop=(fc == 7))
                inc("pe", mm)
                for fc in range(8):
                    e.wait_ge(sems["act"], ACT_NSQ0 + fc + 1)
                    inc("pe", e.matmul(pB0[0:16, 0:256], SELB[:, fc * 16:(fc + 1) * 16],
                                       SQB[:, (fc % 2) * 256:(fc % 2) * 256 + 256],
                                       start=(fc == 0), stop=(fc == 7)))
            blk.tensor(a_mlog_pe)
            PE_ML = cnt["pe"]

            def a_mlog_dve(e, t=t, PE_ML=PE_ML):
                e.wait_ge(sems["pe"], PE_ML - 8)
                inc("dve", e.tensor_copy(out=SMALL[:, 0:256], in_=pA0[0:16, 0:256]))
                w(e, "dve")
                for c in range(4):
                    e.tensor_tensor_scan(out=SMALL[:, 256 + c * 64:256 + (c + 1) * 64],
                                         data0=SMALL[:, c * 64:(c + 1) * 64],
                                         data1=SMALL[:, c * 64:(c + 1) * 64],
                                         initial=0.0, op0=ALU.add, op1=ALU.bypass)
                e.wait_ge(sems["pe"], PE_ML)
                inc("dve", e.tensor_copy(out=SMALL[:, 512:768], in_=pB0[0:16, 0:256]))
            vec_body(a_mlog_dve)
            DVE_QSS = cnt["dve"]
            ACT_KSQ0 = cnt["act"]
            PE_KSS0 = cnt["pe"]

            def a_sqk_act(e, t=t, PE_ML=PE_ML, PE_KSS0=PE_KSS0):
                w(e, "pe", PE_ML)
                for fc in range(8):
                    if fc >= 2:
                        e.wait_ge(sems["pe"], PE_KSS0 + fc - 1)
                    inc("act", e.activation(SQB[:, (fc % 2) * 256:(fc % 2) * 256 + 256],
                                            KF[:, fc * 256:(fc + 1) * 256], AF.Square))
            act_body(a_sqk_act)

            def a_kss_pe(e, t=t, ACT_KSQ0=ACT_KSQ0, DVE_QSS=DVE_QSS):
                w(e, "dve", DVE_QSS)
                for fc in range(8):
                    e.wait_ge(sems["act"], ACT_KSQ0 + fc + 1)
                    inc("pe", e.matmul(pB0[0:16, 0:256], SELB[:, fc * 16:(fc + 1) * 16],
                                       SQB[:, (fc % 2) * 256:(fc % 2) * 256 + 256],
                                       start=(fc == 0), stop=(fc == 7)))
            blk.tensor(a_kss_pe)
            PE_KSS = cnt["pe"]

            def a_kss_dve(e, t=t, PE_KSS=PE_KSS):
                w(e, "pe", PE_KSS)
                inc("dve", e.tensor_copy(out=SMALL[:, 768:1024], in_=pB0[0:16, 0:256]))
            vec_body(a_kss_dve)
            DVE_KSS = cnt["dve"]

            def a_norm_act(e, t=t):
                w(e, "dve", DVE_KSS)
                inc("act", e.activation(SMALL[:, 0:256], SMALL[:, 512:768], AF.Sqrt))
                w(e, "act")
                inc("act", e.activation(SMALL[:, 512:768], SMALL[:, 768:1024], AF.Sqrt))
            act_body(a_norm_act)
            ACT_SQRT = cnt["act"]

            def a_norm_dve(e, t=t):
                w(e, "act", ACT_SQRT)
                inc("dve", e.tensor_scalar(out=SMALL[:, 0:256], in0=SMALL[:, 0:256],
                                           scalar1=1e-12, scalar2=None, op0=ALU.max))
                inc("dve", e.tensor_scalar(out=SMALL[:, 512:768], in0=SMALL[:, 512:768],
                                           scalar1=1e-12, scalar2=None, op0=ALU.max))
                w(e, "dve")
                e.reciprocal(out=SMALL[:, 1024:1280], in_=SMALL[:, 0:256])
                inc("dve", e.reciprocal(out=SMALL[:, 1280:1536], in_=SMALL[:, 512:768]))
            vec_body(a_norm_dve)
            DVE_RQ = cnt["dve"]

            def a_lnr_act(e, t=t):
                w(e, "dve", DVE_RQ)
                inc("act", e.activation(SMALL[:, 512:768], SMALL[:, 1280:1536], AF.Ln))
            act_body(a_lnr_act)
            ACT_LNR = cnt["act"]

            def a_lmq_dve(e, t=t):
                w(e, "act", ACT_LNR)
                inc("dve", e.tensor_tensor(out=SMALL[:, 1792:2048], in0=SMALL[:, 256:512],
                                           in1=SMALL[:, 512:768], op=ALU.subtract))
            vec_body(a_lmq_dve)
            DVE_LMQ = cnt["dve"]

            PE_TT0 = cnt["pe"]
            DVE_TT0 = cnt["dve"]

            def a_tok_pe(e, t=t):
                w(e, "dve", DVE_LMQ)
                for sh in range(2):
                    for j, off in enumerate([1792, 1024, 1280]):
                        k = sh * 3 + j
                        if k >= 1:
                            e.wait_ge(sems["dve"], DVE_TT0 + k)
                        inc("pe", e.transpose(ptrf[:, 0:16],
                                              SMALL[0:16, off + sh * 128: off + sh * 128 + 128],
                                              IDF[0:16, 0:16]))
            blk.tensor(a_tok_pe)

            def a_tok_dve(e, t=t):
                for sh in range(2):
                    s = 2 * t + sh
                    outs = [SCT[:, sh * 16:(sh + 1) * 16], RNQ[:, s * 16:(s + 1) * 16],
                            RKT[:, sh * 16:(sh + 1) * 16]]
                    for j in range(3):
                        k = sh * 3 + j
                        e.wait_ge(sems["pe"], PE_TT0 + k + 1)
                        inc("dve", e.tensor_copy(out=outs[j], in_=ptrf[:, 0:16]))
            vec_body(a_tok_dve)
            DVE_TOK = cnt["dve"]

            PE_KT0 = cnt["pe"]
            DVE_KT0 = cnt["dve"]

            DVE_PRE_KTR = cnt["dve"]

            def a_ktr_pe(e, t=t, DVE_PRE_KTR=DVE_PRE_KTR):
                w(e, "dve", DVE_PRE_KTR)
                for i in range(16):
                    sh, fc = i // 8, i % 8
                    if i >= 4:
                        e.wait_ge(sems["dve"], DVE_KT0 + 2 * (i - 3))
                    inc("pe", e.transpose(ptrbs[i % 4],
                                          KDT[:, fc * 256 + sh * 128: fc * 256 + sh * 128 + 128], IDB[:, :]))
            blk.tensor(a_ktr_pe)

            def a_ktr_dve(e, t=t):
                for i in range(16):
                    sh, fc = i // 8, i % 8
                    e.wait_ge(sems["pe"], PE_KT0 + i + 1)
                    for hh in range(2):
                        inc("dve", e.tensor_scalar(
                            out=XNT[:, sh * 1024 + fc * 128 + hh * 64: sh * 1024 + fc * 128 + hh * 64 + 64],
                            in0=ptrbs[i % 4][:, hh * 64: hh * 64 + 64],
                            scalar1=RKT[:, sh * 16 + 2 * fc + hh: sh * 16 + 2 * fc + hh + 1],
                            scalar2=None, op0=ALU.mult))
            vec_body(a_ktr_dve)
            KTOK = XNT
            DVE_KTOK = cnt["dve"]

            for sh in range(2):
                s = 2 * t + sh
                PE_S0 = cnt["pe"]
                DVE_A0 = cnt["dve"]
                ACT_D0 = cnt["act"]
                stt_cnt = lambda i, b=DVE_A0: b + (1 if i == 0 else 2 * i)
                att_cnt = lambda i, b=DVE_A0: b + (2 * i + 3 if i < 15 else 32)
                exp_cnt = lambda i, b=ACT_D0: b + i + 1

                DVE_NOW = cnt["dve"]

                def g_sb_pe(e, t=t, sh=sh, s=s, DVE_NOW=DVE_NOW, att_cnt=att_cnt):
                    w(e, "dve", DVE_NOW)
                    for i in range(16):
                        hp, hh = i // 2, i % 2
                        if i >= 2:
                            e.wait_ge(sems["dve"], att_cnt(i - 2))
                        hr = slice(hh * 64, hh * 64 + 64)
                        col = hp * 256 + sh * 128
                        e.matmul(pBs[i % 2][:, 0:128],
                                 KF[hr, col:col + 128], QF[hr, col:col + 128], start=True, stop=True)
                        inc("pe", e.matmul(pAs[i % 2][:, 0:128],
                                           BSEL[:, i * 128:(i + 1) * 128],
                                           SMALL[0:16, 256 + sh * 128:256 + sh * 128 + 128],
                                           start=True, stop=True))
                blk.tensor(g_sb_pe)
                PE_SB = cnt["pe"]

                def g_da_dve(e, t=t, sh=sh, s=s, PE_S0=PE_S0, ACT_D0=ACT_D0, exp_cnt=exp_cnt):
                    w(e, "dout")
                    def emit_stt(i):
                        e.wait_ge(sems["pe"], PE_S0 + i + 1)
                        if i >= 2:
                            e.wait_ge(sems["act"], exp_cnt(i - 2))
                        inc("dve", e.scalar_tensor_tensor(
                            out=SCR[:, (i % 2) * 128:(i % 2) * 128 + 128],
                            in0=pAs[i % 2][:, 0:128],
                            scalar=SCT[:, sh * 16 + i: sh * 16 + i + 1],
                            in1=MSK[:, :], op0=ALU.subtract, op1=ALU.add))

                    def emit_att(i):
                        e.wait_ge(sems["act"], exp_cnt(i))
                        inc("dve", e.tensor_tensor(
                            out=ATTB[:, i * 128:(i + 1) * 128],
                            in0=pBs[i % 2][:, 0:128],
                            in1=SQB[:, (i % 4) * 128:(i % 4) * 128 + 128], op=ALU.mult))
                    emit_stt(0)
                    emit_stt(1)
                    for i in range(2, 16):
                        emit_att(i - 2)
                        emit_stt(i)
                    emit_att(14)
                    emit_att(15)
                vec_body(g_da_dve)

                def g_exp_act(e, t=t, sh=sh, s=s, stt_cnt=stt_cnt):
                    for i in range(16):
                        e.wait_ge(sems["dve"], stt_cnt(i))
                        inc("act", e.activation(SQB[:, (i % 4) * 128:(i % 4) * 128 + 128],
                                                SCR[:, (i % 2) * 128:(i % 2) * 128 + 128], AF.Exp))
                act_body(g_exp_act)

                PE_KV0 = cnt["pe"]

                def g_kv_pe(e, t=t, sh=sh, s=s):
                    w(e, "dve")
                    e.matmul(pD[:, :], ONESB[:, :], ZROW[:, :], start=True, stop=False)
                    e.matmul(pD2[:, :], ONESB[:, :], ZROW[:, :], start=True, stop=False)
                    for h in range(16):
                        hp, hh = h // 2, h % 2
                        hr = slice(hh * 64, hh * 64 + 64)
                        mm = e.matmul(pD[hr, hp * 64:(hp + 1) * 64],
                                      KTOK[0:64, sh * 1024 + hp * 128 + hh * 64: sh * 1024 + hp * 128 + hh * 64 + 64],
                                      VT[0:64, sh * 1024 + h * 64: sh * 1024 + (h + 1) * 64],
                                      start=False, stop=False)
                    inc("pe", mm)
                    w(e, "pe")
                    for h in range(16):
                        hp, hh = h // 2, h % 2
                        hr = slice(hh * 64, hh * 64 + 64)
                        e.matmul(pD2[hr, hp * 64:(hp + 1) * 64],
                                 KTOK[64:128, sh * 1024 + hp * 128 + hh * 64: sh * 1024 + hp * 128 + hh * 64 + 64],
                                 VT[64:128, sh * 1024 + h * 64: sh * 1024 + (h + 1) * 64],
                                 start=False, stop=False)
                    e.matmul(pD[:, :], ONESB[:, :], ZROW[:, :], start=False, stop=True)
                    mm = e.matmul(pD2[:, :], ONESB[:, :], ZROW[:, :], start=False, stop=True)
                    inc("pe", mm)
                blk.tensor(g_kv_pe)
                PE_KV1 = cnt["pe"]

                def g_sup_dve(e, t=t, sh=sh, s=s, PE_KV1=PE_KV1):
                    w(e, "pe", PE_KV1)
                    inc("dve", e.tensor_copy(out=SbO[:, :], in_=Sb[:, :]))
                    ci = t * 4 + sh * 2 + 0
                    last = None
                    for hp in range(8):
                        last = e.scalar_tensor_tensor(out=S[:, hp * 64:(hp + 1) * 64],
                                                      in0=S[:, hp * 64:(hp + 1) * 64],
                                                      scalar=CDA[:, hp * 32 + ci: hp * 32 + ci + 1],
                                                      in1=pD[:, hp * 64:(hp + 1) * 64],
                                                      op0=ALU.mult, op1=ALU.add)
                    inc("dve", last)
                    w(e, "dve")
                    inc("dve", e.tensor_copy(out=SbM[:, :], in_=S[:, :]))
                    w(e, "dve")
                    ci2 = t * 4 + sh * 2 + 1
                    last = None
                    for hp in range(8):
                        last = e.scalar_tensor_tensor(out=S[:, hp * 64:(hp + 1) * 64],
                                                      in0=S[:, hp * 64:(hp + 1) * 64],
                                                      scalar=CDA[:, hp * 32 + ci2: hp * 32 + ci2 + 1],
                                                      in1=pD2[:, hp * 64:(hp + 1) * 64],
                                                      op0=ALU.mult, op1=ALU.add)
                    inc("dve", last)
                    w(e, "dve")
                    inc("dve", e.tensor_copy(out=Sb[:, :], in_=S[:, :]))
                blk.vector(g_sup_dve)
                DVE_SUP2 = cnt["dve"]

                def g_o_pe(e, t=t, sh=sh, s=s, att_cnt=att_cnt, DVE_SUP2=DVE_SUP2):
                    w(e, "dve")
                    for h in range(16):
                        e.wait_ge(sems["dve"], att_cnt(h))
                        hp, hh = h // 2, h % 2
                        hr = slice(hh * 64, hh * 64 + 64)
                        pc = pCs[h // 8]
                        col = slice((h % 8) * 64, (h % 8) * 64 + 64)
                        e.matmul(pc[:, col], ATTB[:, h * 128:(h + 1) * 128],
                                 VT[:, sh * 1024 + h * 64: sh * 1024 + (h + 1) * 64],
                                 start=(h % 8 == 0), stop=False)
                        e.matmul(pc[0:64, col],
                                 QDECT[hr, hp * 256 + sh * 128: hp * 256 + sh * 128 + 64],
                                 SbO[hr, hp * 64:(hp + 1) * 64], start=False, stop=False)
                        e.matmul(pc[64:128, col],
                                 QDECT[hr, hp * 256 + sh * 128 + 64: hp * 256 + sh * 128 + 128],
                                 SbM[hr, hp * 64:(hp + 1) * 64], start=False, stop=False)
                        if h % 8 == 7:
                            mm = e.matmul(pc[:, :], ONESB[:, :], ZROW[:, :], start=False, stop=True)
                    inc("pe", mm)
                blk.tensor(g_o_pe)
                PE_O2 = cnt["pe"]

                def g_oe_dve(e, t=t, sh=sh, s=s, PE_O2=PE_O2):
                    w(e, "pe", PE_O2)
                    last = None
                    for h in range(16):
                        last = e.tensor_scalar(out=OTOK[:, h * 64:(h + 1) * 64],
                                               in0=pCs[h // 8][:, (h % 8) * 64:(h % 8) * 64 + 64],
                                               scalar1=RNQ[:, s * 16 + h: s * 16 + h + 1],
                                               scalar2=None, op0=ALU.mult)
                    inc("dve", last)
                blk.vector(g_oe_dve)
                DVE_OE = cnt["dve"]


                PE_OT0 = cnt["pe"]
                DVE_OT0 = cnt["dve"]

                def g_ot_pe(e, t=t, sh=sh, s=s, DVE_OE=DVE_OE, DVE_OT0=DVE_OT0):
                    w(e, "dve", DVE_OE)
                    for fc in range(8):
                        if fc >= 4:
                            e.wait_ge(sems["dve"], DVE_OT0 + fc - 3)
                        inc("pe", e.transpose(ptrbs[fc % 4],
                                              OTOK[:, fc * 128:(fc + 1) * 128], IDB[:, :]))
                blk.tensor(g_ot_pe)

                def g_ot_dve(e, t=t, sh=sh, s=s, PE_OT0=PE_OT0):
                    for fc in range(8):
                        e.wait_ge(sems["pe"], PE_OT0 + fc + 1)
                        inc("dve", e.tensor_copy(out=OT[:, fc * 128:(fc + 1) * 128],
                                                 in_=ptrbs[fc % 4]))
                vec_body(g_ot_dve)
                DVE_OTD = cnt["dve"]

                def g_wo_pe(e, t=t, sh=sh, s=s, DVE_OTD=DVE_OTD):
                    w(e, "dve", DVE_OTD)
                    if t == 0 and sh == 0:
                        e.wait_ge(sems["act"], ACT_WBASE + 41)
                    for nh in range(2):
                        for kc in range(8):
                            mm = e.matmul(pCs[nh][:, :], OT[:, kc * 128:(kc + 1) * 128],
                                          WOs[:, kc * 1024 + nh * 512: kc * 1024 + (nh + 1) * 512],
                                          start=(kc == 0), stop=(kc == 7))
                    inc("pe", mm)
                blk.tensor(g_wo_pe)
                PE_WO = cnt["pe"]

                def g_y2_dve(e, t=t, sh=sh, s=s, PE_WO=PE_WO):
                    w(e, "pe", PE_WO)
                    if s == 4:
                        w(e, "act", ACT_WEND)
                    e.tensor_tensor(out=Y2[:, s * 1024: s * 1024 + 512],
                                    in0=pC0[:, :], in1=YTI[:, sh * 1024: sh * 1024 + 512], op=ALU.add)
                    inc("dve", e.tensor_tensor(out=Y2[:, s * 1024 + 512:(s + 1) * 1024],
                                               in0=pC1[:, :], in1=YTI[:, sh * 1024 + 512:(sh + 1) * 1024],
                                               op=ALU.add))
                vec_body(g_y2_dve)

        global_DOUT_PUB = [0]
        _dox = DO_X
        def x_acore_dve(e):
            w(e, "act")
            last = None
            for hp in range(8):
                last = e.reduce_sum(out=ABUF[:, hp:hp + 1], in_=LAL[:, hp * 32:(hp + 1) * 32], axis=AX.X)
            inc("dve", last)
        vec_body(x_acore_dve)
        DVE_AC = cnt["dve"]

        def x_acore_act(e):
            w(e, "dve", DVE_AC)
            inc("act", e.activation(ABUF[:, :], ABUF[:, :], AF.Exp))
        if _dox: act_body(x_acore_act)
        ACT_AC = cnt["act"]

        DVE_X0 = cnt["dve"]

        def x_pub_gps(e):
            w(e, "act", ACT_AC)
            w(e, "dve")
            inc("dout", e.dma_start(out=cc_src[:, 0:512], in_=S[:, :]), 16)
            inc("dout", e.dma_start(out=cc_src[:, 512:520], in_=ABUF[:, :]), 16)
            e.wait_ge(sems["dout"], cnt["dout"])
            global_DOUT_PUB[0] = cnt["dout"]
            if USE_CC:
                inc("cc", e.collective_compute("AllGather", ALU.bypass,
                                               replica_groups=[list(range(NCORE))],
                                               ins=[cc_src], outs=[cc_dst]), 1)
                e.wait_ge(sems["cc"], cnt["cc"])
            else:
                for r in range(NCORE):
                    inc("cc", e.dma_start(out=cc_dst[r * 128:(r + 1) * 128, :], in_=SGX[:, 0:520]), 16)
                e.wait_ge(sems["cc"], cnt["cc"])
            for j, r in enumerate(range(NCORE - 1, -1, -1)):
                if j >= 2:
                    e.wait_ge(sems["dve"], DVE_X0 + 5 * (j - 1))
                inc("dout", e.dma_start(out=SGX[:, (j % 2) * 520:(j % 2) * 520 + 520],
                                        in_=cc_dst[r * 128:(r + 1) * 128, :]), 16)
                w(e, "dout")
        if _dox: blk.gpsimd(x_pub_gps)
        else: global_DOUT_PUB[0] = cnt['dout']

        DOUT_PUB = global_DOUT_PUB[0]

        def x_sin_dve(e):
            e.wait_ge(sems["dout"], DOUT_PUB)
            e.memset(S[:, :], 0.0)
            for j, r in enumerate(range(NCORE - 1, -1, -1)):
                sgr = SGX[:, (j % 2) * 520:(j % 2) * 520 + 520]
                e.wait_ge(sems["dout"], DOUT_PUB + (j + 1) * 16)
                w(e, "dve")
                inc("dve", e.tensor_scalar(out=PM[:, :], in0=PVEC[:, :], scalar1=M1S[:, r:r + 1],
                                           scalar2=None, op0=ALU.mult))
                w(e, "dve")
                last = None
                for hp in range(8):
                    last = e.scalar_tensor_tensor(out=S[:, hp * 64:(hp + 1) * 64],
                                                  in0=sgr[:, hp * 64:(hp + 1) * 64],
                                                  scalar=PM[:, hp:hp + 1],
                                                  in1=S[:, hp * 64:(hp + 1) * 64],
                                                  op0=ALU.mult, op1=ALU.add)
                inc("dve", last)
                w(e, "dve")
                inc("dve", e.tensor_scalar(out=PM[:, :], in0=sgr[:, 512:520],
                                           scalar1=1.0, scalar2=None, op0=ALU.subtract))
                w(e, "dve")
                inc("dve", e.tensor_scalar(out=PM[:, :], in0=PM[:, :], scalar1=M1S[:, r:r + 1],
                                           scalar2=1.0, op0=ALU.mult, op1=ALU.add))
                w(e, "dve")
                inc("dve", e.tensor_tensor(out=PVEC[:, :], in0=PVEC[:, :], in1=PM[:, :], op=ALU.mult))
            w(e, "dve")
            inc("dve", e.tensor_copy(out=Sb[:, :], in_=S[:, :]))
        if _dox: vec_body(x_sin_dve)

        BSTEP = DO_B if isinstance(DO_B, int) and not isinstance(DO_B, bool) else (99 if DO_B else 0)
        for t in (range(NT) if BSTEP > 0 else []):
            PE_QL = cnt["pe"]

            def b_qload(e, t=t, PE_QL=PE_QL):
                w(e, "pe", PE_QL)
                inc("dout", e.dma_start(out=QDECT[:, :].rearrange("p (f c) -> p f c", c=256),
                                        in_=qdd[:, :, t * 256:(t + 1) * 256].rearrange("f p c -> p f c")), 16)
                w(e, "dout")
            if BSTEP >= 1: blk.gpsimd(b_qload)
            DOUT_QL = cnt["dout"]

            for sh in range(2):
                s_ = 2 * t + sh

                DVE_PB0 = cnt["dve"]

                def b_sup_dve(e, t=t, sh=sh, s=s_):
                    w(e, "dve")
                    inc("dve", e.tensor_copy(out=SbO[:, :], in_=Sb[:, :]))
                    w(e, "dve")
                    ci = s * 2
                    last = None
                    for hp in range(8):
                        last = e.tensor_scalar(out=S[:, hp * 64:(hp + 1) * 64], in0=S[:, hp * 64:(hp + 1) * 64],
                                               scalar1=CDA[:, hp * 32 + ci: hp * 32 + ci + 1],
                                               scalar2=None, op0=ALU.mult)
                    inc("dve", last)
                    w(e, "dve")
                    inc("dve", e.tensor_copy(out=SbM[:, :], in_=S[:, :]))
                    w(e, "dve")
                    ci2 = s * 2 + 1
                    last = None
                    for hp in range(8):
                        last = e.tensor_scalar(out=S[:, hp * 64:(hp + 1) * 64], in0=S[:, hp * 64:(hp + 1) * 64],
                                               scalar1=CDA[:, hp * 32 + ci2: hp * 32 + ci2 + 1],
                                               scalar2=None, op0=ALU.mult)
                    inc("dve", last)
                    w(e, "dve")
                    inc("dve", e.tensor_copy(out=Sb[:, :], in_=S[:, :]))
                blk.vector(b_sup_dve)
                DVE_PB1 = cnt["dve"]

                def b_corr_pe(e, t=t, sh=sh, s=s_, DOUT_QL=DOUT_QL, DVE_PB1=DVE_PB1):
                    w(e, "dve", DVE_PB1)
                    e.wait_ge(sems["dout"], DOUT_QL)
                    mm = e.matmul(pC0[:, :], ONESB[:, :], ZROW[:, :], start=True, stop=False)
                    mm = e.matmul(pC1[:, :], ONESB[:, :], ZROW[:, :], start=True, stop=False)
                    for hh_run in range(2):
                        inc("pe", mm)
                        w(e, "pe")
                        for h in range(hh_run, 16, 2):
                            hp = h // 2
                            hr = slice(hh_run * 64, hh_run * 64 + 64)
                            pc = pCs[h // 8]
                            col = slice((h % 8) * 64, (h % 8) * 64 + 64)
                            e.matmul(pc[0:64, col],
                                     QDECT[hr, hp * 256 + sh * 128: hp * 256 + sh * 128 + 64],
                                     SbO[hr, hp * 64:(hp + 1) * 64], start=False, stop=False)
                            mm = e.matmul(pc[64:128, col],
                                          QDECT[hr, hp * 256 + sh * 128 + 64: hp * 256 + sh * 128 + 128],
                                          SbM[hr, hp * 64:(hp + 1) * 64], start=False, stop=False)
                    inc("pe", mm)
                    w(e, "pe")
                    e.matmul(pC0[:, :], ONESB[:, :], ZROW[:, :], start=False, stop=True)
                    mm = e.matmul(pC1[:, :], ONESB[:, :], ZROW[:, :], start=False, stop=True)
                    inc("pe", mm)
                blk.tensor(b_corr_pe)
                PE_CB2 = cnt["pe"]

                def b_su1_dve(e, s=s_, PE_CB2=PE_CB2):
                    w(e, "pe", PE_CB2)
                    last = None
                    for h in range(16):
                        last = e.tensor_scalar(out=OTOK[:, h * 64:(h + 1) * 64],
                                               in0=pCs[h // 8][:, (h % 8) * 64:(h % 8) * 64 + 64],
                                               scalar1=RNQ[:, s * 16 + h: s * 16 + h + 1],
                                               scalar2=None, op0=ALU.mult)
                    inc("dve", last)
                vec_body(b_su1_dve)
                DVE_BE = cnt["dve"]

                PE_BT0 = cnt["pe"]
                DVE_BT0 = cnt["dve"]

                def b_ot_pe(e, s=s_, DVE_BE=DVE_BE, DVE_BT0=DVE_BT0):
                    w(e, "dve", DVE_BE)
                    for fc in range(8):
                        if fc >= 4:
                            e.wait_ge(sems["dve"], DVE_BT0 + fc - 3)
                        inc("pe", e.transpose(ptrbs[fc % 4],
                                              OTOK[:, fc * 128:(fc + 1) * 128], IDB[:, :]))
                if BSTEP >= 6: blk.tensor(b_ot_pe)

                def b_ot_dve(e, s=s_, PE_BT0=PE_BT0):
                    for fc in range(8):
                        e.wait_ge(sems["pe"], PE_BT0 + fc + 1)
                        inc("dve", e.tensor_copy(out=OT[:, fc * 128:(fc + 1) * 128],
                                                 in_=ptrbs[fc % 4]))
                if BSTEP >= 6: vec_body(b_ot_dve)
                DVE_BTD = cnt["dve"]

                def b_wo_pe(e, s=s_, DVE_BTD=DVE_BTD):
                    w(e, "dve", DVE_BTD)
                    for nh in range(2):
                        for kc in range(8):
                            mm = e.matmul(pCs[nh][:, :], OT[:, kc * 128:(kc + 1) * 128],
                                          WOs[:, kc * 1024 + nh * 512: kc * 1024 + (nh + 1) * 512],
                                          start=(kc == 0), stop=(kc == 7))
                    inc("pe", mm)
                if BSTEP >= 7: blk.tensor(b_wo_pe)
                PE_BWO = cnt["pe"]

                def b_y2_dve(e, s=s_, PE_BWO=PE_BWO):
                    w(e, "pe", PE_BWO)
                    e.tensor_tensor(out=Y2[:, s * 1024: s * 1024 + 512],
                                    in0=pC0[:, :], in1=Y2[:, s * 1024: s * 1024 + 512], op=ALU.add)
                    inc("dve", e.tensor_tensor(out=Y2[:, s * 1024 + 512:(s + 1) * 1024],
                                               in0=pC1[:, :], in1=Y2[:, s * 1024 + 512:(s + 1) * 1024],
                                               op=ALU.add))
                if BSTEP >= 7: vec_body(b_y2_dve)

        def y2_spill(e):
            w(e, "dve")
            inc("dout", e.dma_start(out=y2d.rearrange("(s p) d -> p s d", p=128),
                                    in_=Y2[:, :].rearrange("p (s d) -> p s d", d=1024)), 16)
            w(e, "dout")
        blk.gpsimd(y2_spill)
        DOUT_Y2 = cnt["dout"]

        def bar_pe(e):
            w(e, "dve")
            inc("pe", e.nop())
        blk.tensor(bar_pe)
        PE_BAR = cnt["pe"]

        def bar_act(e):
            w(e, "pe", PE_BAR)
            inc("act", e.nop())
        act_body(bar_act)
        ACT_BAR = cnt["act"]
        stA.close()

        stC = ExitStack()
        EC = stC.enter_context
        W1S = EC(nc.sbuf_tensor("W1S", [128, 8 * FFN], BF16))
        W2S = EC(nc.sbuf_tensor("W2S", [128, 32 * 1024], BF16))
        KQ8S = EC(nc.sbuf_tensor("KQ8S", [128, 8 * 1024], FP8))
        Y2T = EC(nc.sbuf_tensor("Y2T", [128, 2048], BF16))
        XTc = EC(nc.sbuf_tensor("XTc", [128, 2048], F32))
        XY = EC(nc.sbuf_tensor("XYc", [128, 2048], F32))
        X1F = EC(nc.sbuf_tensor("X1F", [128, 2048], F32))
        HB = EC(nc.sbuf_tensor("HB", [128, 2048], BF16))
        HT = EC(nc.sbuf_tensor("HT", [128, 2048], BF16))
        G1 = EC(nc.sbuf_tensor("G1", [128, 32 * 256], BF16))
        ZQ8 = EC(nc.sbuf_tensor("ZQ8", [128, 2048], FP8))
        YBF = EC(nc.sbuf_tensor("YBF", [128, 1024], BF16))
        YBF2 = EC(nc.sbuf_tensor("YBF2", [128, 1024], BF16))
        STATSc = EC(nc.sbuf_tensor("STATSc", [128, 32], F32))
        WSTc = XTc
        WSTc2 = XY
        SCRc = XTc
        ZB = HB
        ZT = HT
        OUTB = XTc

        DIN_CW = cnt["din"]
        ACT_CW = cnt["act"]
        DWC0 = cnt["dw"]
        DWC1 = cnt["dw2"]
        DWC2 = cnt["dw3"]
        CSLOT = [XTc[:, 0:1024], XTc[:, 1024:2048], XY[:, 0:1024], XY[:, 1024:2048],
                 X1F[:, 0:1024], X1F[:, 1024:2048]]
        FKSLOT = [HB[:, 0:1024], HB[:, 1024:2048],
                  X1F.bitcast(BF16)[:, 0:1024], X1F.bitcast(BF16)[:, 1024:2048]]
        wsrcs = []
        for kc in range(8):
            for pp in range(4):
                wsrcs.append((W1[kc * 128:(kc + 1) * 128, pp * 1024:(pp + 1) * 1024],
                              W1S[:, kc * FFN + pp * 1024: kc * FFN + (pp + 1) * 1024]))
        for kc in range(32):
            wsrcs.append((W2[kc * 128:(kc + 1) * 128, :], W2S[:, kc * 1024:(kc + 1) * 1024]))
        NWC = len(wsrcs)

        NS0 = len(range(0, NWC, 3))
        NS1 = len(range(1, NWC, 3))
        NS2 = len(range(2, NWC, 3))

        def c_wload(e):
            w(e, "act", ACT_BAR)
            w(e, "dout")
            for i in range(0, NWC, 3):
                e.wait_ge(sems["act"], ACT_CW + max(0, i - 5))
                inc("dw", e.dma_start(out=CSLOT[i % 6], in_=wsrcs[i][0]), 16)
            for kc in range(8):
                e.wait_ge(sems["act"], ACT_CW + NWC + max(0, kc - 3))
                inc("dw", e.dma_start(out=FKSLOT[kc % 4],
                                      in_=fk[kc * 128:(kc + 1) * 128, :]), 16)
        blk.sync(c_wload)

        def c_wload2(e):
            w(e, "act", ACT_BAR)
            w(e, "dout")
            for i in range(1, NWC, 3):
                e.wait_ge(sems["act"], ACT_CW + max(0, i - 5))
                inc("dw2", e.dma_start(out=CSLOT[i % 6], in_=wsrcs[i][0]), 16)
        blk.gpsimd(c_wload2)



        DOUT_SPILLS = cnt["dout"]

        def c_wcast(e):
            w(e, "dout")
            for i3 in (2, 5):
                e.wait_ge(sems["act"], ACT_CW)
                e.wait_ge(sems["dout"], DOUT_SPILLS)
                inc("dw3", e.dma_start(out=CSLOT[i3 % 6], in_=wsrcs[i3][0]), 16)
            for i in range(NWC):
                j = i // 3
                q = i % 3
                if q == 0:
                    e.wait_ge(sems["dw"], DWC0 + min(j + 2, NS0) * 16)
                elif q == 1:
                    e.wait_ge(sems["dw2"], DWC1 + min(j + 2, NS1) * 16)
                else:
                    e.wait_ge(sems["dw3"], DWC2 + min(j + 2, NS2) * 16)
                inc("act", e.activation(wsrcs[i][1], CSLOT[i % 6], AF.Copy))
                i6 = i + 6
                if i6 < NWC and i6 % 3 == 2:
                    e.wait_ge(sems["act"], ACT_CW + i6 - 5)
                    inc("dw3", e.dma_start(out=CSLOT[i6 % 6], in_=wsrcs[i6][0]), 16)
            for kc in range(8):
                e.wait_ge(sems["dw"], DWC0 + (NS0 + min(kc + 4, 8)) * 16)
                inc("act", e.activation(KQ8S[:, kc * 1024:(kc + 1) * 1024],
                                        FKSLOT[kc % 4],
                                        AF.Copy, scale=RSK[:, 0:1]))
        act_body(c_wcast)
        ACT_CWE = cnt["act"]

        for t in (range(NT) if DO_C else []):
            def c_x(e, t=t):
                w(e, "dve")
                w(e, "act")
                w(e, "dout")
                if t == 0:
                    e.wait_ge(sems["dout"], DOUT_Y2)
                for sh in range(2):
                    inc("din", e.dma_start(out=XTc[:, sh * 1024:(sh + 1) * 1024],
                                           in_=x_in[t * 256 + sh * 128: t * 256 + sh * 128 + 128, :]), 16)
                    inc("din", e.dma_start(out=Y2T[:, sh * 1024:(sh + 1) * 1024],
                                           in_=y2d[t * 256 + sh * 128: t * 256 + sh * 128 + 128, :]), 16)
                w(e, "din")
            blk.sync(c_x)

            def c_xy_dve(e, t=t):
                w(e, "din")
                w(e, "act", ACT_CWE)
                for sh in range(2):
                    inc("dve", e.tensor_tensor(out=XY[:, sh * 1024:(sh + 1) * 1024],
                                               in0=XTc[:, sh * 1024:(sh + 1) * 1024],
                                               in1=Y2T[:, sh * 1024:(sh + 1) * 1024], op=ALU.add))
            vec_body(c_xy_dve)

            def c_sq2(e, t=t):
                w(e, "act", ACT_CWE)
                w(e, "dve")
                for sh in range(2):
                    c0 = sh * 16
                    inc("act", e.activation(SCRc[:, sh * 1024:(sh + 1) * 1024],
                                            Y2T[:, sh * 1024:(sh + 1) * 1024], AF.Square,
                                            accum_out=STATSc[:, c0 + 1:c0 + 2]))
            act_body(c_sq2)
            ACT_S2 = cnt["act"]

            def c_ln2(e, t=t, ACT_S2=ACT_S2):
                for sh in range(2):
                    c0 = sh * 16
                    inc("dve", e.reduce_sum(out=STATSc[:, c0:c0 + 1],
                                            in_=Y2T[:, sh * 1024:(sh + 1) * 1024], axis=AX.X))
                    w(e, "dve")
                    inc("dve", e.tensor_scalar(out=STATSc[:, c0 + 2:c0 + 3], in0=STATSc[:, c0:c0 + 1],
                                               scalar1=1.0 / D, scalar2=None, op0=ALU.mult))
                    w(e, "dve")
                    inc("dve", e.tensor_tensor(out=STATSc[:, c0 + 3:c0 + 4], in0=STATSc[:, c0 + 2:c0 + 3],
                                               in1=STATSc[:, c0 + 2:c0 + 3], op=ALU.mult))
                    e.wait_ge(sems["act"], ACT_S2 - 1 + sh)
                    w(e, "dve")
                    inc("dve", e.scalar_tensor_tensor(out=STATSc[:, c0 + 4:c0 + 5], in0=STATSc[:, c0 + 1:c0 + 2],
                                                      scalar=1.0 / D, in1=STATSc[:, c0 + 3:c0 + 4],
                                                      op0=ALU.mult, op1=ALU.subtract))
            vec_body(c_ln2)
            DVE_V2 = cnt["dve"]

            def c_sd2(e, t=t):
                w(e, "dve", DVE_V2)
                for sh in range(2):
                    c0 = sh * 16
                    inc("act", e.activation(STATSc[:, c0 + 5:c0 + 6], STATSc[:, c0 + 4:c0 + 5],
                                            AF.Sqrt, bias=EPSC[:, 0:1]))
            act_body(c_sd2)
            ACT_SD2 = cnt["act"]

            def c_h(e, t=t):
                w(e, "act", ACT_SD2)
                e.reciprocal(out=STATSc[:, 6:7], in_=STATSc[:, 5:6])
                inc("dve", e.reciprocal(out=STATSc[:, 22:23], in_=STATSc[:, 21:22]))
                w(e, "dve")
                for sh in range(2):
                    c0 = sh * 16
                    inc("dve", e.tensor_scalar(out=HB[:, sh * 1024:(sh + 1) * 1024],
                                               in0=Y2T[:, sh * 1024:(sh + 1) * 1024],
                                               scalar1=STATSc[:, c0 + 2:c0 + 3],
                                               scalar2=STATSc[:, c0 + 6:c0 + 7],
                                               op0=ALU.subtract, op1=ALU.mult))
            vec_body(c_h)
            DVE_H = cnt["dve"]

            PE_HT0 = cnt["pe"]
            DVE_HT0 = cnt["dve"]

            def c_ht_pe(e, t=t, DVE_H=DVE_H, DVE_HT0=DVE_HT0):
                w(e, "dve", DVE_H)
                for i in range(16):
                    sh, fc = i // 8, i % 8
                    if i >= 4:
                        e.wait_ge(sems["dve"], DVE_HT0 + i - 3)
                    inc("pe", e.transpose(ptrbs[i % 4],
                                          HB[:, sh * 1024 + fc * 128: sh * 1024 + fc * 128 + 128], IDB[:, :]))
            blk.tensor(c_ht_pe)

            def c_ht_dve(e, t=t, PE_HT0=PE_HT0):
                for i in range(16):
                    sh, fc = i // 8, i % 8
                    e.wait_ge(sems["pe"], PE_HT0 + i + 1)
                    inc("dve", e.tensor_scalar(out=HT[:, fc * 256 + sh * 128: fc * 256 + sh * 128 + 128],
                                               in0=ptrbs[i % 4],
                                               scalar1=PRM[:, 24 + fc:25 + fc], scalar2=PRM[:, 32 + fc:33 + fc],
                                               op0=ALU.mult, op1=ALU.add))
            vec_body(c_ht_dve)
            DVE_HT = cnt["dve"]

            PE_W10 = cnt["pe"]
            ACT_G10 = cnt["act"]

            def c_w1_pe(e, t=t, DVE_HT=DVE_HT, ACT_G10=ACT_G10):
                w(e, "dve", DVE_HT)
                for oc in range(32):
                    if oc >= 2:
                        e.wait_ge(sems["act"], ACT_G10 + oc - 1)
                    for kc in range(8):
                        mm = e.matmul(pAs[oc % 2][:, :],
                                      W1S[:, kc * FFN + oc * 128: kc * FFN + (oc + 1) * 128],
                                      HT[:, kc * 256:(kc + 1) * 256], start=(kc == 0), stop=(kc == 7))
                    inc("pe", mm)
            blk.tensor(c_w1_pe)

            def c_gelu_act(e, t=t, PE_W10=PE_W10):
                for oc in range(32):
                    e.wait_ge(sems["pe"], PE_W10 + oc + 1)
                    inc("act", e.activation(G1[:, oc * 256:(oc + 1) * 256], pAs[oc % 2][:, :],
                                            AF.Gelu, bias=B1C[:, oc:oc + 1]))
            act_body(c_gelu_act)
            ACT_G1E = cnt["act"]

            for th in range(2):
                def c_w2_pe(e, t=t, th=th, ACT_G1E=ACT_G1E):
                    w(e, "act", ACT_G1E)
                    if th == 1:
                        w(e, "dve")
                    for oc in range(32):
                        for nh in range(2):
                            e.matmul(pCs[nh][:, :],
                                     G1[:, oc * 256 + th * 128: oc * 256 + th * 128 + 128],
                                     W2S[:, oc * 1024 + nh * 512: oc * 1024 + (nh + 1) * 512],
                                     start=(oc == 0), stop=False)
                    for nh in range(2):
                        mm = e.matmul(pCs[nh][:, :], ONESB[:, :],
                                      B2ROWB[:, nh * 512:(nh + 1) * 512], start=False, stop=True)
                    inc("pe", mm)
                blk.tensor(c_w2_pe)
                PE_W2 = cnt["pe"]

                def c_x1f_dve(e, t=t, th=th, PE_W2=PE_W2):
                    w(e, "pe", PE_W2)
                    e.scalar_tensor_tensor(out=X1F[:, th * 1024: th * 1024 + 512],
                                           in0=pC0[:, :], scalar=0.0,
                                           in1=XY[:, th * 1024: th * 1024 + 512],
                                           op0=ALU.bypass, op1=ALU.add)
                    inc("dve", e.scalar_tensor_tensor(out=X1F[:, th * 1024 + 512:(th + 1) * 1024],
                                                      in0=pC1[:, :], scalar=0.0,
                                                      in1=XY[:, th * 1024 + 512:(th + 1) * 1024],
                                                      op0=ALU.bypass, op1=ALU.add))
                vec_body(c_x1f_dve)

            DVE_X1F = cnt["dve"]

            def c_sq3(e, t=t, DVE_X1F=DVE_X1F):
                w(e, "dve", DVE_X1F)
                for sh in range(2):
                    c0 = sh * 16
                    inc("act", e.activation(SCRc[:, sh * 1024:(sh + 1) * 1024],
                                            X1F[:, sh * 1024:(sh + 1) * 1024], AF.Square,
                                            accum_out=STATSc[:, c0 + 1:c0 + 2]))
            act_body(c_sq3)
            ACT_S3 = cnt["act"]

            def c_ln3(e, t=t, DVE_X1F=DVE_X1F, ACT_S3=ACT_S3):
                w(e, "dve", DVE_X1F)
                for sh in range(2):
                    c0 = sh * 16
                    inc("dve", e.reduce_sum(out=STATSc[:, c0:c0 + 1],
                                            in_=X1F[:, sh * 1024:(sh + 1) * 1024], axis=AX.X))
                    w(e, "dve")
                    inc("dve", e.tensor_scalar(out=STATSc[:, c0 + 2:c0 + 3], in0=STATSc[:, c0:c0 + 1],
                                               scalar1=1.0 / D, scalar2=None, op0=ALU.mult))
                    w(e, "dve")
                    inc("dve", e.tensor_tensor(out=STATSc[:, c0 + 3:c0 + 4], in0=STATSc[:, c0 + 2:c0 + 3],
                                               in1=STATSc[:, c0 + 2:c0 + 3], op=ALU.mult))
                    e.wait_ge(sems["act"], ACT_S3 - 1 + sh)
                    w(e, "dve")
                    inc("dve", e.scalar_tensor_tensor(out=STATSc[:, c0 + 4:c0 + 5], in0=STATSc[:, c0 + 1:c0 + 2],
                                                      scalar=1.0 / D, in1=STATSc[:, c0 + 3:c0 + 4],
                                                      op0=ALU.mult, op1=ALU.subtract))
            vec_body(c_ln3)
            DVE_V3 = cnt["dve"]

            def c_sd3(e, t=t, DVE_V3=DVE_V3):
                w(e, "dve", DVE_V3)
                for sh in range(2):
                    c0 = sh * 16
                    inc("act", e.activation(STATSc[:, c0 + 5:c0 + 6], STATSc[:, c0 + 4:c0 + 5],
                                            AF.Sqrt, bias=EPSC[:, 0:1]))
            act_body(c_sd3)
            ACT_SD3 = cnt["act"]

            def c_z(e, t=t, ACT_SD3=ACT_SD3):
                w(e, "act", ACT_SD3)
                e.reciprocal(out=STATSc[:, 6:7], in_=STATSc[:, 5:6])
                inc("dve", e.reciprocal(out=STATSc[:, 22:23], in_=STATSc[:, 21:22]))
                w(e, "dve")
                for sh in range(2):
                    c0 = sh * 16
                    inc("dve", e.tensor_scalar(out=ZB[:, sh * 1024:(sh + 1) * 1024],
                                               in0=X1F[:, sh * 1024:(sh + 1) * 1024],
                                               scalar1=STATSc[:, c0 + 2:c0 + 3],
                                               scalar2=STATSc[:, c0 + 6:c0 + 7],
                                               op0=ALU.subtract, op1=ALU.mult))
            vec_body(c_z)
            DVE_Z = cnt["dve"]

            PE_ZT0 = cnt["pe"]
            DVE_ZT0 = cnt["dve"]

            def c_zt_pe(e, t=t, DVE_Z=DVE_Z, DVE_ZT0=DVE_ZT0):
                w(e, "dve", DVE_Z)
                for i in range(16):
                    sh, fc = i // 8, i % 8
                    if i >= 4:
                        e.wait_ge(sems["dve"], DVE_ZT0 + i - 3)
                    inc("pe", e.transpose(ptrbs[i % 4],
                                          ZB[:, sh * 1024 + fc * 128: sh * 1024 + fc * 128 + 128], IDB[:, :]))
            blk.tensor(c_zt_pe)

            def c_zt_dve(e, t=t, PE_ZT0=PE_ZT0):
                for i in range(16):
                    sh, fc = i // 8, i % 8
                    e.wait_ge(sems["pe"], PE_ZT0 + i + 1)
                    inc("dve", e.tensor_copy(out=ZT[:, fc * 256 + sh * 128: fc * 256 + sh * 128 + 128],
                                             in_=ptrbs[i % 4]))
            vec_body(c_zt_dve)
            DVE_ZT = cnt["dve"]

            def c_zq_act(e, t=t, DVE_ZT=DVE_ZT):
                w(e, "dve", DVE_ZT)
                for fc in range(8):
                    inc("act", e.activation(ZQ8[:, fc * 256:(fc + 1) * 256],
                                            ZT[:, fc * 256:(fc + 1) * 256],
                                            AF.Copy, scale=RSIN[:, 0:1]))
            act_body(c_zq_act)
            ACT_ZQ = cnt["act"]

            for th in range(2):
                def c_fk_pe(e, t=t, th=th):
                    w(e, "act")
                    w(e, "dve")
                    for nh in range(2):
                        for kc in range(8):
                            mm = e.matmul(pCs[nh][:, :],
                                          ZQ8[:, kc * 256 + th * 128: kc * 256 + th * 128 + 128],
                                          KQ8S[:, kc * 1024 + nh * 512: kc * 1024 + (nh + 1) * 512],
                                          start=(kc == 0), stop=(kc == 7))
                    inc("pe", mm)
                blk.tensor(c_fk_pe)
                PE_FK = cnt["pe"]

                def c_yb_act(e, t=t, th=th, PE_FK=PE_FK):
                    w(e, "pe", PE_FK)
                    e.activation(YBF[:, 0:512], pC0[:, :], AF.Copy, scale=SPROD[:, 0:1])
                    inc("act", e.activation(YBF[:, 512:1024], pC1[:, :], AF.Copy, scale=SPROD[:, 0:1]))
                act_body(c_yb_act)
                ACT_YB = cnt["act"]

                def c_out_dve(e, t=t, th=th, ACT_YB=ACT_YB):
                    w(e, "act", ACT_YB)
                    w(e, "dout")
                    inc("dve", e.tensor_tensor(out=YBF2[:, :], in0=YBF[:, :], in1=BIASB[:, :], op=ALU.add))
                    w(e, "dve")
                    inc("dve", e.tensor_tensor(out=OUTB[:, th * 1024:(th + 1) * 1024],
                                               in0=X1F[:, th * 1024:(th + 1) * 1024],
                                               in1=YBF2[:, :], op=ALU.add))
                vec_body(c_out_dve)
                DVE_OUT = cnt["dve"]

                def c_store(e, t=t, th=th, DVE_OUT=DVE_OUT):
                    w(e, "dve", DVE_OUT)
                    inc("dout", e.dma_start(out=out[t * 256 + th * 128: t * 256 + th * 128 + 128, :],
                                            in_=OUTB[:, th * 1024:(th + 1) * 1024]), 16)
                    w(e, "dout")
                blk.gpsimd(c_store)

        def fin(e):
            w(e, "dout")
        blk.gpsimd(fin)
        stC.close()
    return nc


def _consts():
    identf = np.eye(128, dtype=np.float32)
    identb = identf.astype(ml_dtypes.bfloat16)
    mrow = np.arange(128)[:, None]
    ccol = np.arange(128)[None, :]
    valid = (mrow // 64 == ccol // 64) & (mrow <= ccol)
    maskneg = np.where(valid, 0.0, NEG).astype(np.float32)
    sel16f = np.zeros((128, 128), np.float32)
    sel1b_ = np.zeros((128, 128), np.float32)
    for fc in range(8):
        sel16f[0:64, fc * 16 + 2 * fc] = 1.0 / 64
        sel16f[64:128, fc * 16 + 2 * fc + 1] = 1.0 / 64
        sel1b_[0:64, fc * 16 + 2 * fc] = 1.0
        sel1b_[64:128, fc * 16 + 2 * fc + 1] = 1.0
    sel1b = sel1b_.astype(ml_dtypes.bfloat16)
    bsel = np.zeros((16, 16 * 128), np.float32)
    for h in range(16):
        bsel[h, h * 128:(h + 1) * 128] = 1.0
    onesf = np.ones((1, 128), np.float32)
    onesb = np.ones((1, 128), ml_dtypes.bfloat16)
    return dict(identf=identf, identb=identb, maskneg=maskneg, sel16f=sel16f,
                sel1b=sel1b, bsel=bsel, onesf=onesf, onesb=onesb)


_NC_CACHE = {}


def kernel(L0_attn_x, L0_attn_Wq, L0_attn_Wk, L0_attn_Wv, L0_attn_Wg, L0_attn_bg,
           L0_attn_Wo, L0_attn_ln1_w, L0_attn_ln1_b, L0_attn_ln2_w, L0_attn_ln2_b,
           L0_attn_W1, L0_attn_b1, L0_attn_W2, L0_attn_b2,
           L0_ffn_kernel, L0_ffn_input_scale, L0_ffn_kernel_scale, L0_ffn_bias):
    if "nc" not in _NC_CACHE:
        _NC_CACHE["nc"] = build_nc()
    nc = _NC_CACHE["nc"]
    consts = _consts()
    x = np.ascontiguousarray(np.asarray(L0_attn_x).reshape(16384, 1024))
    shared = dict(
        Wq=np.ascontiguousarray(L0_attn_Wq), Wk=np.ascontiguousarray(L0_attn_Wk),
        Wv=np.ascontiguousarray(L0_attn_Wv), Wg=np.ascontiguousarray(L0_attn_Wg),
        bg=np.ascontiguousarray(L0_attn_bg), Wo=np.ascontiguousarray(L0_attn_Wo),
        ln1w=np.ascontiguousarray(L0_attn_ln1_w), ln1b=np.ascontiguousarray(L0_attn_ln1_b),
        ln2w=np.ascontiguousarray(L0_attn_ln2_w), ln2b=np.ascontiguousarray(L0_attn_ln2_b),
        W1=np.ascontiguousarray(L0_attn_W1), b1=np.ascontiguousarray(L0_attn_b1),
        W2=np.ascontiguousarray(L0_attn_W2), b2=np.ascontiguousarray(L0_attn_b2),
        fk=np.ascontiguousarray(L0_ffn_kernel),
        fsin=np.ascontiguousarray(L0_ffn_input_scale),
        fsk=np.ascontiguousarray(L0_ffn_kernel_scale),
        fbias=np.ascontiguousarray(L0_ffn_bias),
        **consts,
    )
    in_maps = []
    for c in range(NCORE):
        m1c = np.zeros((128, 8), np.float32)
        m1c[:, :c] = 1.0
        in_maps.append(dict(shared, x=np.ascontiguousarray(x[c * T:(c + 1) * T]), m1=m1c))
    res = run_bass_kernel_spmd(nc, in_maps, core_ids=list(range(NCORE)))
    full = np.concatenate([res.results[c]["out"] for c in range(NCORE)], axis=0)
    return full.reshape(1, 16384, 1024)


# version 11
# speedup vs baseline: 1.0760x; 1.0149x over previous
import numpy as np
import ml_dtypes
import concourse.bass as bass
import concourse.mybir as mybir
from concourse.bass_utils import run_bass_kernel_spmd
from contextlib import ExitStack

F32 = mybir.dt.float32
BF16 = mybir.dt.bfloat16
FP8 = mybir.dt.float8e4
AF = mybir.ActivationFunctionType
ALU = mybir.AluOpType
AX = mybir.AxisListType

NCORE = 8
T = 2048
D = 1024
FFN = 4096
NT = 8
NS = 16
EPS = 1e-5
NEG = -1e30
USE_CC = True
DO_B = True
DO_C = True
CORR_MODE = 0
DO_X = True


def build_nc():
    nc = bass.Bass("TRN2", target_bir_lowering=False, num_devices=NCORE)
    g = lambda n, s, d, k="ExternalInput": nc.dram_tensor(n, s, d, kind=k).ap()
    x_in = g("x", [T, D], F32)
    Wq = g("Wq", [D, D], F32); Wk = g("Wk", [D, D], F32)
    Wv = g("Wv", [D, D], F32); Wg = g("Wg", [D, D], F32)
    bgv = g("bg", [D], F32); Wo = g("Wo", [D, D], F32)
    ln1w = g("ln1w", [D], F32); ln1b = g("ln1b", [D], F32)
    ln2w = g("ln2w", [D], F32); ln2b = g("ln2b", [D], F32)
    W1 = g("W1", [D, FFN], F32); b1 = g("b1", [FFN], F32)
    W2 = g("W2", [FFN, D], F32); b2 = g("b2", [D], F32)
    fk = g("fk", [D, D], BF16)
    fsin = g("fsin", [1], F32); fsk = g("fsk", [1], F32)
    fbias = g("fbias", [D], BF16)
    identf = g("identf", [128, 128], F32); identb = g("identb", [128, 128], BF16)
    maskneg = g("maskneg", [128, 128], F32)
    sel16f = g("sel16f", [128, 128], F32); sel1b = g("sel1b", [128, 128], BF16)
    bsel = g("bsel", [16, 16 * 128], F32)
    onesf = g("onesf", [1, 128], F32); onesb = g("onesb", [1, 128], BF16)
    m1 = g("m1", [128, 8], F32)
    out = g("out", [T, D], F32, k="ExternalOutput")
    cc_src = g("cc_src", [128, 520], F32, k="Internal")
    cc_dst = g("cc_dst", [NCORE * 128, 520], F32, k="Internal")
    y2d = g("y2d", [T, D], BF16, k="Internal")
    qdd = g("qdd", [8, 128, T], BF16, k="Internal")

    st = ExitStack()
    E = st.enter_context
    with st:
        IDF = E(nc.sbuf_tensor("IDF", [128, 128], F32))
        IDB = E(nc.sbuf_tensor("IDB", [128, 128], BF16))
        ONESF = E(nc.sbuf_tensor("ONESF", [1, 128], F32))
        ONESB = E(nc.sbuf_tensor("ONESB", [1, 128], BF16))
        M1S = E(nc.sbuf_tensor("M1S", [128, 8], F32))
        PRM = E(nc.sbuf_tensor("PRM", [128, 48], F32))
        B1C = E(nc.sbuf_tensor("B1C", [128, 32], F32))
        PRMROW = E(nc.sbuf_tensor("PRMROW", [32, 128], F32))
        B2ROWB = E(nc.sbuf_tensor("B2ROWB", [1, D], BF16))
        FBROW = E(nc.sbuf_tensor("FBROW", [1, D], BF16))
        BIASB = E(nc.sbuf_tensor("BIASB", [128, D], BF16))
        SCAL = E(nc.sbuf_tensor("SCAL", [1, 8], F32))
        ZROW = E(nc.sbuf_tensor("ZROW", [1, 512], BF16))
        EPSC = E(nc.sbuf_tensor("EPSC", [128, 1], F32))
        RSIN = E(nc.sbuf_tensor("RSIN", [128, 1], F32))
        RSK = E(nc.sbuf_tensor("RSK", [128, 1], F32))
        SPROD = E(nc.sbuf_tensor("SPROD", [128, 1], F32))
        pA0 = E(nc.psum_tensor("pA0", [128, 256], F32))
        pA1 = E(nc.psum_tensor("pA1", [128, 256], F32))
        pB0 = E(nc.psum_tensor("pB0", [128, 256], F32))
        pB1 = E(nc.psum_tensor("pB1", [128, 256], F32))
        pC0 = E(nc.psum_tensor("pC0", [128, 512], F32))
        pC1 = E(nc.psum_tensor("pC1", [128, 512], F32))
        pAs = [pA0, pA1]
        ptrf = pA1
        ptrbs = None
        pBs = [pB0, pB1]
        pCs = [pC0, pC1]
        pD = E(nc.psum_tensor("pD", [128, 512], F32))
        pD2 = E(nc.psum_tensor("pD2", [128, 512], F32))
        ptrbs = [pA0.bitcast(BF16)[:, 0:128], pA1.bitcast(BF16)[:, 0:128],
                 pD.bitcast(BF16)[:, 0:128], pD2.bitcast(BF16)[:, 0:128]]
        sems = {}
        for name in ["pe", "dve", "act", "din", "dout", "cc", "dw", "dw2", "dw3"]:
            sems[name] = E(nc.semaphore("sm_" + name))
        cnt = {k: 0 for k in sems}
        blk = E(nc.Block())

        def vec_body(f):
            def g(e):
                w(e, "dve")
                f(e)
            blk.vector(g)

        def act_body(f):
            def g(e):
                w(e, "act")
                f(e)
            blk.scalar(g)

        def inc(name, inst, n=1):
            inst.then_inc(sems[name], n)
            cnt[name] += n
            return cnt[name]

        def w(e, name, v=None):
            e.wait_ge(sems[name], cnt[name] if v is None else v)

        def ph0_sync(e):
            for ap, sb in [(identf, IDF), (identb, IDB), (onesf, ONESF), (onesb, ONESB), (m1, M1S)]:
                inc("din", e.dma_start(out=sb[:, :], in_=ap), 16)
            inc("din", e.dma_start(out=FBROW[:, :], in_=fbias.rearrange("(o d) -> o d", o=1)), 16)
            inc("din", e.dma_start(out=SCAL[0:1, 0:1], in_=fsin.rearrange("(o d) -> o d", o=1)), 16)
            inc("din", e.dma_start(out=SCAL[0:1, 1:2], in_=fsk.rearrange("(o d) -> o d", o=1)), 16)
            w(e, "din")
        blk.sync(ph0_sync)

        for i, ap in enumerate([bgv, ln1w, ln1b, ln2w, ln2b]):
            PE_PV = cnt["pe"]

            def pv_sync(e, ap=ap, i=i, PE_PV=PE_PV):
                w(e, "pe", PE_PV)
                inc("din", e.dma_start(out=PRMROW[0:8, 0:128].rearrange("p (o a) -> p o a", o=1),
                                       in_=ap.rearrange("(a o p) -> a o p", o=1, p=128)), 16)
                w(e, "din")
            blk.sync(pv_sync)
            DVE_PV = cnt["dve"]

            def pv_pe(e, i=i, DVE_PV=DVE_PV):
                w(e, "din")
                w(e, "dve", DVE_PV)
                inc("pe", e.transpose(ptrf[:, 0:8], PRMROW[0:8, 0:128], IDF[0:8, 0:8]))
            blk.tensor(pv_pe)

            def pv_dve(e, i=i):
                w(e, "pe")
                inc("dve", e.tensor_copy(out=PRM[:, i * 8:(i + 1) * 8], in_=ptrf[:, 0:8]))
            vec_body(pv_dve)
        PE_PARAMS = cnt["pe"]

        def b1_sync(e):
            w(e, "pe", PE_PARAMS)
            inc("din", e.dma_start(out=PRMROW[0:32, 0:128].rearrange("p (o a) -> p o a", o=1),
                                   in_=b1.rearrange("(a o p) -> a o p", o=1, p=128)), 16)
            w(e, "din")
        blk.sync(b1_sync)

        def b1_pe(e):
            w(e, "din")
            w(e, "dve")
            inc("pe", e.transpose(ptrf[:, 0:32], PRMROW[0:32, 0:128], IDF[0:32, 0:32]))
        blk.tensor(b1_pe)
        PE_B1 = cnt["pe"]

        def b1_dve(e):
            w(e, "pe", PE_B1)
            inc("dve", e.tensor_copy(out=B1C[:, :], in_=ptrf[:, 0:32]))
        vec_body(b1_dve)

        def sc_dve(e):
            w(e, "din")
            e.memset(EPSC[:, :], EPS)
            e.memset(ZROW[:, :], 0.0)
            e.reciprocal(out=SCAL[0:1, 2:3], in_=SCAL[0:1, 0:1])
            e.reciprocal(out=SCAL[0:1, 3:4], in_=SCAL[0:1, 1:2])
            inc("dve", e.tensor_tensor(out=SCAL[0:1, 4:5], in0=SCAL[0:1, 0:1],
                                       in1=SCAL[0:1, 1:2], op=ALU.mult))
        vec_body(sc_dve)

        def sc_pe(e):
            w(e, "dve")
            e.matmul(pA0[:, 0:1], ONESF[:, :], SCAL[0:1, 2:3], start=True, stop=False)
            e.matmul(pA0[:, 1:2], ONESF[:, :], SCAL[0:1, 3:4], start=False, stop=False)
            e.matmul(pA0[:, 2:3], ONESF[:, :], SCAL[0:1, 4:5], start=False, stop=True)
            e.matmul(pC0[:, :], ONESB[:, :], FBROW[:, 0:512], start=True, stop=True)
            inc("pe", e.matmul(pC1[:, :], ONESB[:, :], FBROW[:, 512:1024], start=True, stop=True))
        blk.tensor(sc_pe)

        def sc_dve2(e):
            w(e, "pe")
            e.tensor_copy(out=RSIN[:, :], in_=pA0[:, 0:1])
            e.tensor_copy(out=RSK[:, :], in_=pA0[:, 1:2])
            e.tensor_copy(out=SPROD[:, :], in_=pA0[:, 2:3])
            e.tensor_copy(out=BIASB[:, 0:512], in_=pC0[:, :])
            inc("dve", e.tensor_copy(out=BIASB[:, 512:1024], in_=pC1[:, :]))
        vec_body(sc_dve2)


        stA = ExitStack()
        EA = stA.enter_context
        Y2 = EA(nc.sbuf_tensor("Y2", [128, NS * 1024], BF16))
        QDECT = EA(nc.sbuf_tensor("QDECT", [128, 2048], BF16))
        RNQ = EA(nc.sbuf_tensor("RNQ", [128, NS * 16], F32))
        LAL = EA(nc.sbuf_tensor("LAL", [128, 8 * 32], F32))
        CDA = EA(nc.sbuf_tensor("CDA", [128, 8 * 32], F32))
        S = EA(nc.sbuf_tensor("S", [128, 8 * 64], F32))
        Sb = EA(nc.sbuf_tensor("Sb", [128, 8 * 64], BF16))
        SbO = EA(nc.sbuf_tensor("SbO", [128, 8 * 64], BF16))
        SbM = EA(nc.sbuf_tensor("SbM", [128, 8 * 64], BF16))
        ABUF = EA(nc.sbuf_tensor("ABUF", [128, 8], F32))
        SGX = EA(nc.sbuf_tensor("SGX", [128, 2 * 520], F32))
        PVEC = EA(nc.sbuf_tensor("PVEC", [128, 8], F32))
        PM = EA(nc.sbuf_tensor("PM", [128, 8], F32))
        MSK = EA(nc.sbuf_tensor("MSK", [128, 128], F32))
        SELF = EA(nc.sbuf_tensor("SELF", [128, 128], F32))
        SELB = EA(nc.sbuf_tensor("SELB", [128, 128], BF16))
        BSEL = EA(nc.sbuf_tensor("BSEL", [16, 16 * 128], F32))
        SMALL = EA(nc.sbuf_tensor("SMALL", [16, 2048], F32))
        SCT = EA(nc.sbuf_tensor("SCT", [128, 32], F32))
        RKT = EA(nc.sbuf_tensor("RKT", [128, 32], F32))
        WQ = EA(nc.sbuf_tensor("WQs", [128, 8 * 1024], BF16))
        WK = EA(nc.sbuf_tensor("WKs", [128, 8 * 1024], BF16))
        WV = EA(nc.sbuf_tensor("WVs", [128, 8 * 1024], BF16))
        WGs = EA(nc.sbuf_tensor("WGs", [128, 8 * 1024], BF16))
        WOs = EA(nc.sbuf_tensor("WOs", [128, 8 * 1024], BF16))
        XT = EA(nc.sbuf_tensor("XT", [128, 1024], F32))
        SCR = EA(nc.sbuf_tensor("SCR", [128, 1024], F32))
        YTI = EA(nc.sbuf_tensor("YTI", [128, 2048], BF16))
        XN = EA(nc.sbuf_tensor("XN", [128, 2048], BF16))
        XNT = EA(nc.sbuf_tensor("XNT", [128, 2048], BF16))
        QF = EA(nc.sbuf_tensor("QF", [128, 2048], BF16))
        KF = EA(nc.sbuf_tensor("KF", [128, 2048], BF16))
        VT = EA(nc.sbuf_tensor("VT", [128, 2048], BF16))
        ALPHA = EA(nc.sbuf_tensor("ALPHA", [128, 2048], F32))
        LACUM = EA(nc.sbuf_tensor("LACUM", [128, 512], F32))
        EXPB = EA(nc.sbuf_tensor("EXPB", [128, 512], BF16))
        SQB = EA(nc.sbuf_tensor("SQB", [128, 512], BF16))
        ATTB = EA(nc.sbuf_tensor("ATTB", [128, 2048], BF16))
        OTOK = EA(nc.sbuf_tensor("OTOK", [128, 1024], BF16))
        OT = EA(nc.sbuf_tensor("OT", [128, 1024], BF16))
        STATS = EA(nc.sbuf_tensor("STATS", [128, 32], F32))

        def a_init_dve(e):
            e.memset(Y2[:, :], 0.0)
            e.memset(LAL[:, :], 0.0)
            e.memset(S[:, :], 0.0)
            e.memset(Sb[:, :], 0.0)
            inc("dve", e.memset(PVEC[:, :], 1.0))
        vec_body(a_init_dve)

        DIN_WBASE = cnt["din"]
        ACT_WBASE = cnt["act"]
        DW0 = cnt["dw"]
        DW20 = cnt["dw2"]
        DW30 = cnt["dw3"]
        DVE_INIT = cnt["dve"]
        Y2F = Y2.bitcast(F32)
        WSLOT = [Y2F[:, 2048 + k * 1024: 3072 + k * 1024] for k in range(6)]

        def wload(e):
            for ap, sb in [(maskneg, MSK), (sel16f, SELF), (sel1b, SELB), (bsel, BSEL)]:
                inc("din", e.dma_start(out=sb[:, :], in_=ap), 16)
            w(e, "din")
            e.wait_ge(sems["dve"], DVE_INIT)
            inc("din", e.dma_start(out=WSLOT[0][0:1, 0:1024], in_=b2.rearrange("(o d) -> o d", o=1)), 16)
            w(e, "din")
            i = 0
            for ap in [Wq, Wk, Wg, Wv, Wo]:
                for kc in range(8):
                    if i % 3 == 0:
                        e.wait_ge(sems["act"], ACT_WBASE + 1 + max(0, i - 5))
                        inc("dw", e.dma_start(out=WSLOT[i % 6][:, 0:1024],
                                              in_=ap[kc * 128:(kc + 1) * 128, :]), 16)
                    i += 1
        blk.sync(wload)

        def wload2(e):
            e.wait_ge(sems["dve"], DVE_INIT)
            e.wait_ge(sems["din"], DIN_WBASE + 5 * 16)
            i = 0
            for ap in [Wq, Wk, Wg, Wv, Wo]:
                for kc in range(8):
                    if i % 3 == 1:
                        e.wait_ge(sems["act"], ACT_WBASE + 1 + max(0, i - 5))
                        inc("dw2", e.dma_start(out=WSLOT[i % 6][:, 0:1024],
                                               in_=ap[kc * 128:(kc + 1) * 128, :]), 16)
                    i += 1
        blk.gpsimd(wload2)

        wflat = []
        for ap3 in [Wq, Wk, Wg, Wv, Wo]:
            for kc in range(8):
                wflat.append(ap3[kc * 128:(kc + 1) * 128, :])

        def wcast(e):
            e.wait_ge(sems["din"], DIN_WBASE + 5 * 16)
            for i3 in (2, 5):
                e.wait_ge(sems["dve"], DVE_INIT)
                inc("dw3", e.dma_start(out=WSLOT[i3 % 6][:, 0:1024], in_=wflat[i3]), 16)
            inc("act", e.activation(B2ROWB[:, :], WSLOT[0][0:1, 0:1024], AF.Copy))
            i = 0
            for sb in [WQ, WK, WGs, WV, WOs]:
                for kc in range(8):
                    j = i // 3
                    q = i % 3
                    if q == 0:
                        e.wait_ge(sems["dw"], DW0 + min(j + 2, 14) * 16)
                    elif q == 1:
                        e.wait_ge(sems["dw2"], DW20 + min(j + 2, 13) * 16)
                    else:
                        e.wait_ge(sems["dw3"], DW30 + min(j + 2, 13) * 16)
                    inc("act", e.activation(sb[:, kc * 1024:(kc + 1) * 1024],
                                            WSLOT[i % 6][:, 0:1024], AF.Copy))
                    i6 = i + 6
                    if i6 < 40 and i6 % 3 == 2:
                        e.wait_ge(sems["act"], ACT_WBASE + i6 - 4)
                        inc("dw3", e.dma_start(out=WSLOT[i6 % 6][:, 0:1024], in_=wflat[i6]), 16)
                    i += 1
        act_body(wcast)
        ACT_WEND = cnt["act"]

        PREV_SQ = [0]
        for t in range(NT):
            for sh in range(2):
                DVE_PREV = cnt["dve"]

                def a_x(e, t=t, sh=sh, DVE_PREV=DVE_PREV, psq=PREV_SQ[0]):
                    w(e, "dve", DVE_PREV)
                    if psq:
                        e.wait_ge(sems["act"], psq)
                    inc("din", e.dma_start(out=XT[:, 0:1024],
                                           in_=x_in[t * 256 + sh * 128: t * 256 + sh * 128 + 128, :]), 16)
                    w(e, "din")
                blk.sync(a_x)

                def a_sq(e, t=t, sh=sh):
                    w(e, "din")
                    c0 = sh * 16
                    inc("act", e.activation(SCR[:, 0:1024], XT[:, 0:1024], AF.Square,
                                            accum_out=STATS[:, c0 + 1:c0 + 2]))
                act_body(a_sq)
                ACT_SQ0 = cnt["act"]
                PREV_SQ[0] = ACT_SQ0

                def a_ln0(e, t=t, sh=sh, ACT_SQ0=ACT_SQ0):
                    w(e, "din")
                    c0 = sh * 16
                    inc("dve", e.reduce_sum(out=STATS[:, c0:c0 + 1], in_=XT[:, 0:1024], axis=AX.X))
                    w(e, "dve")
                    inc("dve", e.tensor_scalar(out=STATS[:, c0 + 2:c0 + 3], in0=STATS[:, c0:c0 + 1],
                                               scalar1=1.0 / D, scalar2=None, op0=ALU.mult))
                    w(e, "dve")
                    inc("dve", e.tensor_tensor(out=STATS[:, c0 + 3:c0 + 4], in0=STATS[:, c0 + 2:c0 + 3],
                                               in1=STATS[:, c0 + 2:c0 + 3], op=ALU.mult))
                    e.wait_ge(sems["act"], ACT_SQ0)
                    w(e, "dve")
                    inc("dve", e.scalar_tensor_tensor(out=STATS[:, c0 + 4:c0 + 5],
                                                      in0=STATS[:, c0 + 1:c0 + 2],
                                                      scalar=1.0 / D, in1=STATS[:, c0 + 3:c0 + 4],
                                                      op0=ALU.mult, op1=ALU.subtract))
                vec_body(a_ln0)
                DVE_VAR0 = cnt["dve"]

                def a_sd(e, t=t, sh=sh, DVE_VAR0=DVE_VAR0):
                    w(e, "dve", DVE_VAR0)
                    c0 = sh * 16
                    inc("act", e.activation(STATS[:, c0 + 5:c0 + 6], STATS[:, c0 + 4:c0 + 5],
                                            AF.Sqrt, bias=EPSC[:, 0:1]))
                act_body(a_sd)
                ACT_SD0 = cnt["act"]

                def a_y(e, t=t, sh=sh, ACT_SD0=ACT_SD0):
                    w(e, "act", ACT_SD0)
                    c0 = sh * 16
                    inc("dve", e.reciprocal(out=STATS[:, c0 + 6:c0 + 7], in_=STATS[:, c0 + 5:c0 + 6]))
                    w(e, "dve")
                    inc("dve", e.tensor_scalar(out=YTI[:, sh * 1024:(sh + 1) * 1024],
                                               in0=XT[:, 0:1024],
                                               scalar1=STATS[:, c0 + 2:c0 + 3],
                                               scalar2=STATS[:, c0 + 6:c0 + 7],
                                               op0=ALU.subtract, op1=ALU.mult))
                vec_body(a_y)
                DVE_Y = cnt["dve"]

                def a_sq1(e, t=t, sh=sh, DVE_Y=DVE_Y):
                    w(e, "dve", DVE_Y)
                    c0 = sh * 16 + 8
                    inc("act", e.activation(SCR[:, 0:1024], YTI[:, sh * 1024:(sh + 1) * 1024],
                                            AF.Square, accum_out=STATS[:, c0 + 1:c0 + 2]))
                act_body(a_sq1)
                ACT_SQ1 = cnt["act"]

                def a_ln1(e, t=t, sh=sh, DVE_Y=DVE_Y, ACT_SQ1=ACT_SQ1):
                    w(e, "dve", DVE_Y)
                    c0 = sh * 16 + 8
                    inc("dve", e.reduce_sum(out=STATS[:, c0:c0 + 1],
                                            in_=YTI[:, sh * 1024:(sh + 1) * 1024], axis=AX.X))
                    w(e, "dve")
                    inc("dve", e.tensor_scalar(out=STATS[:, c0 + 2:c0 + 3], in0=STATS[:, c0:c0 + 1],
                                               scalar1=1.0 / D, scalar2=None, op0=ALU.mult))
                    w(e, "dve")
                    inc("dve", e.tensor_tensor(out=STATS[:, c0 + 3:c0 + 4], in0=STATS[:, c0 + 2:c0 + 3],
                                               in1=STATS[:, c0 + 2:c0 + 3], op=ALU.mult))
                    e.wait_ge(sems["act"], ACT_SQ1)
                    w(e, "dve")
                    inc("dve", e.scalar_tensor_tensor(out=STATS[:, c0 + 4:c0 + 5],
                                                      in0=STATS[:, c0 + 1:c0 + 2],
                                                      scalar=1.0 / D, in1=STATS[:, c0 + 3:c0 + 4],
                                                      op0=ALU.mult, op1=ALU.subtract))
                vec_body(a_ln1)
                DVE_VAR1 = cnt["dve"]

                def a_sd1(e, t=t, sh=sh, DVE_VAR1=DVE_VAR1):
                    w(e, "dve", DVE_VAR1)
                    c0 = sh * 16 + 8
                    inc("act", e.activation(STATS[:, c0 + 5:c0 + 6], STATS[:, c0 + 4:c0 + 5],
                                            AF.Sqrt, bias=EPSC[:, 0:1]))
                act_body(a_sd1)
                ACT_SD1 = cnt["act"]

                def a_xn(e, t=t, sh=sh, ACT_SD1=ACT_SD1):
                    w(e, "act", ACT_SD1)
                    c0 = sh * 16 + 8
                    inc("dve", e.reciprocal(out=STATS[:, c0 + 6:c0 + 7], in_=STATS[:, c0 + 5:c0 + 6]))
                    w(e, "dve")
                    inc("dve", e.tensor_scalar(out=XN[:, sh * 1024:(sh + 1) * 1024],
                                               in0=YTI[:, sh * 1024:(sh + 1) * 1024],
                                               scalar1=STATS[:, c0 + 2:c0 + 3],
                                               scalar2=STATS[:, c0 + 6:c0 + 7],
                                               op0=ALU.subtract, op1=ALU.mult))
                vec_body(a_xn)
            DVE_XN = cnt["dve"]

            PE_T0 = cnt["pe"]
            DVE_T0 = cnt["dve"]

            def a_xnt_pe(e, t=t):
                w(e, "dve", DVE_XN)
                for i in range(16):
                    sh, fc = i // 8, i % 8
                    if i >= 4:
                        e.wait_ge(sems["dve"], DVE_T0 + i - 3)
                    inc("pe", e.transpose(ptrbs[i % 4],
                                          XN[:, sh * 1024 + fc * 128: sh * 1024 + fc * 128 + 128], IDB[:, :]))
            blk.tensor(a_xnt_pe)

            def a_xnt_dve(e, t=t):
                for i in range(16):
                    sh, fc = i // 8, i % 8
                    e.wait_ge(sems["pe"], PE_T0 + i + 1)
                    inc("dve", e.tensor_scalar(out=XNT[:, fc * 256 + sh * 128: fc * 256 + sh * 128 + 128],
                                               in0=ptrbs[i % 4],
                                               scalar1=PRM[:, 8 + fc:9 + fc], scalar2=PRM[:, 16 + fc:17 + fc],
                                               op0=ALU.mult, op1=ALU.add))
            vec_body(a_xnt_dve)
            DVE_XNT = cnt["dve"]

            PE_PROJ0 = cnt["pe"]
            ACT_PQ0 = cnt["act"]
            DVE_PK0 = cnt["dve"]

            def a_proj_pe(e, t=t):
                w(e, "dve", DVE_XNT)
                if t == 0:
                    e.wait_ge(sems["act"], ACT_WBASE + 9)
                for oc in range(8):
                    if oc >= 2:
                        e.wait_ge(sems["act"], ACT_PQ0 + oc - 1)
                    for kc in range(8):
                        mm = e.matmul(pAs[oc % 2][:, :],
                                      WQ[:, kc * 1024 + oc * 128: kc * 1024 + (oc + 1) * 128],
                                      XNT[:, kc * 256:(kc + 1) * 256], start=(kc == 0), stop=(kc == 7))
                    inc("pe", mm)
                if t == 0:
                    e.wait_ge(sems["act"], ACT_WBASE + 17)
                for oc in range(8):
                    if oc >= 2:
                        e.wait_ge(sems["dve"], DVE_PK0 + oc - 1)
                    for kc in range(8):
                        mm = e.matmul(pBs[oc % 2][:, :],
                                      WK[:, kc * 1024 + oc * 128: kc * 1024 + (oc + 1) * 128],
                                      XNT[:, kc * 256:(kc + 1) * 256], start=(kc == 0), stop=(kc == 7))
                    inc("pe", mm)
                if t == 0:
                    e.wait_ge(sems["act"], ACT_WBASE + 25)
                for oc in range(8):
                    e.wait_ge(sems["act"], ACT_PQ0 + 7 + oc)
                    for kc in range(8):
                        mm = e.matmul(pAs[oc % 2][:, :],
                                      WGs[:, kc * 1024 + oc * 128: kc * 1024 + (oc + 1) * 128],
                                      XNT[:, kc * 256:(kc + 1) * 256], start=(kc == 0), stop=(kc == 7))
                    inc("pe", mm)
                if t == 0:
                    e.wait_ge(sems["act"], ACT_WBASE + 33)
                for sh in range(2):
                    if sh == 1:
                        e.wait_ge(sems["dve"], DVE_PK0 + 9)
                    for nh in range(2):
                        for kc in range(8):
                            mm = e.matmul(pCs[nh][:, :],
                                          XNT[:, kc * 256 + sh * 128: kc * 256 + sh * 128 + 128],
                                          WV[:, kc * 1024 + nh * 512: kc * 1024 + (nh + 1) * 512],
                                          start=(kc == 0), stop=(kc == 7))
                    inc("pe", mm)
            blk.tensor(a_proj_pe)

            def a_projq_act(e, t=t):
                for oc in range(8):
                    e.wait_ge(sems["pe"], PE_PROJ0 + oc + 1)
                    inc("act", e.activation(QF[:, oc * 256:(oc + 1) * 256], pAs[oc % 2][:, :], AF.Copy))
                for oc in range(8):
                    e.wait_ge(sems["pe"], PE_PROJ0 + 16 + oc + 1)
                    inc("act", e.activation(ALPHA[:, oc * 256:(oc + 1) * 256], pAs[oc % 2][:, :],
                                            AF.Sigmoid, bias=PRM[:, 0 + oc:1 + oc]))
            act_body(a_projq_act)
            ACT_G = cnt["act"]

            def a_projk_dve(e, t=t):
                for oc in range(8):
                    e.wait_ge(sems["pe"], PE_PROJ0 + 8 + oc + 1)
                    inc("dve", e.tensor_copy(out=KF[:, oc * 256:(oc + 1) * 256], in_=pBs[oc % 2][:, :]))
                for sh in range(2):
                    e.wait_ge(sems["pe"], PE_PROJ0 + 24 + sh + 1)
                    e.tensor_copy(out=VT[:, sh * 1024: sh * 1024 + 512], in_=pC0[:, :])
                    inc("dve", e.tensor_copy(out=VT[:, sh * 1024 + 512:(sh + 1) * 1024], in_=pC1[:, :]))
            vec_body(a_projk_dve)
            DVE_V = cnt["dve"]

            DVE_CL0 = cnt["dve"]
            ACT_LN0 = cnt["act"]

            def a_clamp_dve(e, t=t, ACT_LN0=ACT_LN0):
                w(e, "act", ACT_G)
                for oc in range(8):
                    if oc >= 2:
                        e.wait_ge(sems["act"], ACT_LN0 + oc - 1)
                    inc("dve", e.tensor_scalar(out=SCR[:, (oc % 2) * 256:(oc % 2) * 256 + 256],
                                               in0=ALPHA[:, oc * 256:(oc + 1) * 256],
                                               scalar1=1e-6, scalar2=None, op0=ALU.max))
            vec_body(a_clamp_dve)

            def a_ln_act(e, t=t, DVE_CL0=DVE_CL0):
                for oc in range(8):
                    e.wait_ge(sems["dve"], DVE_CL0 + oc + 1)
                    inc("act", e.activation(ALPHA[:, oc * 256:(oc + 1) * 256],
                                            SCR[:, (oc % 2) * 256:(oc % 2) * 256 + 256], AF.Ln))
            act_body(a_ln_act)
            ACT_LA = cnt["act"]
            LA = ALPHA

            dseq = []
            for fc in range(8):
                dseq.append(("scan", fc))
                if fc >= 1:
                    dseq.append(("qdec", fc - 1))
                    dseq.append(("dte", fc - 1))
                if fc >= 2:
                    dseq.append(("kdt", fc - 2))
            dseq += [("qdec", 7), ("dte", 7), ("kdt", 6), ("kdt", 7)]
            aseq = [("eL", 0), ("eL", 1)]
            for fc in range(8):
                if fc >= 2:
                    aseq.append(("eL", fc))
                aseq.append(("eD", fc))
            DVE_SC0 = cnt["dve"]
            ACT_SC0 = cnt["act"]
            dcnt = {}
            acnt = {}
            _c = DVE_SC0
            for it in dseq:
                _c += 2 if it[0] == "scan" else 1
                dcnt[it] = _c
            for j, it in enumerate(aseq):
                acnt[it] = ACT_SC0 + j + 1
            ACT_LA_t = ACT_LA

            def a_scanpipe_dve(e, t=t, dcnt=dcnt, acnt=acnt, ACT_LA_t=ACT_LA_t):
                w(e, "act", ACT_LA_t)
                w(e, "dout")
                for op, fc in dseq:
                    sl = (fc % 2) * 256
                    if op == "scan":
                        if fc >= 2:
                            w(e, "dve", dcnt[("dte", fc - 2)])
                        last = None
                        for c in range(4):
                            last = e.tensor_tensor_scan(out=LACUM[:, sl + c * 64: sl + (c + 1) * 64],
                                                        data0=LA[:, fc * 256 + c * 64: fc * 256 + (c + 1) * 64],
                                                        data1=LA[:, fc * 256 + c * 64: fc * 256 + (c + 1) * 64],
                                                        initial=0.0, op0=ALU.add, op1=ALU.bypass)
                        inc("dve", last)
                        w(e, "dve")
                        inc("dve", e.tensor_copy(
                            out=LAL[:, fc * 32 + t * 4: fc * 32 + t * 4 + 4],
                            in_=LACUM[:, sl:sl + 256].rearrange("p (c z) -> p c z", z=64)[:, :, 63:64]))
                    elif op == "qdec":
                        e.wait_ge(sems["act"], acnt[("eL", fc)])
                        inc("dve", e.tensor_tensor(out=QDECT[:, fc * 256:(fc + 1) * 256],
                                                   in0=QF[:, fc * 256:(fc + 1) * 256],
                                                   in1=EXPB[:, sl:sl + 256], op=ALU.mult))
                    elif op == "dte":
                        if fc >= 2:
                            e.wait_ge(sems["act"], acnt[("eD", fc - 2)])
                        last = None
                        for c in range(4):
                            last = e.tensor_scalar(out=SCR[:, sl + c * 64: sl + (c + 1) * 64],
                                                   in0=LACUM[:, sl + c * 64: sl + (c + 1) * 64],
                                                   scalar1=LAL[:, fc * 32 + t * 4 + c: fc * 32 + t * 4 + c + 1],
                                                   scalar2=None, op0=ALU.subtract)
                        inc("dve", last)
                    elif op == "kdt":
                        e.wait_ge(sems["act"], acnt[("eD", fc)])
                        inc("dve", e.tensor_tensor(out=XN[:, fc * 256:(fc + 1) * 256],
                                                   in0=KF[:, fc * 256:(fc + 1) * 256],
                                                   in1=EXPB[:, sl:sl + 256], op=ALU.mult))
            vec_body(a_scanpipe_dve)

            def a_scanpipe_act(e, t=t, dcnt=dcnt):
                for op, fc in aseq:
                    sl = (fc % 2) * 256
                    if op == "eL":
                        e.wait_ge(sems["dve"], dcnt[("scan", fc)])
                        if fc >= 2:
                            e.wait_ge(sems["dve"], dcnt[("kdt", fc - 2)])
                        inc("act", e.activation(EXPB[:, sl:sl + 256], LACUM[:, sl:sl + 256], AF.Exp))
                    else:
                        e.wait_ge(sems["dve"], dcnt[("dte", fc)])
                        inc("act", e.activation(EXPB[:, sl:sl + 256], SCR[:, sl:sl + 256],
                                                AF.Exp, scale=-1.0))
            act_body(a_scanpipe_act)
            DVE_SCAN = dcnt[("scan", 7)]
            DVE_KDT = dcnt[("kdt", 7)]
            KDT = XN
            DVE_QD7 = dcnt[("qdec", 7)]

            def a_qspill(e, t=t, DVE_QD7=DVE_QD7):
                e.wait_ge(sems["dve"], DVE_QD7)
                inc("dout", e.dma_start(out=qdd[:, :, t * 256:(t + 1) * 256].rearrange("f p c -> p f c"),
                                        in_=QDECT[:, :].rearrange("p (f c) -> p f c", c=256)), 16)
                w(e, "dout")
            blk.gpsimd(a_qspill)
            DOUT_QS = cnt["dout"]

            def a_cd_act(e, t=t):
                w(e, "dve", DVE_SCAN)
                inc("act", e.activation(
                    CDA[:, :].rearrange("p (f c) -> p f c", c=32)[:, :, t * 4:(t + 1) * 4],
                    LAL[:, :].rearrange("p (f c) -> p f c", c=32)[:, :, t * 4:(t + 1) * 4], AF.Exp))
            act_body(a_cd_act)

            PE_ML0 = cnt["pe"]
            ACT_NSQ0 = cnt["act"]

            def a_sqn_act(e, t=t, PE_ML0=PE_ML0):
                w(e, "dve", DVE_V)
                for fc in range(8):
                    if fc >= 2:
                        e.wait_ge(sems["pe"], PE_ML0 + 1 + fc - 1)
                    inc("act", e.activation(SQB[:, (fc % 2) * 256:(fc % 2) * 256 + 256],
                                            QF[:, fc * 256:(fc + 1) * 256], AF.Square))
            act_body(a_sqn_act)

            def a_mlog_pe(e, t=t, ACT_NSQ0=ACT_NSQ0):
                w(e, "dve", DVE_SCAN)
                for fc in range(8):
                    mm = e.matmul(pA0[0:16, 0:256], SELF[:, fc * 16:(fc + 1) * 16],
                                  LA[:, fc * 256:(fc + 1) * 256], start=(fc == 0), stop=(fc == 7))
                inc("pe", mm)
                for fc in range(8):
                    e.wait_ge(sems["act"], ACT_NSQ0 + fc + 1)
                    inc("pe", e.matmul(pB0[0:16, 0:256], SELB[:, fc * 16:(fc + 1) * 16],
                                       SQB[:, (fc % 2) * 256:(fc % 2) * 256 + 256],
                                       start=(fc == 0), stop=(fc == 7)))
            blk.tensor(a_mlog_pe)
            PE_ML = cnt["pe"]

            def a_mlog_dve(e, t=t, PE_ML=PE_ML):
                e.wait_ge(sems["pe"], PE_ML - 8)
                inc("dve", e.tensor_copy(out=SMALL[:, 0:256], in_=pA0[0:16, 0:256]))
                w(e, "dve")
                for c in range(4):
                    e.tensor_tensor_scan(out=SMALL[:, 256 + c * 64:256 + (c + 1) * 64],
                                         data0=SMALL[:, c * 64:(c + 1) * 64],
                                         data1=SMALL[:, c * 64:(c + 1) * 64],
                                         initial=0.0, op0=ALU.add, op1=ALU.bypass)
                e.wait_ge(sems["pe"], PE_ML)
                inc("dve", e.tensor_copy(out=SMALL[:, 512:768], in_=pB0[0:16, 0:256]))
            vec_body(a_mlog_dve)
            DVE_QSS = cnt["dve"]
            ACT_KSQ0 = cnt["act"]
            PE_KSS0 = cnt["pe"]

            def a_sqk_act(e, t=t, PE_ML=PE_ML, PE_KSS0=PE_KSS0):
                w(e, "pe", PE_ML)
                for fc in range(8):
                    if fc >= 2:
                        e.wait_ge(sems["pe"], PE_KSS0 + fc - 1)
                    inc("act", e.activation(SQB[:, (fc % 2) * 256:(fc % 2) * 256 + 256],
                                            KF[:, fc * 256:(fc + 1) * 256], AF.Square))
            act_body(a_sqk_act)

            def a_kss_pe(e, t=t, ACT_KSQ0=ACT_KSQ0, DVE_QSS=DVE_QSS):
                w(e, "dve", DVE_QSS)
                for fc in range(8):
                    e.wait_ge(sems["act"], ACT_KSQ0 + fc + 1)
                    inc("pe", e.matmul(pB0[0:16, 0:256], SELB[:, fc * 16:(fc + 1) * 16],
                                       SQB[:, (fc % 2) * 256:(fc % 2) * 256 + 256],
                                       start=(fc == 0), stop=(fc == 7)))
            blk.tensor(a_kss_pe)
            PE_KSS = cnt["pe"]

            def a_kss_dve(e, t=t, PE_KSS=PE_KSS):
                w(e, "pe", PE_KSS)
                inc("dve", e.tensor_copy(out=SMALL[:, 768:1024], in_=pB0[0:16, 0:256]))
            vec_body(a_kss_dve)
            DVE_KSS = cnt["dve"]

            def a_norm_act(e, t=t):
                w(e, "dve", DVE_KSS)
                inc("act", e.activation(SMALL[:, 0:256], SMALL[:, 512:768], AF.Sqrt))
                w(e, "act")
                inc("act", e.activation(SMALL[:, 512:768], SMALL[:, 768:1024], AF.Sqrt))
            act_body(a_norm_act)
            ACT_SQRT = cnt["act"]

            def a_norm_dve(e, t=t):
                w(e, "act", ACT_SQRT)
                inc("dve", e.tensor_scalar(out=SMALL[:, 0:256], in0=SMALL[:, 0:256],
                                           scalar1=1e-12, scalar2=None, op0=ALU.max))
                inc("dve", e.tensor_scalar(out=SMALL[:, 512:768], in0=SMALL[:, 512:768],
                                           scalar1=1e-12, scalar2=None, op0=ALU.max))
                w(e, "dve")
                e.reciprocal(out=SMALL[:, 1024:1280], in_=SMALL[:, 0:256])
                inc("dve", e.reciprocal(out=SMALL[:, 1280:1536], in_=SMALL[:, 512:768]))
            vec_body(a_norm_dve)
            DVE_RQ = cnt["dve"]

            def a_lnr_act(e, t=t):
                w(e, "dve", DVE_RQ)
                inc("act", e.activation(SMALL[:, 512:768], SMALL[:, 1280:1536], AF.Ln))
            act_body(a_lnr_act)
            ACT_LNR = cnt["act"]

            def a_lmq_dve(e, t=t):
                w(e, "act", ACT_LNR)
                inc("dve", e.tensor_tensor(out=SMALL[:, 1792:2048], in0=SMALL[:, 256:512],
                                           in1=SMALL[:, 512:768], op=ALU.subtract))
            vec_body(a_lmq_dve)
            DVE_LMQ = cnt["dve"]

            PE_TT0 = cnt["pe"]
            DVE_TT0 = cnt["dve"]

            def a_tok_pe(e, t=t):
                w(e, "dve", DVE_LMQ)
                for sh in range(2):
                    for j, off in enumerate([1792, 1024, 1280]):
                        k = sh * 3 + j
                        if k >= 1:
                            e.wait_ge(sems["dve"], DVE_TT0 + k)
                        inc("pe", e.transpose(ptrf[:, 0:16],
                                              SMALL[0:16, off + sh * 128: off + sh * 128 + 128],
                                              IDF[0:16, 0:16]))
            blk.tensor(a_tok_pe)

            def a_tok_dve(e, t=t):
                for sh in range(2):
                    s = 2 * t + sh
                    outs = [SCT[:, sh * 16:(sh + 1) * 16], RNQ[:, s * 16:(s + 1) * 16],
                            RKT[:, sh * 16:(sh + 1) * 16]]
                    for j in range(3):
                        k = sh * 3 + j
                        e.wait_ge(sems["pe"], PE_TT0 + k + 1)
                        inc("dve", e.tensor_copy(out=outs[j], in_=ptrf[:, 0:16]))
            vec_body(a_tok_dve)
            DVE_TOK = cnt["dve"]

            PE_KT0 = cnt["pe"]
            DVE_KT0 = cnt["dve"]

            DVE_PRE_KTR = cnt["dve"]

            def a_ktr_pe(e, t=t, DVE_PRE_KTR=DVE_PRE_KTR):
                w(e, "dve", DVE_PRE_KTR)
                for i in range(16):
                    sh, fc = i // 8, i % 8
                    if i >= 4:
                        e.wait_ge(sems["dve"], DVE_KT0 + 2 * (i - 3))
                    inc("pe", e.transpose(ptrbs[i % 4],
                                          KDT[:, fc * 256 + sh * 128: fc * 256 + sh * 128 + 128], IDB[:, :]))
            blk.tensor(a_ktr_pe)

            def a_ktr_dve(e, t=t):
                for i in range(16):
                    sh, fc = i // 8, i % 8
                    e.wait_ge(sems["pe"], PE_KT0 + i + 1)
                    for hh in range(2):
                        inc("dve", e.tensor_scalar(
                            out=XNT[:, sh * 1024 + fc * 128 + hh * 64: sh * 1024 + fc * 128 + hh * 64 + 64],
                            in0=ptrbs[i % 4][:, hh * 64: hh * 64 + 64],
                            scalar1=RKT[:, sh * 16 + 2 * fc + hh: sh * 16 + 2 * fc + hh + 1],
                            scalar2=None, op0=ALU.mult))
            vec_body(a_ktr_dve)
            KTOK = XNT
            DVE_KTOK = cnt["dve"]

            for sh in range(2):
                s = 2 * t + sh
                PE_S0 = cnt["pe"]
                DVE_A0 = cnt["dve"]
                ACT_D0 = cnt["act"]
                stt_cnt = lambda i, b=DVE_A0: b + (1 if i == 0 else 2 * i)
                att_cnt = lambda i, b=DVE_A0: b + (2 * i + 3 if i < 15 else 32)
                exp_cnt = lambda i, b=ACT_D0: b + i + 1

                DVE_NOW = cnt["dve"]

                def g_sb_pe(e, t=t, sh=sh, s=s, DVE_NOW=DVE_NOW, att_cnt=att_cnt):
                    w(e, "dve", DVE_NOW)
                    for i in range(16):
                        hp, hh = i // 2, i % 2
                        if i >= 2:
                            e.wait_ge(sems["dve"], att_cnt(i - 2))
                        hr = slice(hh * 64, hh * 64 + 64)
                        col = hp * 256 + sh * 128
                        e.matmul(pBs[i % 2][:, 0:128],
                                 KF[hr, col:col + 128], QF[hr, col:col + 128], start=True, stop=True)
                        inc("pe", e.matmul(pAs[i % 2][:, 0:128],
                                           BSEL[:, i * 128:(i + 1) * 128],
                                           SMALL[0:16, 256 + sh * 128:256 + sh * 128 + 128],
                                           start=True, stop=True))
                blk.tensor(g_sb_pe)
                PE_SB = cnt["pe"]

                def g_da_dve(e, t=t, sh=sh, s=s, PE_S0=PE_S0, ACT_D0=ACT_D0, exp_cnt=exp_cnt):
                    w(e, "dout")
                    def emit_stt(i):
                        e.wait_ge(sems["pe"], PE_S0 + i + 1)
                        if i >= 2:
                            e.wait_ge(sems["act"], exp_cnt(i - 2))
                        inc("dve", e.scalar_tensor_tensor(
                            out=SCR[:, (i % 2) * 128:(i % 2) * 128 + 128],
                            in0=pAs[i % 2][:, 0:128],
                            scalar=SCT[:, sh * 16 + i: sh * 16 + i + 1],
                            in1=MSK[:, :], op0=ALU.subtract, op1=ALU.add))

                    def emit_att(i):
                        e.wait_ge(sems["act"], exp_cnt(i))
                        inc("dve", e.tensor_tensor(
                            out=ATTB[:, i * 128:(i + 1) * 128],
                            in0=pBs[i % 2][:, 0:128],
                            in1=SQB[:, (i % 4) * 128:(i % 4) * 128 + 128], op=ALU.mult))
                    emit_stt(0)
                    emit_stt(1)
                    for i in range(2, 16):
                        emit_att(i - 2)
                        emit_stt(i)
                    emit_att(14)
                    emit_att(15)
                vec_body(g_da_dve)

                def g_exp_act(e, t=t, sh=sh, s=s, stt_cnt=stt_cnt):
                    for i in range(16):
                        e.wait_ge(sems["dve"], stt_cnt(i))
                        inc("act", e.activation(SQB[:, (i % 4) * 128:(i % 4) * 128 + 128],
                                                SCR[:, (i % 2) * 128:(i % 2) * 128 + 128], AF.Exp))
                act_body(g_exp_act)

                PE_KV0 = cnt["pe"]

                def g_kv_pe(e, t=t, sh=sh, s=s):
                    w(e, "dve")
                    e.matmul(pD[:, :], ONESB[:, :], ZROW[:, :], start=True, stop=False)
                    e.matmul(pD2[:, :], ONESB[:, :], ZROW[:, :], start=True, stop=False)
                    for h in range(16):
                        hp, hh = h // 2, h % 2
                        hr = slice(hh * 64, hh * 64 + 64)
                        mm = e.matmul(pD[hr, hp * 64:(hp + 1) * 64],
                                      KTOK[0:64, sh * 1024 + hp * 128 + hh * 64: sh * 1024 + hp * 128 + hh * 64 + 64],
                                      VT[0:64, sh * 1024 + h * 64: sh * 1024 + (h + 1) * 64],
                                      start=False, stop=False)
                    inc("pe", mm)
                    w(e, "pe")
                    for h in range(16):
                        hp, hh = h // 2, h % 2
                        hr = slice(hh * 64, hh * 64 + 64)
                        e.matmul(pD2[hr, hp * 64:(hp + 1) * 64],
                                 KTOK[64:128, sh * 1024 + hp * 128 + hh * 64: sh * 1024 + hp * 128 + hh * 64 + 64],
                                 VT[64:128, sh * 1024 + h * 64: sh * 1024 + (h + 1) * 64],
                                 start=False, stop=False)
                    e.matmul(pD[:, :], ONESB[:, :], ZROW[:, :], start=False, stop=True)
                    mm = e.matmul(pD2[:, :], ONESB[:, :], ZROW[:, :], start=False, stop=True)
                    inc("pe", mm)
                blk.tensor(g_kv_pe)
                PE_KV1 = cnt["pe"]

                def g_sup_dve(e, t=t, sh=sh, s=s, PE_KV1=PE_KV1):
                    w(e, "pe", PE_KV1)
                    inc("dve", e.tensor_copy(out=SbO[:, :], in_=Sb[:, :]))
                    ci = t * 4 + sh * 2 + 0
                    last = None
                    for hp in range(8):
                        last = e.scalar_tensor_tensor(out=S[:, hp * 64:(hp + 1) * 64],
                                                      in0=S[:, hp * 64:(hp + 1) * 64],
                                                      scalar=CDA[:, hp * 32 + ci: hp * 32 + ci + 1],
                                                      in1=pD[:, hp * 64:(hp + 1) * 64],
                                                      op0=ALU.mult, op1=ALU.add)
                    inc("dve", last)
                    w(e, "dve")
                    inc("dve", e.tensor_copy(out=SbM[:, :], in_=S[:, :]))
                    w(e, "dve")
                    ci2 = t * 4 + sh * 2 + 1
                    last = None
                    for hp in range(8):
                        last = e.scalar_tensor_tensor(out=S[:, hp * 64:(hp + 1) * 64],
                                                      in0=S[:, hp * 64:(hp + 1) * 64],
                                                      scalar=CDA[:, hp * 32 + ci2: hp * 32 + ci2 + 1],
                                                      in1=pD2[:, hp * 64:(hp + 1) * 64],
                                                      op0=ALU.mult, op1=ALU.add)
                    inc("dve", last)
                    w(e, "dve")
                    inc("dve", e.tensor_copy(out=Sb[:, :], in_=S[:, :]))
                blk.vector(g_sup_dve)
                DVE_SUP2 = cnt["dve"]

                def g_o_pe(e, t=t, sh=sh, s=s, att_cnt=att_cnt, DVE_SUP2=DVE_SUP2):
                    w(e, "dve")
                    for h in range(16):
                        e.wait_ge(sems["dve"], att_cnt(h))
                        hp, hh = h // 2, h % 2
                        hr = slice(hh * 64, hh * 64 + 64)
                        pc = pCs[h // 8]
                        col = slice((h % 8) * 64, (h % 8) * 64 + 64)
                        e.matmul(pc[:, col], ATTB[:, h * 128:(h + 1) * 128],
                                 VT[:, sh * 1024 + h * 64: sh * 1024 + (h + 1) * 64],
                                 start=(h % 8 == 0), stop=False)
                        e.matmul(pc[0:64, col],
                                 QDECT[hr, hp * 256 + sh * 128: hp * 256 + sh * 128 + 64],
                                 SbO[hr, hp * 64:(hp + 1) * 64], start=False, stop=False)
                        e.matmul(pc[64:128, col],
                                 QDECT[hr, hp * 256 + sh * 128 + 64: hp * 256 + sh * 128 + 128],
                                 SbM[hr, hp * 64:(hp + 1) * 64], start=False, stop=False)
                        if h % 8 == 7:
                            mm = e.matmul(pc[:, :], ONESB[:, :], ZROW[:, :], start=False, stop=True)
                    inc("pe", mm)
                blk.tensor(g_o_pe)
                PE_O2 = cnt["pe"]

                def g_oe_dve(e, t=t, sh=sh, s=s, PE_O2=PE_O2):
                    w(e, "pe", PE_O2)
                    last = None
                    for h in range(16):
                        last = e.tensor_scalar(out=OTOK[:, h * 64:(h + 1) * 64],
                                               in0=pCs[h // 8][:, (h % 8) * 64:(h % 8) * 64 + 64],
                                               scalar1=RNQ[:, s * 16 + h: s * 16 + h + 1],
                                               scalar2=None, op0=ALU.mult)
                    inc("dve", last)
                blk.vector(g_oe_dve)
                DVE_OE = cnt["dve"]


                PE_OT0 = cnt["pe"]
                DVE_OT0 = cnt["dve"]

                def g_ot_pe(e, t=t, sh=sh, s=s, DVE_OE=DVE_OE, DVE_OT0=DVE_OT0):
                    w(e, "dve", DVE_OE)
                    for fc in range(8):
                        if fc >= 4:
                            e.wait_ge(sems["dve"], DVE_OT0 + fc - 3)
                        inc("pe", e.transpose(ptrbs[fc % 4],
                                              OTOK[:, fc * 128:(fc + 1) * 128], IDB[:, :]))
                blk.tensor(g_ot_pe)

                def g_ot_dve(e, t=t, sh=sh, s=s, PE_OT0=PE_OT0):
                    for fc in range(8):
                        e.wait_ge(sems["pe"], PE_OT0 + fc + 1)
                        inc("dve", e.tensor_copy(out=OT[:, fc * 128:(fc + 1) * 128],
                                                 in_=ptrbs[fc % 4]))
                vec_body(g_ot_dve)
                DVE_OTD = cnt["dve"]

                def g_wo_pe(e, t=t, sh=sh, s=s, DVE_OTD=DVE_OTD):
                    w(e, "dve", DVE_OTD)
                    if t == 0 and sh == 0:
                        e.wait_ge(sems["act"], ACT_WBASE + 41)
                    for nh in range(2):
                        for kc in range(8):
                            mm = e.matmul(pCs[nh][:, :], OT[:, kc * 128:(kc + 1) * 128],
                                          WOs[:, kc * 1024 + nh * 512: kc * 1024 + (nh + 1) * 512],
                                          start=(kc == 0), stop=(kc == 7))
                    inc("pe", mm)
                blk.tensor(g_wo_pe)
                PE_WO = cnt["pe"]

                def g_y2_dve(e, t=t, sh=sh, s=s, PE_WO=PE_WO):
                    w(e, "pe", PE_WO)
                    if s == 4:
                        w(e, "act", ACT_WEND)
                    e.tensor_tensor(out=Y2[:, s * 1024: s * 1024 + 512],
                                    in0=pC0[:, :], in1=YTI[:, sh * 1024: sh * 1024 + 512], op=ALU.add)
                    inc("dve", e.tensor_tensor(out=Y2[:, s * 1024 + 512:(s + 1) * 1024],
                                               in0=pC1[:, :], in1=YTI[:, sh * 1024 + 512:(sh + 1) * 1024],
                                               op=ALU.add))
                vec_body(g_y2_dve)

        global_DOUT_PUB = [0]
        _dox = DO_X
        def x_acore_dve(e):
            w(e, "act")
            last = None
            for hp in range(8):
                last = e.reduce_sum(out=ABUF[:, hp:hp + 1], in_=LAL[:, hp * 32:(hp + 1) * 32], axis=AX.X)
            inc("dve", last)
        vec_body(x_acore_dve)
        DVE_AC = cnt["dve"]

        def x_acore_act(e):
            w(e, "dve", DVE_AC)
            inc("act", e.activation(ABUF[:, :], ABUF[:, :], AF.Exp))
        if _dox: act_body(x_acore_act)
        ACT_AC = cnt["act"]

        DVE_X0 = cnt["dve"]

        def x_pub_gps(e):
            w(e, "act", ACT_AC)
            w(e, "dve")
            inc("dout", e.dma_start(out=cc_src[:, 0:512], in_=S[:, :]), 16)
            inc("dout", e.dma_start(out=cc_src[:, 512:520], in_=ABUF[:, :]), 16)
            e.wait_ge(sems["dout"], cnt["dout"])
            global_DOUT_PUB[0] = cnt["dout"]
            if USE_CC:
                inc("cc", e.collective_compute("AllGather", ALU.bypass,
                                               replica_groups=[list(range(NCORE))],
                                               ins=[cc_src], outs=[cc_dst]), 1)
                e.wait_ge(sems["cc"], cnt["cc"])
            else:
                for r in range(NCORE):
                    inc("cc", e.dma_start(out=cc_dst[r * 128:(r + 1) * 128, :], in_=SGX[:, 0:520]), 16)
                e.wait_ge(sems["cc"], cnt["cc"])
            for j, r in enumerate(range(NCORE - 1, -1, -1)):
                if j >= 2:
                    e.wait_ge(sems["dve"], DVE_X0 + 5 * (j - 1))
                inc("dout", e.dma_start(out=SGX[:, (j % 2) * 520:(j % 2) * 520 + 520],
                                        in_=cc_dst[r * 128:(r + 1) * 128, :]), 16)
                w(e, "dout")
        if _dox: blk.gpsimd(x_pub_gps)
        else: global_DOUT_PUB[0] = cnt['dout']

        DOUT_PUB = global_DOUT_PUB[0]

        def x_sin_dve(e):
            e.wait_ge(sems["dout"], DOUT_PUB)
            e.memset(S[:, :], 0.0)
            for j, r in enumerate(range(NCORE - 1, -1, -1)):
                sgr = SGX[:, (j % 2) * 520:(j % 2) * 520 + 520]
                e.wait_ge(sems["dout"], DOUT_PUB + (j + 1) * 16)
                w(e, "dve")
                inc("dve", e.tensor_scalar(out=PM[:, :], in0=PVEC[:, :], scalar1=M1S[:, r:r + 1],
                                           scalar2=None, op0=ALU.mult))
                w(e, "dve")
                last = None
                for hp in range(8):
                    last = e.scalar_tensor_tensor(out=S[:, hp * 64:(hp + 1) * 64],
                                                  in0=sgr[:, hp * 64:(hp + 1) * 64],
                                                  scalar=PM[:, hp:hp + 1],
                                                  in1=S[:, hp * 64:(hp + 1) * 64],
                                                  op0=ALU.mult, op1=ALU.add)
                inc("dve", last)
                w(e, "dve")
                inc("dve", e.tensor_scalar(out=PM[:, :], in0=sgr[:, 512:520],
                                           scalar1=1.0, scalar2=None, op0=ALU.subtract))
                w(e, "dve")
                inc("dve", e.tensor_scalar(out=PM[:, :], in0=PM[:, :], scalar1=M1S[:, r:r + 1],
                                           scalar2=1.0, op0=ALU.mult, op1=ALU.add))
                w(e, "dve")
                inc("dve", e.tensor_tensor(out=PVEC[:, :], in0=PVEC[:, :], in1=PM[:, :], op=ALU.mult))
            w(e, "dve")
            inc("dve", e.tensor_copy(out=Sb[:, :], in_=S[:, :]))
        if _dox: vec_body(x_sin_dve)

        BSTEP = DO_B if isinstance(DO_B, int) and not isinstance(DO_B, bool) else (99 if DO_B else 0)
        for t in (range(NT) if BSTEP > 0 else []):
            PE_QL = cnt["pe"]

            def b_qload(e, t=t, PE_QL=PE_QL):
                w(e, "pe", PE_QL)
                inc("dout", e.dma_start(out=QDECT[:, :].rearrange("p (f c) -> p f c", c=256),
                                        in_=qdd[:, :, t * 256:(t + 1) * 256].rearrange("f p c -> p f c")), 16)
                w(e, "dout")
            if BSTEP >= 1: blk.gpsimd(b_qload)
            DOUT_QL = cnt["dout"]

            for sh in range(2):
                s_ = 2 * t + sh

                DVE_PB0 = cnt["dve"]

                def b_sup_dve(e, t=t, sh=sh, s=s_):
                    w(e, "dve")
                    inc("dve", e.tensor_copy(out=SbO[:, :], in_=Sb[:, :]))
                    w(e, "dve")
                    ci = s * 2
                    last = None
                    for hp in range(8):
                        last = e.tensor_scalar(out=S[:, hp * 64:(hp + 1) * 64], in0=S[:, hp * 64:(hp + 1) * 64],
                                               scalar1=CDA[:, hp * 32 + ci: hp * 32 + ci + 1],
                                               scalar2=None, op0=ALU.mult)
                    inc("dve", last)
                    w(e, "dve")
                    inc("dve", e.tensor_copy(out=SbM[:, :], in_=S[:, :]))
                    w(e, "dve")
                    ci2 = s * 2 + 1
                    last = None
                    for hp in range(8):
                        last = e.tensor_scalar(out=S[:, hp * 64:(hp + 1) * 64], in0=S[:, hp * 64:(hp + 1) * 64],
                                               scalar1=CDA[:, hp * 32 + ci2: hp * 32 + ci2 + 1],
                                               scalar2=None, op0=ALU.mult)
                    inc("dve", last)
                    w(e, "dve")
                    inc("dve", e.tensor_copy(out=Sb[:, :], in_=S[:, :]))
                blk.vector(b_sup_dve)
                DVE_PB1 = cnt["dve"]

                def b_corr_pe(e, t=t, sh=sh, s=s_, DOUT_QL=DOUT_QL, DVE_PB1=DVE_PB1):
                    w(e, "dve", DVE_PB1)
                    e.wait_ge(sems["dout"], DOUT_QL)
                    mm = e.matmul(pC0[:, :], ONESB[:, :], ZROW[:, :], start=True, stop=False)
                    mm = e.matmul(pC1[:, :], ONESB[:, :], ZROW[:, :], start=True, stop=False)
                    for hh_run in range(2):
                        inc("pe", mm)
                        w(e, "pe")
                        for h in range(hh_run, 16, 2):
                            hp = h // 2
                            hr = slice(hh_run * 64, hh_run * 64 + 64)
                            pc = pCs[h // 8]
                            col = slice((h % 8) * 64, (h % 8) * 64 + 64)
                            e.matmul(pc[0:64, col],
                                     QDECT[hr, hp * 256 + sh * 128: hp * 256 + sh * 128 + 64],
                                     SbO[hr, hp * 64:(hp + 1) * 64], start=False, stop=False)
                            mm = e.matmul(pc[64:128, col],
                                          QDECT[hr, hp * 256 + sh * 128 + 64: hp * 256 + sh * 128 + 128],
                                          SbM[hr, hp * 64:(hp + 1) * 64], start=False, stop=False)
                    inc("pe", mm)
                    w(e, "pe")
                    e.matmul(pC0[:, :], ONESB[:, :], ZROW[:, :], start=False, stop=True)
                    mm = e.matmul(pC1[:, :], ONESB[:, :], ZROW[:, :], start=False, stop=True)
                    inc("pe", mm)
                blk.tensor(b_corr_pe)
                PE_CB2 = cnt["pe"]

                def b_su1_dve(e, s=s_, PE_CB2=PE_CB2):
                    w(e, "pe", PE_CB2)
                    last = None
                    for h in range(16):
                        last = e.tensor_scalar(out=OTOK[:, h * 64:(h + 1) * 64],
                                               in0=pCs[h // 8][:, (h % 8) * 64:(h % 8) * 64 + 64],
                                               scalar1=RNQ[:, s * 16 + h: s * 16 + h + 1],
                                               scalar2=None, op0=ALU.mult)
                    inc("dve", last)
                vec_body(b_su1_dve)
                DVE_BE = cnt["dve"]

                PE_BT0 = cnt["pe"]
                DVE_BT0 = cnt["dve"]

                def b_ot_pe(e, s=s_, DVE_BE=DVE_BE, DVE_BT0=DVE_BT0):
                    w(e, "dve", DVE_BE)
                    for fc in range(8):
                        if fc >= 4:
                            e.wait_ge(sems["dve"], DVE_BT0 + fc - 3)
                        inc("pe", e.transpose(ptrbs[fc % 4],
                                              OTOK[:, fc * 128:(fc + 1) * 128], IDB[:, :]))
                if BSTEP >= 6: blk.tensor(b_ot_pe)

                def b_ot_dve(e, s=s_, PE_BT0=PE_BT0):
                    for fc in range(8):
                        e.wait_ge(sems["pe"], PE_BT0 + fc + 1)
                        inc("dve", e.tensor_copy(out=OT[:, fc * 128:(fc + 1) * 128],
                                                 in_=ptrbs[fc % 4]))
                if BSTEP >= 6: vec_body(b_ot_dve)
                DVE_BTD = cnt["dve"]

                def b_wo_pe(e, s=s_, DVE_BTD=DVE_BTD):
                    w(e, "dve", DVE_BTD)
                    for nh in range(2):
                        for kc in range(8):
                            mm = e.matmul(pCs[nh][:, :], OT[:, kc * 128:(kc + 1) * 128],
                                          WOs[:, kc * 1024 + nh * 512: kc * 1024 + (nh + 1) * 512],
                                          start=(kc == 0), stop=(kc == 7))
                    inc("pe", mm)
                if BSTEP >= 7: blk.tensor(b_wo_pe)
                PE_BWO = cnt["pe"]

                def b_y2_dve(e, s=s_, PE_BWO=PE_BWO):
                    w(e, "pe", PE_BWO)
                    e.tensor_tensor(out=Y2[:, s * 1024: s * 1024 + 512],
                                    in0=pC0[:, :], in1=Y2[:, s * 1024: s * 1024 + 512], op=ALU.add)
                    inc("dve", e.tensor_tensor(out=Y2[:, s * 1024 + 512:(s + 1) * 1024],
                                               in0=pC1[:, :], in1=Y2[:, s * 1024 + 512:(s + 1) * 1024],
                                               op=ALU.add))
                if BSTEP >= 7: vec_body(b_y2_dve)

        def y2_spill(e):
            w(e, "dve")
            inc("dout", e.dma_start(out=y2d.rearrange("(s p) d -> p s d", p=128),
                                    in_=Y2[:, :].rearrange("p (s d) -> p s d", d=1024)), 16)
            w(e, "dout")
        blk.gpsimd(y2_spill)
        DOUT_Y2 = cnt["dout"]

        def bar_pe(e):
            w(e, "dve")
            inc("pe", e.nop())
        blk.tensor(bar_pe)
        PE_BAR = cnt["pe"]

        def bar_act(e):
            w(e, "pe", PE_BAR)
            inc("act", e.nop())
        act_body(bar_act)
        ACT_BAR = cnt["act"]
        stA.close()

        stC = ExitStack()
        EC = stC.enter_context
        W1S = EC(nc.sbuf_tensor("W1S", [128, 8 * FFN], BF16))
        W2S = EC(nc.sbuf_tensor("W2S", [128, 32 * 1024], BF16))
        KQ8S = EC(nc.sbuf_tensor("KQ8S", [128, 8 * 1024], FP8))
        Y2T = EC(nc.sbuf_tensor("Y2T", [128, 2048], BF16))
        XTc = EC(nc.sbuf_tensor("XTc", [128, 2048], F32))
        XY = EC(nc.sbuf_tensor("XYc", [128, 2048], F32))
        X1F = EC(nc.sbuf_tensor("X1F", [128, 2048], F32))
        HB = EC(nc.sbuf_tensor("HB", [128, 2048], BF16))
        HT = EC(nc.sbuf_tensor("HT", [128, 2048], BF16))
        G1 = EC(nc.sbuf_tensor("G1", [128, 32 * 256], BF16))
        ZQ8 = EC(nc.sbuf_tensor("ZQ8", [128, 2048], FP8))
        YBF = EC(nc.sbuf_tensor("YBF", [128, 1024], BF16))
        YBF2 = EC(nc.sbuf_tensor("YBF2", [128, 1024], BF16))
        STATSc = EC(nc.sbuf_tensor("STATSc", [128, 32], F32))
        WSTc = XTc
        WSTc2 = XY
        SCRc = XTc
        ZB = HB
        ZT = HT
        OUTB = XTc

        DIN_CW = cnt["din"]
        ACT_CW = cnt["act"]
        DWC0 = cnt["dw"]
        DWC1 = cnt["dw2"]
        DWC2 = cnt["dw3"]
        CSLOT = [XTc[:, 0:1024], XTc[:, 1024:2048], XY[:, 0:1024], XY[:, 1024:2048],
                 X1F[:, 0:1024], X1F[:, 1024:2048]]
        FKSLOT = [HB[:, 0:1024], HB[:, 1024:2048],
                  X1F.bitcast(BF16)[:, 0:1024], X1F.bitcast(BF16)[:, 1024:2048]]
        wsrcs = []
        for kc in range(8):
            for pp in range(4):
                wsrcs.append((W1[kc * 128:(kc + 1) * 128, pp * 1024:(pp + 1) * 1024],
                              W1S[:, kc * FFN + pp * 1024: kc * FFN + (pp + 1) * 1024]))
        for kc in range(32):
            wsrcs.append((W2[kc * 128:(kc + 1) * 128, :], W2S[:, kc * 1024:(kc + 1) * 1024]))
        NWC = len(wsrcs)

        NS0 = len(range(0, NWC, 3))
        NS1 = len(range(1, NWC, 3))
        NS2 = len(range(2, NWC, 3))

        def c_wload(e):
            w(e, "act", ACT_BAR)
            w(e, "dout")
            for i in range(0, NWC, 3):
                e.wait_ge(sems["act"], ACT_CW + max(0, i - 5))
                inc("dw", e.dma_start(out=CSLOT[i % 6], in_=wsrcs[i][0]), 16)
            for kc in range(8):
                e.wait_ge(sems["act"], ACT_CW + NWC + max(0, kc - 3))
                inc("dw", e.dma_start(out=FKSLOT[kc % 4],
                                      in_=fk[kc * 128:(kc + 1) * 128, :]), 16)
        blk.sync(c_wload)

        def c_wload2(e):
            w(e, "act", ACT_BAR)
            w(e, "dout")
            for i in range(1, NWC, 3):
                e.wait_ge(sems["act"], ACT_CW + max(0, i - 5))
                inc("dw2", e.dma_start(out=CSLOT[i % 6], in_=wsrcs[i][0]), 16)
        blk.gpsimd(c_wload2)



        DOUT_SPILLS = cnt["dout"]

        def c_wcast(e):
            w(e, "dout")
            for i3 in (2, 5):
                e.wait_ge(sems["act"], ACT_CW)
                e.wait_ge(sems["dout"], DOUT_SPILLS)
                inc("dw3", e.dma_start(out=CSLOT[i3 % 6], in_=wsrcs[i3][0]), 16)
            for i in range(NWC):
                j = i // 3
                q = i % 3
                if q == 0:
                    e.wait_ge(sems["dw"], DWC0 + min(j + 2, NS0) * 16)
                elif q == 1:
                    e.wait_ge(sems["dw2"], DWC1 + min(j + 2, NS1) * 16)
                else:
                    e.wait_ge(sems["dw3"], DWC2 + min(j + 2, NS2) * 16)
                inc("act", e.activation(wsrcs[i][1], CSLOT[i % 6], AF.Copy))
                i6 = i + 6
                if i6 < NWC and i6 % 3 == 2:
                    e.wait_ge(sems["act"], ACT_CW + i6 - 5)
                    inc("dw3", e.dma_start(out=CSLOT[i6 % 6], in_=wsrcs[i6][0]), 16)
            for kc in range(8):
                e.wait_ge(sems["dw"], DWC0 + (NS0 + min(kc + 4, 8)) * 16)
                inc("act", e.activation(KQ8S[:, kc * 1024:(kc + 1) * 1024],
                                        FKSLOT[kc % 4],
                                        AF.Copy, scale=RSK[:, 0:1]))
        act_body(c_wcast)
        ACT_CWE = cnt["act"]

        for t in (range(NT) if DO_C else []):
            def c_x(e, t=t):
                w(e, "dve")
                if t == 0:
                    e.wait_ge(sems["act"], ACT_CW + NWC)
                else:
                    w(e, "act")
                w(e, "dout")
                if t == 0:
                    e.wait_ge(sems["dout"], DOUT_Y2)
                for sh in range(2):
                    inc("din", e.dma_start(out=XTc[:, sh * 1024:(sh + 1) * 1024],
                                           in_=x_in[t * 256 + sh * 128: t * 256 + sh * 128 + 128, :]), 16)
                    inc("din", e.dma_start(out=Y2T[:, sh * 1024:(sh + 1) * 1024],
                                           in_=y2d[t * 256 + sh * 128: t * 256 + sh * 128 + 128, :]), 16)
                w(e, "din")
            blk.sync(c_x)

            def c_xy_dve(e, t=t):
                w(e, "din")
                w(e, "act", ACT_CW + NWC)
                for sh in range(2):
                    inc("dve", e.tensor_tensor(out=XY[:, sh * 1024:(sh + 1) * 1024],
                                               in0=XTc[:, sh * 1024:(sh + 1) * 1024],
                                               in1=Y2T[:, sh * 1024:(sh + 1) * 1024], op=ALU.add))
            vec_body(c_xy_dve)

            def c_sq2(e, t=t):
                w(e, "act", ACT_CW + NWC)
                w(e, "dve")
                for sh in range(2):
                    c0 = sh * 16
                    inc("act", e.activation(SCRc[:, sh * 1024:(sh + 1) * 1024],
                                            Y2T[:, sh * 1024:(sh + 1) * 1024], AF.Square,
                                            accum_out=STATSc[:, c0 + 1:c0 + 2]))
            act_body(c_sq2)
            ACT_S2 = cnt["act"]

            def c_ln2(e, t=t, ACT_S2=ACT_S2):
                for sh in range(2):
                    c0 = sh * 16
                    inc("dve", e.reduce_sum(out=STATSc[:, c0:c0 + 1],
                                            in_=Y2T[:, sh * 1024:(sh + 1) * 1024], axis=AX.X))
                    w(e, "dve")
                    inc("dve", e.tensor_scalar(out=STATSc[:, c0 + 2:c0 + 3], in0=STATSc[:, c0:c0 + 1],
                                               scalar1=1.0 / D, scalar2=None, op0=ALU.mult))
                    w(e, "dve")
                    inc("dve", e.tensor_tensor(out=STATSc[:, c0 + 3:c0 + 4], in0=STATSc[:, c0 + 2:c0 + 3],
                                               in1=STATSc[:, c0 + 2:c0 + 3], op=ALU.mult))
                    e.wait_ge(sems["act"], ACT_S2 - 1 + sh)
                    w(e, "dve")
                    inc("dve", e.scalar_tensor_tensor(out=STATSc[:, c0 + 4:c0 + 5], in0=STATSc[:, c0 + 1:c0 + 2],
                                                      scalar=1.0 / D, in1=STATSc[:, c0 + 3:c0 + 4],
                                                      op0=ALU.mult, op1=ALU.subtract))
            vec_body(c_ln2)
            DVE_V2 = cnt["dve"]

            def c_sd2(e, t=t):
                w(e, "dve", DVE_V2)
                for sh in range(2):
                    c0 = sh * 16
                    inc("act", e.activation(STATSc[:, c0 + 5:c0 + 6], STATSc[:, c0 + 4:c0 + 5],
                                            AF.Sqrt, bias=EPSC[:, 0:1]))
            act_body(c_sd2)
            ACT_SD2 = cnt["act"]

            def c_h(e, t=t):
                w(e, "act", ACT_SD2)
                if t == 0:
                    w(e, "act", ACT_CWE)
                e.reciprocal(out=STATSc[:, 6:7], in_=STATSc[:, 5:6])
                inc("dve", e.reciprocal(out=STATSc[:, 22:23], in_=STATSc[:, 21:22]))
                w(e, "dve")
                for sh in range(2):
                    c0 = sh * 16
                    inc("dve", e.tensor_scalar(out=HB[:, sh * 1024:(sh + 1) * 1024],
                                               in0=Y2T[:, sh * 1024:(sh + 1) * 1024],
                                               scalar1=STATSc[:, c0 + 2:c0 + 3],
                                               scalar2=STATSc[:, c0 + 6:c0 + 7],
                                               op0=ALU.subtract, op1=ALU.mult))
            vec_body(c_h)
            DVE_H = cnt["dve"]

            PE_HT0 = cnt["pe"]
            DVE_HT0 = cnt["dve"]

            def c_ht_pe(e, t=t, DVE_H=DVE_H, DVE_HT0=DVE_HT0):
                w(e, "dve", DVE_H)
                for i in range(16):
                    sh, fc = i // 8, i % 8
                    if i >= 4:
                        e.wait_ge(sems["dve"], DVE_HT0 + i - 3)
                    inc("pe", e.transpose(ptrbs[i % 4],
                                          HB[:, sh * 1024 + fc * 128: sh * 1024 + fc * 128 + 128], IDB[:, :]))
            blk.tensor(c_ht_pe)

            def c_ht_dve(e, t=t, PE_HT0=PE_HT0):
                for i in range(16):
                    sh, fc = i // 8, i % 8
                    e.wait_ge(sems["pe"], PE_HT0 + i + 1)
                    inc("dve", e.tensor_scalar(out=HT[:, fc * 256 + sh * 128: fc * 256 + sh * 128 + 128],
                                               in0=ptrbs[i % 4],
                                               scalar1=PRM[:, 24 + fc:25 + fc], scalar2=PRM[:, 32 + fc:33 + fc],
                                               op0=ALU.mult, op1=ALU.add))
            vec_body(c_ht_dve)
            DVE_HT = cnt["dve"]

            PE_W10 = cnt["pe"]
            ACT_G10 = cnt["act"]

            def c_w1_pe(e, t=t, DVE_HT=DVE_HT, ACT_G10=ACT_G10):
                w(e, "dve", DVE_HT)
                for oc in range(32):
                    if oc >= 2:
                        e.wait_ge(sems["act"], ACT_G10 + oc - 1)
                    for kc in range(8):
                        mm = e.matmul(pAs[oc % 2][:, :],
                                      W1S[:, kc * FFN + oc * 128: kc * FFN + (oc + 1) * 128],
                                      HT[:, kc * 256:(kc + 1) * 256], start=(kc == 0), stop=(kc == 7))
                    inc("pe", mm)
            blk.tensor(c_w1_pe)

            def c_gelu_act(e, t=t, PE_W10=PE_W10):
                for oc in range(32):
                    e.wait_ge(sems["pe"], PE_W10 + oc + 1)
                    inc("act", e.activation(G1[:, oc * 256:(oc + 1) * 256], pAs[oc % 2][:, :],
                                            AF.Gelu, bias=B1C[:, oc:oc + 1]))
            act_body(c_gelu_act)
            ACT_G1E = cnt["act"]

            for th in range(2):
                def c_w2_pe(e, t=t, th=th, ACT_G1E=ACT_G1E):
                    w(e, "act", ACT_G1E)
                    if th == 1:
                        w(e, "dve")
                    for oc in range(32):
                        for nh in range(2):
                            e.matmul(pCs[nh][:, :],
                                     G1[:, oc * 256 + th * 128: oc * 256 + th * 128 + 128],
                                     W2S[:, oc * 1024 + nh * 512: oc * 1024 + (nh + 1) * 512],
                                     start=(oc == 0), stop=False)
                    for nh in range(2):
                        mm = e.matmul(pCs[nh][:, :], ONESB[:, :],
                                      B2ROWB[:, nh * 512:(nh + 1) * 512], start=False, stop=True)
                    inc("pe", mm)
                blk.tensor(c_w2_pe)
                PE_W2 = cnt["pe"]

                def c_x1f_dve(e, t=t, th=th, PE_W2=PE_W2):
                    w(e, "pe", PE_W2)
                    if t == 0 and th == 0:
                        w(e, "act", ACT_CWE)
                    e.scalar_tensor_tensor(out=X1F[:, th * 1024: th * 1024 + 512],
                                           in0=pC0[:, :], scalar=0.0,
                                           in1=XY[:, th * 1024: th * 1024 + 512],
                                           op0=ALU.bypass, op1=ALU.add)
                    inc("dve", e.scalar_tensor_tensor(out=X1F[:, th * 1024 + 512:(th + 1) * 1024],
                                                      in0=pC1[:, :], scalar=0.0,
                                                      in1=XY[:, th * 1024 + 512:(th + 1) * 1024],
                                                      op0=ALU.bypass, op1=ALU.add))
                vec_body(c_x1f_dve)

            DVE_X1F = cnt["dve"]

            def c_sq3(e, t=t, DVE_X1F=DVE_X1F):
                w(e, "dve", DVE_X1F)
                for sh in range(2):
                    c0 = sh * 16
                    inc("act", e.activation(SCRc[:, sh * 1024:(sh + 1) * 1024],
                                            X1F[:, sh * 1024:(sh + 1) * 1024], AF.Square,
                                            accum_out=STATSc[:, c0 + 1:c0 + 2]))
            act_body(c_sq3)
            ACT_S3 = cnt["act"]

            def c_ln3(e, t=t, DVE_X1F=DVE_X1F, ACT_S3=ACT_S3):
                w(e, "dve", DVE_X1F)
                for sh in range(2):
                    c0 = sh * 16
                    inc("dve", e.reduce_sum(out=STATSc[:, c0:c0 + 1],
                                            in_=X1F[:, sh * 1024:(sh + 1) * 1024], axis=AX.X))
                    w(e, "dve")
                    inc("dve", e.tensor_scalar(out=STATSc[:, c0 + 2:c0 + 3], in0=STATSc[:, c0:c0 + 1],
                                               scalar1=1.0 / D, scalar2=None, op0=ALU.mult))
                    w(e, "dve")
                    inc("dve", e.tensor_tensor(out=STATSc[:, c0 + 3:c0 + 4], in0=STATSc[:, c0 + 2:c0 + 3],
                                               in1=STATSc[:, c0 + 2:c0 + 3], op=ALU.mult))
                    e.wait_ge(sems["act"], ACT_S3 - 1 + sh)
                    w(e, "dve")
                    inc("dve", e.scalar_tensor_tensor(out=STATSc[:, c0 + 4:c0 + 5], in0=STATSc[:, c0 + 1:c0 + 2],
                                                      scalar=1.0 / D, in1=STATSc[:, c0 + 3:c0 + 4],
                                                      op0=ALU.mult, op1=ALU.subtract))
            vec_body(c_ln3)
            DVE_V3 = cnt["dve"]

            def c_sd3(e, t=t, DVE_V3=DVE_V3):
                w(e, "dve", DVE_V3)
                for sh in range(2):
                    c0 = sh * 16
                    inc("act", e.activation(STATSc[:, c0 + 5:c0 + 6], STATSc[:, c0 + 4:c0 + 5],
                                            AF.Sqrt, bias=EPSC[:, 0:1]))
            act_body(c_sd3)
            ACT_SD3 = cnt["act"]

            def c_z(e, t=t, ACT_SD3=ACT_SD3):
                w(e, "act", ACT_SD3)
                e.reciprocal(out=STATSc[:, 6:7], in_=STATSc[:, 5:6])
                inc("dve", e.reciprocal(out=STATSc[:, 22:23], in_=STATSc[:, 21:22]))
                w(e, "dve")
                for sh in range(2):
                    c0 = sh * 16
                    inc("dve", e.tensor_scalar(out=ZB[:, sh * 1024:(sh + 1) * 1024],
                                               in0=X1F[:, sh * 1024:(sh + 1) * 1024],
                                               scalar1=STATSc[:, c0 + 2:c0 + 3],
                                               scalar2=STATSc[:, c0 + 6:c0 + 7],
                                               op0=ALU.subtract, op1=ALU.mult))
            vec_body(c_z)
            DVE_Z = cnt["dve"]

            PE_ZT0 = cnt["pe"]
            DVE_ZT0 = cnt["dve"]

            def c_zt_pe(e, t=t, DVE_Z=DVE_Z, DVE_ZT0=DVE_ZT0):
                w(e, "dve", DVE_Z)
                for i in range(16):
                    sh, fc = i // 8, i % 8
                    if i >= 4:
                        e.wait_ge(sems["dve"], DVE_ZT0 + i - 3)
                    inc("pe", e.transpose(ptrbs[i % 4],
                                          ZB[:, sh * 1024 + fc * 128: sh * 1024 + fc * 128 + 128], IDB[:, :]))
            blk.tensor(c_zt_pe)

            def c_zt_dve(e, t=t, PE_ZT0=PE_ZT0):
                for i in range(16):
                    sh, fc = i // 8, i % 8
                    e.wait_ge(sems["pe"], PE_ZT0 + i + 1)
                    inc("dve", e.tensor_copy(out=ZT[:, fc * 256 + sh * 128: fc * 256 + sh * 128 + 128],
                                             in_=ptrbs[i % 4]))
            vec_body(c_zt_dve)
            DVE_ZT = cnt["dve"]

            def c_zq_act(e, t=t, DVE_ZT=DVE_ZT):
                w(e, "dve", DVE_ZT)
                for fc in range(8):
                    inc("act", e.activation(ZQ8[:, fc * 256:(fc + 1) * 256],
                                            ZT[:, fc * 256:(fc + 1) * 256],
                                            AF.Copy, scale=RSIN[:, 0:1]))
            act_body(c_zq_act)
            ACT_ZQ = cnt["act"]

            for th in range(2):
                def c_fk_pe(e, t=t, th=th):
                    w(e, "act")
                    w(e, "dve")
                    for nh in range(2):
                        for kc in range(8):
                            mm = e.matmul(pCs[nh][:, :],
                                          ZQ8[:, kc * 256 + th * 128: kc * 256 + th * 128 + 128],
                                          KQ8S[:, kc * 1024 + nh * 512: kc * 1024 + (nh + 1) * 512],
                                          start=(kc == 0), stop=(kc == 7))
                    inc("pe", mm)
                blk.tensor(c_fk_pe)
                PE_FK = cnt["pe"]

                def c_yb_act(e, t=t, th=th, PE_FK=PE_FK):
                    w(e, "pe", PE_FK)
                    e.activation(YBF[:, 0:512], pC0[:, :], AF.Copy, scale=SPROD[:, 0:1])
                    inc("act", e.activation(YBF[:, 512:1024], pC1[:, :], AF.Copy, scale=SPROD[:, 0:1]))
                act_body(c_yb_act)
                ACT_YB = cnt["act"]

                def c_out_dve(e, t=t, th=th, ACT_YB=ACT_YB):
                    w(e, "act", ACT_YB)
                    w(e, "dout")
                    inc("dve", e.tensor_tensor(out=YBF2[:, :], in0=YBF[:, :], in1=BIASB[:, :], op=ALU.add))
                    w(e, "dve")
                    inc("dve", e.tensor_tensor(out=OUTB[:, th * 1024:(th + 1) * 1024],
                                               in0=X1F[:, th * 1024:(th + 1) * 1024],
                                               in1=YBF2[:, :], op=ALU.add))
                vec_body(c_out_dve)
                DVE_OUT = cnt["dve"]

                def c_store(e, t=t, th=th, DVE_OUT=DVE_OUT):
                    w(e, "dve", DVE_OUT)
                    inc("dout", e.dma_start(out=out[t * 256 + th * 128: t * 256 + th * 128 + 128, :],
                                            in_=OUTB[:, th * 1024:(th + 1) * 1024]), 16)
                    w(e, "dout")
                blk.gpsimd(c_store)

        def fin(e):
            w(e, "dout")
        blk.gpsimd(fin)
        stC.close()
    return nc


def _consts():
    identf = np.eye(128, dtype=np.float32)
    identb = identf.astype(ml_dtypes.bfloat16)
    mrow = np.arange(128)[:, None]
    ccol = np.arange(128)[None, :]
    valid = (mrow // 64 == ccol // 64) & (mrow <= ccol)
    maskneg = np.where(valid, 0.0, NEG).astype(np.float32)
    sel16f = np.zeros((128, 128), np.float32)
    sel1b_ = np.zeros((128, 128), np.float32)
    for fc in range(8):
        sel16f[0:64, fc * 16 + 2 * fc] = 1.0 / 64
        sel16f[64:128, fc * 16 + 2 * fc + 1] = 1.0 / 64
        sel1b_[0:64, fc * 16 + 2 * fc] = 1.0
        sel1b_[64:128, fc * 16 + 2 * fc + 1] = 1.0
    sel1b = sel1b_.astype(ml_dtypes.bfloat16)
    bsel = np.zeros((16, 16 * 128), np.float32)
    for h in range(16):
        bsel[h, h * 128:(h + 1) * 128] = 1.0
    onesf = np.ones((1, 128), np.float32)
    onesb = np.ones((1, 128), ml_dtypes.bfloat16)
    return dict(identf=identf, identb=identb, maskneg=maskneg, sel16f=sel16f,
                sel1b=sel1b, bsel=bsel, onesf=onesf, onesb=onesb)


_NC_CACHE = {}


def kernel(L0_attn_x, L0_attn_Wq, L0_attn_Wk, L0_attn_Wv, L0_attn_Wg, L0_attn_bg,
           L0_attn_Wo, L0_attn_ln1_w, L0_attn_ln1_b, L0_attn_ln2_w, L0_attn_ln2_b,
           L0_attn_W1, L0_attn_b1, L0_attn_W2, L0_attn_b2,
           L0_ffn_kernel, L0_ffn_input_scale, L0_ffn_kernel_scale, L0_ffn_bias):
    if "nc" not in _NC_CACHE:
        _NC_CACHE["nc"] = build_nc()
    nc = _NC_CACHE["nc"]
    consts = _consts()
    x = np.ascontiguousarray(np.asarray(L0_attn_x).reshape(16384, 1024))
    shared = dict(
        Wq=np.ascontiguousarray(L0_attn_Wq), Wk=np.ascontiguousarray(L0_attn_Wk),
        Wv=np.ascontiguousarray(L0_attn_Wv), Wg=np.ascontiguousarray(L0_attn_Wg),
        bg=np.ascontiguousarray(L0_attn_bg), Wo=np.ascontiguousarray(L0_attn_Wo),
        ln1w=np.ascontiguousarray(L0_attn_ln1_w), ln1b=np.ascontiguousarray(L0_attn_ln1_b),
        ln2w=np.ascontiguousarray(L0_attn_ln2_w), ln2b=np.ascontiguousarray(L0_attn_ln2_b),
        W1=np.ascontiguousarray(L0_attn_W1), b1=np.ascontiguousarray(L0_attn_b1),
        W2=np.ascontiguousarray(L0_attn_W2), b2=np.ascontiguousarray(L0_attn_b2),
        fk=np.ascontiguousarray(L0_ffn_kernel),
        fsin=np.ascontiguousarray(L0_ffn_input_scale),
        fsk=np.ascontiguousarray(L0_ffn_kernel_scale),
        fbias=np.ascontiguousarray(L0_ffn_bias),
        **consts,
    )
    in_maps = []
    for c in range(NCORE):
        m1c = np.zeros((128, 8), np.float32)
        m1c[:, :c] = 1.0
        in_maps.append(dict(shared, x=np.ascontiguousarray(x[c * T:(c + 1) * T]), m1=m1c))
    res = run_bass_kernel_spmd(nc, in_maps, core_ids=list(range(NCORE)))
    full = np.concatenate([res.results[c]["out"] for c in range(NCORE)], axis=0)
    return full.reshape(1, 16384, 1024)
